# version 03
# baseline (speedup 1.0000x reference)
import numpy as np
import ml_dtypes
from contextlib import ExitStack
import concourse.bass as bass
import concourse.mybir as mybir
from concourse.bass_utils import run_bass_kernel_spmd

F32 = mybir.dt.float32
BF16 = mybir.dt.bfloat16
AF = mybir.ActivationFunctionType
ALU = mybir.AluOpType
AX = mybir.AxisListType

D = 2048
NF = 16
TP = 1024
NP = 2
T = 2048
CH = 512

DBG = False

EV, VV, MM, LD, ST, WD, GG, CC = "ev", "vv", "mm", "ld", "st", "wd", "gg", "cc"


class KB:
    def __init__(self, nc, stack):
        self.nc = nc
        self.cnt = {kk: 0 for kk in (EV, VV, MM, LD, ST, WD, GG, CC)}
        self.sem = {kk: stack.enter_context(nc.semaphore("s_" + kk))
                    for kk in (EV, VV, MM, LD, ST, WD, GG, CC)}
        self.waited = {}
        self.bank_free = [(None, 0)] * 6
        self.bidx = 0
        self.pt_free = [(None, 0)] * 2
        self.pti = 0
        self.stg_free = [(None, 0)] * 2
        self.stg_i = 0

    def w(self, eng, key, val):
        if not key or val is None or val <= 0:
            return
        kk = (eng.engine, key)
        if self.waited.get(kk, 0) >= val:
            return
        eng.wait_ge(self.sem[key], val)
        self.waited[kk] = val

    def snap(self):
        return {kk: self.cnt[kk] for kk in (EV, VV, MM, LD, ST)}

    def wsnap(self, eng, sn, keys=(EV, VV, MM, ST)):
        for kk in keys:
            self.w(eng, kk, sn[kk])

    def inc(self, instr, key, n=1):
        instr.then_inc(self.sem[key], n)
        self.cnt[key] += n
        return self.cnt[key]

    def ld(self, out, in_, waits=()):
        for kk, vv in waits:
            self.w(self.nc.sync, kk, vv)
        return self.inc(self.nc.sync.dma_start(out=out, in_=in_), LD, 16)

    def st(self, out, in_, waits=()):
        for kk, vv in waits:
            self.w(self.nc.sync, kk, vv)
        return self.inc(self.nc.sync.dma_start(out=out, in_=in_), ST, 16)

    def bank(self):
        b = self.bidx % 6
        self.bidx += 1
        kk, vv = self.bank_free[b]
        self.w(self.nc.tensor, kk, vv)
        return b

    def ptbank(self):
        b = self.pti % 2
        self.pti += 1
        kk, vv = self.pt_free[b]
        self.w(self.nc.tensor, kk, vv)
        return b

    def chain(self, bank_ap, pairs, waits=(), start=True, stop=True):
        t = self.nc.tensor
        for kk, vv in waits:
            self.w(t, kk, vv)
        n = len(pairs)
        mm = None
        for i, (l, r) in enumerate(pairs):
            mm = t.matmul(bank_ap, l, r,
                          start=(start and i == 0), stop=(stop and i == n - 1))
        return self.inc(mm, MM, 1)

    def stage(self, loads, extra_wait=()):
        h = self.stg_i % 2
        self.stg_i += 1
        kk, vv = self.stg_free[h]
        waits = list(extra_wait) + ([(kk, vv)] if kk else [])
        stg = self.stgv[h]
        lc = 0
        for dvf, src in loads:
            lc = self.ld(dvf(stg), src, waits=waits)
            waits = []
        self._last_half = h
        return stg, lc

    def stage_consumed(self, key, val):
        self.stg_free[self._last_half] = (key, val)


def build_nc():
    nc = bass.Bass("TRN2", target_bir_lowering=False)
    EI, EO, IN = "ExternalInput", "ExternalOutput", "Internal"
    dt = nc.dram_tensor
    x_in = dt("x", [T, D], F32, kind=EI)
    w_f = {n: dt(n, sh, F32, kind=EI) for n, sh in [
        ("wq", [D, D]), ("wk", [D, D]), ("wv", [D, D]), ("ws", [D, D]), ("wt", [D, D]),
        ("w1", [2 * D, 4 * D]), ("w2", [4 * D, 2 * D]),
        ("wsgu", [D, 11264]), ("wsdn", [5632, D]),
        ("wegu", [8 * D, 2816]), ("wedn", [8 * 1408, D])]}
    wrt = dt("wrt", [128, 128], F32, kind=EI)
    wsgt = dt("wsgt", [128, 16], F32, kind=EI)
    b1t_in = dt("b1t", [128, 64], F32, kind=EI)
    b2t_in = dt("b2t", [128, 32], F32, kind=EI)
    cmat = dt("cmat", [D, D], BF16, kind=EI)
    smat = dt("smat", [D, D], BF16, kind=EI)
    idn_in = dt("idn", [128, 128], BF16, kind=EI)
    sel8_in = dt("sel8", [8, 8 * 128], BF16, kind=EI)
    mask_in = dt("mask", [128, 1], F32, kind=EI)
    out = dt("out", [T, D], F32, kind=EO)

    w_b = {n: dt("b" + n, list(w_f[n].shape), BF16, kind=IN) for n in w_f}

    spk = EO if DBG else IN
    sp_xh = dt("sp_xh", [NP * 2 * D, TP], BF16, kind=spk)
    sp_sf = dt("sp_sf", [NP * 2 * D, TP], BF16, kind=spk)
    sp_tf = dt("sp_tf", [NP * 2 * D, TP], BF16, kind=spk)
    sp_g = dt("sp_g", [NP * D, TP], BF16, kind=spk)
    sp_z0 = dt("sp_z0", [NP * 2 * D, TP], BF16, kind=spk)
    sp_h1 = dt("sp_h1", [4 * D, TP], BF16, kind=spk)
    sp_h = dt("sp_h", [2 * D, TP], BF16, kind=spk)
    sp_hg = dt("sp_hg", [5632, TP], BF16, kind=spk)
    sp_hsh = dt("sp_hsh", [5632, TP], BF16, kind=spk)
    sp_ta = dt("sp_ta", [NP * 2 * D, TP], BF16, kind=spk)
    sp_xn = dt("sp_xn", [NP * D, TP], BF16, kind=spk)
    dbg_ce = dt("dbg_ce", [8, TP], BF16, kind=spk)
    dbg_lg = dt("dbg_lg", [8, TP], F32, kind=spk)
    dbg_sg = dt("dbg_sg", [128, TP], BF16, kind=spk)
    dbg_F = dt("dbg_F", [NP * D, TP], F32, kind=spk)
    cc_in = dt("cc_in", [128, 32], F32, kind=IN)
    cc_out = dt("cc_out", [256, 32], F32, kind=IN)

    stack = ExitStack()
    sb = lambda n, sh, d: stack.enter_context(nc.sbuf_tensor(n, sh, d))
    psal = lambda n, sh, d: stack.enter_context(nc.psum_tensor(n, sh, d))
    xn = sb("xn", [128, NF * TP], BF16)
    ybuf = sb("ybuf", [128, NF * TP], BF16)
    big2 = sb("big2", [128, 2 * NF * TP], BF16)
    bigF = big2.bitcast(F32)
    stg0 = sb("stg0", [128, 4096], BF16)
    stg1 = sb("stg1", [128, 4096], BF16)
    scr = sb("scr", [128, 4096], F32)
    spl = sb("spl", [128, 4096], BF16)
    msc = sb("msc", [128, 1024], F32)
    idn = sb("idnb", [128, 128], BF16)
    idnf = sb("idnf", [128, 128], F32)
    ones1 = sb("ones1", [1, 128], BF16)
    sel8 = sb("sel8b", [8, 8 * 128], BF16)
    wrtb = sb("wrtb", [128, 128], BF16)
    wsgtb = sb("wsgtb", [128, 16], BF16)
    lgf = sb("lgf", [8, TP], F32)
    cebf = sb("cebf", [8, TP], BF16)
    sgf = sb("sgf", [1, TP], F32)
    sgb = sb("sgb", [1, TP], BF16)
    sgbc = sb("sgbc", [128, TP], BF16)
    cebc = sb("cebc", [128, TP], BF16)
    pb = [psal(f"pb{i}", [128, 512], F32) for i in range(6)]
    pt = [psal(f"pt{i}", [128, 512], BF16) for i in range(2)]

    MB1, MB2, MMK = 0, 64, 96
    MLN = 100
    MTOT = 200
    MCAR = 232
    MRT = 264

    k = KB(nc, stack)
    k.stgv = [stg0, stg1]
    sy, g, t, s, v = nc.sync, nc.gpsimd, nc.tensor, nc.scalar, nc.vector

    r3 = lambda w: w.ap().rearrange("(a p) c -> p a c", p=128)
    st3 = lambda stg, n, c: stg[:, 0:n * c].rearrange("p (a c) -> p a c", c=c)

    k.ld(idn[:, :], idn_in[:, :])
    k.ld(sel8[:, :], sel8_in[:, :])
    k.ld(msc[:, MB1:MB1 + 64], b1t_in[:, :])
    k.ld(msc[:, MB2:MB2 + 32], b2t_in[:, :])
    k.ld(msc[:, MMK:MMK + 1], mask_in[:, :])
    lc = k.ld(scr[:, 0:128], wrt[:, :])
    lc = k.ld(scr[:, 128:144], wsgt[:, :])
    k.w(v, LD, lc)
    v.tensor_copy(wrtb[:, :], scr[:, 0:128])
    v.tensor_copy(wsgtb[:, :], scr[:, 128:144])
    v.tensor_copy(idnf[:, :], idn[:, :])
    k.inc(v.memset(msc[:, MTOT:MTOT + 32], 0.0), VV)
    k.inc(g.memset(ones1[:, :], 1.0), GG)

    conv_done = {}

    def convert(name):
        src, dst = w_f[name], w_b[name]
        rows, cols = (int(z) for z in src.shape)
        step = max(128, ((8 << 20) // (cols * 4)) // 128 * 128)
        c = 0
        for r0 in range(0, rows, step):
            r1 = min(rows, r0 + step)
            c = k.inc(g.dma_start(out=dst[r0:r1, :], in_=src[r0:r1, :]), WD, 16)
        conv_done[name] = c

    for n in ("wk", "wv", "wq", "ws", "wt", "wsgu", "wsdn", "wegu", "wedn", "w1", "w2"):
        convert(n)

    spl_free = [(None, 0)] * 4
    spl_ii = [0]

    def spl_slot(prod_eng):
        i = spl_ii[0] % 4
        spl_ii[0] += 1
        kk, vv = spl_free[i]
        if kk:
            k.w(prod_eng, kk, vv)
        return i

    marks = {"ybuf": (None, 0), "xnreg": (None, 0)}

    def phase_A(p):
        sn = k.snap()
        for tt in range(8):
            row0 = p * TP + tt * 128
            wts = ([(VV, sn[VV]), (EV, sn[EV]), (MM, sn[MM])] if tt == 0
                   else [(VV, k.cnt[VV]), (EV, k.cnt[EV])])
            lc = k.ld(scr[:, 0:2048], x_in[row0:row0 + 128, :], waits=wts)
            k.w(v, LD, lc)
            vc = k.inc(v.reduce_sum(msc[:, MLN:MLN + 1], scr[:, 0:2048], AX.X), VV)
            k.w(v, VV, vc)
            vc = k.inc(v.tensor_scalar_mul(msc[:, MLN:MLN + 1],
                                           msc[:, MLN:MLN + 1], 1.0 / D), VV)
            k.w(v, VV, vc)
            vc = k.inc(v.tensor_scalar(scr[:, 2048:4096], scr[:, 0:2048],
                                       msc[:, MLN:MLN + 1], None, ALU.subtract), VV)
            k.w(s, VV, vc)
            ec = k.inc(s.activation(scr[:, 0:2048], scr[:, 2048:4096], AF.Square,
                                    accum_out=msc[:, MLN + 2:MLN + 3]), EV)
            k.w(v, EV, ec)
            vc = k.inc(v.tensor_scalar(msc[:, MLN + 2:MLN + 3], msc[:, MLN + 2:MLN + 3],
                                       1.0 / D, 1e-5, ALU.mult, ALU.add), VV)
            k.w(v, VV, vc)
            vc = k.inc(v.reciprocal(msc[:, MLN + 1:MLN + 2], msc[:, MLN + 2:MLN + 3]), VV)
            k.w(s, VV, vc)
            ec = k.inc(s.activation(msc[:, MLN + 1:MLN + 2],
                                    msc[:, MLN + 1:MLN + 2], AF.Sqrt), EV)
            k.w(v, EV, ec)
            for kk, vv in (spl_free[0], spl_free[1]):
                if kk:
                    k.w(v, kk, vv)
            vc = k.inc(v.tensor_scalar(spl[:, 0:2048], scr[:, 2048:4096],
                                       msc[:, MLN + 1:MLN + 2], None, ALU.mult), VV)
            for f in range(NF):
                b = k.ptbank()
                k.w(t, VV, vc)
                if tt == 0 and f == 0:
                    k.w(t, MM, marks["xnreg"][1] if marks["xnreg"][0] else 0)
                mc = k.inc(t.transpose(pt[b][:, 0:128],
                                       spl[:, f * 128:(f + 1) * 128], idn[:, :]), MM)
                k.w(s, MM, mc)
                if tt == 0 and f == 0:
                    k.wsnap(s, sn, keys=(MM, VV))
                ec = k.inc(s.copy(xn[:, f * TP + tt * 128: f * TP + (tt + 1) * 128],
                                  pt[b][:, 0:128]), EV)
                k.pt_free[b] = (EV, ec)
            spl_free[0] = (MM, k.cnt[MM])
            spl_free[1] = (MM, k.cnt[MM])
        if DBG:
            for f in range(NF):
                k.st(sp_xn[p * D + f * 128: p * D + (f + 1) * 128, :],
                     xn[:, f * TP:(f + 1) * TP], waits=[(EV, k.cnt[EV])])
        return k.cnt[EV]

    def proj(wname, xn_ev):
        ymark = marks["ybuf"]
        for mg in range(8):
            stg, lc = k.stage([(lambda sg: st3(sg, NF, 256),
                                r3(w_b[wname])[:, :, mg * 256:(mg + 1) * 256])],
                              extra_wait=[(WD, conv_done[wname])])
            for m2 in range(2):
                mglob = mg * 2 + m2
                for ncz in range(2):
                    b = k.bank()
                    pairs = [(stg[:, kk * 256 + m2 * 128: kk * 256 + m2 * 128 + 128],
                              xn[:, kk * TP + ncz * CH: kk * TP + ncz * CH + CH])
                             for kk in range(NF)]
                    mc = k.chain(pb[b][:, 0:CH], pairs,
                                 waits=[(LD, lc), (EV, xn_ev)])
                    k.w(s, MM, mc)
                    if ymark[0]:
                        k.w(s, ymark[0], ymark[1])
                    ec = k.inc(s.copy(ybuf[:, mglob * TP + ncz * CH:
                                           mglob * TP + ncz * CH + CH],
                                      pb[b][:, 0:CH]), EV)
                    k.bank_free[b] = (EV, ec)
            k.stage_consumed(MM, k.cnt[MM])
        return k.cnt[EV]

    def fft_mtile(mf, rhs_tensor, rhs_ready):
        stg, lc = k.stage([
            (lambda sg: sg[:, 0:NF * 128].rearrange("p (a c) -> p a c", c=128),
             r3(cmat)[:, :, mf * 128:(mf + 1) * 128]),
            (lambda sg: sg[:, NF * 128:2 * NF * 128].rearrange("p (a c) -> p a c", c=128),
             r3(smat)[:, :, mf * 128:(mf + 1) * 128])])
        res = []
        for plane in range(2):
            off = plane * NF * 128
            for ncz in range(2):
                b = k.bank()
                pairs = [(stg[:, off + kk * 128: off + (kk + 1) * 128],
                          rhs_tensor[:, kk * TP + ncz * CH: kk * TP + ncz * CH + CH])
                         for kk in range(NF)]
                mc = k.chain(pb[b][:, 0:CH], pairs,
                             waits=[(LD, lc)] + ([rhs_ready] if rhs_ready else []))
                res.append((b, mc))
        k.stage_consumed(MM, res[-1][1])
        return res

    def phase_C(xn_ev, b2_mark):
        yev = proj("wk", xn_ev)
        for mf in range(NF):
            res = fft_mtile(mf, ybuf, (EV, yev))
            for plane in range(2):
                for ncz in range(2):
                    b, mc = res[plane * 2 + ncz]
                    k.w(s, MM, mc)
                    if mf == 0 and plane == 0 and ncz == 0 and b2_mark[0]:
                        k.w(s, b2_mark[0], b2_mark[1])
                    c0 = plane * NF * TP + mf * TP + ncz * CH
                    ec = k.inc(s.copy(big2[:, c0:c0 + CH], pb[b][:, 0:CH]), EV)
                    k.bank_free[b] = (EV, ec)
        marks["ybuf"] = (MM, k.cnt[MM])
        return k.cnt[EV]

    def phase_D(xn_ev, kf_ev):
        yev = proj("wv", xn_ev)
        for mf in range(NF):
            res = fft_mtile(mf, ybuf, (EV, yev))
            for ncz in range(2):
                bR, mR = res[0 * 2 + ncz]
                bI, mI = res[1 * 2 + ncz]
                c0 = mf * TP + ncz * CH
                kR = big2[:, c0:c0 + CH]
                kI = big2[:, NF * TP + c0:NF * TP + c0 + CH]
                k.w(v, MM, max(mR, mI))
                k.w(v, EV, kf_ev)
                v.tensor_mul(scr[:, 0:512], pb[bR][:, 0:CH], kR)
                v.tensor_mul(scr[:, 512:1024], pb[bI][:, 0:CH], kI)
                v.tensor_sub(scr[:, 1024:1536], scr[:, 0:512], scr[:, 512:1024])
                v.tensor_mul(scr[:, 0:512], pb[bI][:, 0:CH], kR)
                v.tensor_mul(scr[:, 512:1024], pb[bR][:, 0:CH], kI)
                v.tensor_add(kI, scr[:, 0:512], scr[:, 512:1024])
                vc = k.inc(v.tensor_copy(kR, scr[:, 1024:1536]), VV)
                k.bank_free[bR] = (VV, vc)
                k.bank_free[bI] = (VV, vc)
        marks["ybuf"] = (MM, k.cnt[MM])
        return k.cnt[VV]

    def phase_E(p, xn_ev, kv_vv):
        yev = proj("wq", xn_ev)
        for mf in range(NF):
            res = fft_mtile(mf, ybuf, (EV, yev))
            for ncz in range(2):
                bR, mR = res[0 * 2 + ncz]
                bI, mI = res[1 * 2 + ncz]
                c0 = mf * TP + ncz * CH
                kvR = big2[:, c0:c0 + CH]
                kvI = big2[:, NF * TP + c0:NF * TP + c0 + CH]
                k.w(v, MM, max(mR, mI))
                k.w(v, VV, kv_vv)
                tA = scr[:, 2048 + ncz * CH:2048 + ncz * CH + CH]
                tB = scr[:, 3072 + ncz * CH:3072 + ncz * CH + CH]
                v.tensor_mul(tA, pb[bR][:, 0:CH], kvR)
                v.tensor_mul(tB, pb[bI][:, 0:CH], kvI)
                v.tensor_add(scr[:, ncz * CH:ncz * CH + CH], tA, tB)
                v.tensor_mul(tA, pb[bR][:, 0:CH], kvI)
                v.tensor_mul(tB, pb[bI][:, 0:CH], kvR)
                vc = k.inc(v.tensor_sub(scr[:, 1024 + ncz * CH:1024 + ncz * CH + CH],
                                        tA, tB), VV)
                k.bank_free[bR] = (VV, vc)
                k.bank_free[bI] = (VV, vc)
            res2 = fft_mtile(mf, xn, (EV, xn_ev))
            totR = msc[:, MTOT + mf:MTOT + mf + 1]
            totI = msc[:, MTOT + 16 + mf:MTOT + 16 + mf + 1]
            k.w(v, VV, k.cnt[VV])
            init_r = 0.0 if p == 0 else totR
            init_i = 0.0 if p == 0 else totI
            vc = k.inc(v.tensor_tensor_scan(scr[:, 2048:3072], scr[:, 0:1024],
                                            scr[:, 0:1024], init_r,
                                            ALU.add, ALU.bypass), VV)
            vc = k.inc(v.tensor_tensor_scan(scr[:, 3072:4096], scr[:, 1024:2048],
                                            scr[:, 1024:2048], init_i,
                                            ALU.add, ALU.bypass), VV)
            k.w(v, VV, vc)
            v.tensor_copy(totR, scr[:, 3071:3072])
            vc = k.inc(v.tensor_copy(totI, scr[:, 4095:4096]), VV)
            slR = spl_slot(v)
            slI = spl_slot(v)
            lastv = 0
            for ncz in range(2):
                bR, mR = res2[0 * 2 + ncz]
                bI, mI = res2[1 * 2 + ncz]
                k.w(v, MM, max(mR, mI))
                v.tensor_add(spl[:, slR * 1024 + ncz * CH: slR * 1024 + ncz * CH + CH],
                             scr[:, 2048 + ncz * CH:2048 + ncz * CH + CH],
                             pb[bR][:, 0:CH])
                lastv = k.inc(v.tensor_add(spl[:, slI * 1024 + ncz * CH:
                                               slI * 1024 + ncz * CH + CH],
                                           scr[:, 3072 + ncz * CH:3072 + ncz * CH + CH],
                                           pb[bI][:, 0:CH]), VV)
                k.bank_free[bR] = (VV, lastv)
                k.bank_free[bI] = (VV, lastv)
            r0 = p * 2 * D + mf * 128
            sc = k.st(sp_xh[r0:r0 + 128, :], spl[:, slR * 1024:slR * 1024 + 1024],
                      waits=[(VV, lastv)])
            sc = k.st(sp_xh[r0 + D:r0 + D + 128, :],
                      spl[:, slI * 1024:slI * 1024 + 1024])
            spl_free[slR] = (ST, sc)
            spl_free[slI] = (ST, sc)
        marks["ybuf"] = (MM, k.cnt[MM])
        return k.cnt[VV]

    def phase_GH(p, wname, sp, xn_ev):
        yev = proj(wname, xn_ev)
        for mf in range(NF):
            res = fft_mtile(mf, ybuf, (EV, yev))
            slR = spl_slot(s)
            slI = spl_slot(s)
            last = 0
            for plane, sl in ((0, slR), (1, slI)):
                for ncz in range(2):
                    b, mc = res[plane * 2 + ncz]
                    k.w(s, MM, mc)
                    last = k.inc(s.copy(spl[:, sl * 1024 + ncz * CH:
                                            sl * 1024 + ncz * CH + CH],
                                        pb[b][:, 0:CH]), EV)
                    k.bank_free[b] = (EV, last)
            r0 = p * 2 * D + mf * 128
            sc = k.st(sp[r0:r0 + 128, :], spl[:, slR * 1024:slR * 1024 + 1024],
                      waits=[(EV, last)])
            sc = k.st(sp[r0 + D:r0 + D + 128, :], spl[:, slI * 1024:slI * 1024 + 1024])
            spl_free[slR] = (ST, sc)
            spl_free[slI] = (ST, sc)
        marks["ybuf"] = (MM, k.cnt[MM])

    def phase_B(p, xn_ev, e_vv):
        for ncz in range(2):
            b = k.bank()
            pairs = [(wrtb[:, kk * 8:(kk + 1) * 8],
                      xn[:, kk * TP + ncz * CH: kk * TP + ncz * CH + CH])
                     for kk in range(NF)]
            mc = k.chain(pb[b][0:8, 0:CH], pairs, waits=[(EV, xn_ev), (VV, 1)])
            k.w(s, MM, mc)
            ec = k.inc(s.copy(lgf[0:8, ncz * CH:ncz * CH + CH], pb[b][0:8, 0:CH]), EV)
            k.bank_free[b] = (EV, ec)
            b = k.bank()
            pairs = [(wsgtb[:, kk:kk + 1],
                      xn[:, kk * TP + ncz * CH: kk * TP + ncz * CH + CH])
                     for kk in range(NF)]
            mc = k.chain(pb[b][0:1, 0:CH], pairs)
            k.w(s, MM, mc)
            ec = k.inc(s.activation(sgf[0:1, ncz * CH:ncz * CH + CH],
                                    pb[b][0:1, 0:CH], AF.Sigmoid), EV)
            k.bank_free[b] = (EV, ec)
        lg_ev = k.cnt[EV]
        prev_ce_mm = 0
        for tt in range(8):
            b = k.bank()
            k.w(t, EV, lg_ev)
            mc = k.inc(t.transpose(pb[b][:, 0:8],
                                   lgf[0:8, tt * 128:(tt + 1) * 128],
                                   idnf[0:8, 0:8]), MM)
            k.w(s, MM, mc)
            ec = k.inc(s.copy(scr[:, 0:8], pb[b][:, 0:8]), EV)
            k.bank_free[b] = (EV, ec)
            k.w(v, EV, ec)
            vc = k.inc(v.reduce_max(msc[:, MRT:MRT + 1], scr[:, 0:8], AX.X,
                                    negate=True), VV)
            k.w(s, VV, vc)
            ec = k.inc(s.activation(scr[:, 8:16], scr[:, 0:8], AF.Exp,
                                    bias=msc[:, MRT:MRT + 1],
                                    accum_out=msc[:, MRT + 1:MRT + 2]), EV)
            k.w(v, EV, ec)
            vc = k.inc(v.reciprocal(msc[:, MRT + 2:MRT + 3],
                                    msc[:, MRT + 1:MRT + 2]), VV)
            k.w(v, VV, vc)
            vc = k.inc(v.tensor_scalar_mul(scr[:, 8:16], scr[:, 8:16],
                                           msc[:, MRT + 2:MRT + 3]), VV)
            k.w(v, VV, vc)
            vc = k.inc(v.reduce_max(msc[:, MRT + 3:MRT + 4], scr[:, 8:16], AX.X), VV)
            k.w(v, VV, vc)
            k.w(v, MM, prev_ce_mm)
            vc = k.inc(v.tensor_scalar(scr[:, 16:24], scr[:, 8:16],
                                       msc[:, MRT + 3:MRT + 4], None, ALU.is_lt), VV)
            k.w(v, VV, vc)
            vc = k.inc(v.tensor_mul(scr[:, 16:24], scr[:, 16:24], scr[:, 8:16]), VV)
            k.w(v, VV, vc)
            vc = k.inc(v.reduce_max(msc[:, MRT + 4:MRT + 5], scr[:, 16:24], AX.X), VV)
            k.w(v, VV, vc)
            vc = k.inc(v.tensor_scalar(scr[:, 16:24], scr[:, 8:16],
                                       msc[:, MRT + 4:MRT + 5], None, ALU.is_ge), VV)
            k.w(v, VV, vc)
            vc = k.inc(v.tensor_mul(scr[:, 16:24], scr[:, 16:24], scr[:, 8:16]), VV)
            b = k.bank()
            k.w(t, VV, vc)
            mc = k.inc(t.transpose(pb[b][0:8, 0:128], scr[:, 16:24], idnf[:, :]), MM)
            prev_ce_mm = mc
            k.w(s, MM, mc)
            ec = k.inc(s.copy(cebf[0:8, tt * 128:(tt + 1) * 128],
                              pb[b][0:8, 0:128]), EV)
            k.bank_free[b] = (EV, ec)
        ce_ev = k.cnt[EV]
        k.w(v, EV, lg_ev)
        vc = k.inc(v.tensor_copy(sgb[0:1, :], sgf[0:1, :]), VV)
        for ncz in range(2):
            b = k.bank()
            k.w(t, GG, 1)
            mc = k.chain(pb[b][:, 0:CH],
                         [(ones1[0:1, 0:128], sgb[0:1, ncz * CH:ncz * CH + CH])],
                         waits=[(VV, vc)])
            k.w(s, MM, mc)
            ec = k.inc(s.copy(sgbc[:, ncz * CH:ncz * CH + CH], pb[b][:, 0:CH]), EV)
            k.bank_free[b] = (EV, ec)
        sg_ev = k.cnt[EV]
        row_mm = k.cnt[MM]
        if DBG:
            k.st(dbg_ce[:, :], cebf[0:8, :], waits=[(EV, max(ce_ev, sg_ev))])
            k.st(dbg_sg[:, :], sgbc[:, :])
            k.st(dbg_lg[:, :], lgf[0:8, :])

        gate_st = [0] * 44
        for jt in range(44):
            stg, lc = k.stage([(lambda sg: st3(sg, NF, 128),
                                r3(w_b["wsgu"])[:, :, jt * 128:(jt + 1) * 128])],
                              extra_wait=[(WD, conv_done["wsgu"])])
            sl = spl_slot(s)
            last = 0
            for ncz in range(2):
                b = k.bank()
                pairs = [(stg[:, kk * 128:(kk + 1) * 128],
                          xn[:, kk * TP + ncz * CH: kk * TP + ncz * CH + CH])
                         for kk in range(NF)]
                mc = k.chain(pb[b][:, 0:CH], pairs, waits=[(LD, lc)])
                k.w(s, MM, mc)
                last = k.inc(s.activation(spl[:, sl * 1024 + ncz * CH:
                                              sl * 1024 + ncz * CH + CH],
                                          pb[b][:, 0:CH], AF.Silu), EV)
                k.bank_free[b] = (EV, last)
            k.stage_consumed(MM, k.cnt[MM])
            sc = k.st(sp_hg[jt * 128:(jt + 1) * 128, :],
                      spl[:, sl * 1024:sl * 1024 + 1024], waits=[(EV, last)])
            gate_st[jt] = sc
            spl_free[sl] = (ST, sc)
        ymark = marks["ybuf"]
        yb_free = [ymark if ymark[0] else (None, 0)] * 4
        for jt in range(44):
            stg, lc = k.stage([(lambda sg: st3(sg, NF, 128),
                                r3(w_b["wsgu"])[:, :, (44 + jt) * 128:(45 + jt) * 128])])
            ysl = jt % 4
            glc = k.ld(ybuf[:, ysl * 1024:(ysl + 1) * 1024],
                       sp_hg[jt * 128:(jt + 1) * 128, :],
                       waits=[(ST, gate_st[jt])] +
                             ([yb_free[ysl]] if yb_free[ysl][0] else []))
            sl = spl_slot(v)
            lastv = 0
            for ncz in range(2):
                b = k.bank()
                pairs = [(stg[:, kk * 128:(kk + 1) * 128],
                          xn[:, kk * TP + ncz * CH: kk * TP + ncz * CH + CH])
                         for kk in range(NF)]
                mc = k.chain(pb[b][:, 0:CH], pairs, waits=[(LD, lc)])
                k.w(v, MM, mc)
                k.w(v, LD, glc)
                k.w(v, EV, sg_ev)
                v.tensor_mul(scr[:, 512:1024], pb[b][:, 0:CH],
                             ybuf[:, ysl * 1024 + ncz * CH: ysl * 1024 + ncz * CH + CH])
                lastv = k.inc(v.tensor_mul(spl[:, sl * 1024 + ncz * CH:
                                               sl * 1024 + ncz * CH + CH],
                                           scr[:, 512:1024],
                                           sgbc[:, ncz * CH:ncz * CH + CH]), VV)
                k.bank_free[b] = (VV, lastv)
            k.stage_consumed(MM, k.cnt[MM])
            yb_free[ysl] = (VV, lastv)
            sc = k.st(sp_hsh[jt * 128:(jt + 1) * 128, :],
                      spl[:, sl * 1024:sl * 1024 + 1024], waits=[(VV, lastv)])
            spl_free[sl] = (ST, sc)
        hsh_st = k.cnt[ST]
        yb_last = (VV, k.cnt[VV])

        rW = r3(w_b["wsdn"])
        rH = sp_hsh.ap().rearrange("(a p) c -> p a c", p=128)
        yb_half_free = [yb_last, yb_last]
        for mg2 in range(8):
            banks = [k.bank(), k.bank(), k.bank(), k.bank()]
            runs = [(0, 8), (8, 16), (16, 24), (24, 32), (32, 40), (40, 44)]
            for ri, (ka, kb_) in enumerate(runs):
                nk = kb_ - ka
                stg, lc = k.stage([(lambda sg, nk=nk: st3(sg, nk, 256),
                                    rW[:, ka:kb_, mg2 * 256:(mg2 + 1) * 256])],
                                  extra_wait=[(WD, conv_done["wsdn"])])
                yh = ri % 2
                ylc = k.ld(ybuf[:, yh * 8192:yh * 8192 + nk * TP]
                           .rearrange("p (a c) -> p a c", c=TP),
                           rH[:, ka:kb_, :],
                           waits=[(ST, hsh_st)] +
                                 ([yb_half_free[yh]] if yb_half_free[yh][0] else []))
                for m2 in range(2):
                    for ncz in range(2):
                        bi = banks[m2 * 2 + ncz]
                        pairs = [(stg[:, kk * 256 + m2 * 128: kk * 256 + m2 * 128 + 128],
                                  ybuf[:, yh * 8192 + kk * TP + ncz * CH:
                                       yh * 8192 + kk * TP + ncz * CH + CH])
                                 for kk in range(nk)]
                        k.chain(pb[bi][:, 0:CH], pairs,
                                waits=[(LD, max(lc, ylc))],
                                start=(ri == 0), stop=(ri == len(runs) - 1))
                yb_half_free[yh] = (MM, k.cnt[MM])
                k.stage_consumed(MM, k.cnt[MM])
            for m2 in range(2):
                for ncz in range(2):
                    bi = banks[m2 * 2 + ncz]
                    mglob = mg2 * 2 + m2
                    k.w(s, MM, k.cnt[MM])
                    k.w(s, VV, e_vv)
                    ec = k.inc(s.copy(bigF[:, mglob * TP + ncz * CH:
                                           mglob * TP + ncz * CH + CH],
                                      pb[bi][:, 0:CH]), EV)
                    k.bank_free[bi] = (EV, ec)

        for e in range(8):
            ex_mm = k.cnt[MM]
            for ncz in range(2):
                b = k.bank()
                mc = k.chain(pb[b][:, 0:CH],
                             [(sel8[:, e * 128:(e + 1) * 128],
                               cebf[0:8, ncz * CH:ncz * CH + CH])],
                             waits=[(EV, ce_ev), (LD, 16 * 2)])
                k.w(s, MM, mc)
                ec = k.inc(s.copy(cebc[:, ncz * CH:ncz * CH + CH], pb[b][:, 0:CH]), EV)
                k.bank_free[b] = (EV, ec)
            ce_bc_ev = k.cnt[EV]
            wge = w_b["wegu"].ap()[e * D:(e + 1) * D, :].rearrange(
                "(a p) c -> p a c", p=128)
            for jt in range(11):
                stg, lc = k.stage([(lambda sg: st3(sg, NF, 128),
                                    wge[:, :, jt * 128:(jt + 1) * 128])],
                                  extra_wait=[(WD, conv_done["wegu"])])
                for ncz in range(2):
                    b = k.bank()
                    pairs = [(stg[:, kk * 128:(kk + 1) * 128],
                              xn[:, kk * TP + ncz * CH: kk * TP + ncz * CH + CH])
                             for kk in range(NF)]
                    mc = k.chain(pb[b][:, 0:CH], pairs, waits=[(LD, lc)])
                    k.w(s, MM, max(mc, ex_mm))
                    ec = k.inc(s.activation(ybuf[:, jt * TP + ncz * CH:
                                                 jt * TP + ncz * CH + CH],
                                            pb[b][:, 0:CH], AF.Silu), EV)
                    k.bank_free[b] = (EV, ec)
                k.stage_consumed(MM, k.cnt[MM])
            for jt in range(11):
                stg, lc = k.stage([(lambda sg: st3(sg, NF, 128),
                                    wge[:, :, (11 + jt) * 128:(12 + jt) * 128])])
                for ncz in range(2):
                    b = k.bank()
                    pairs = [(stg[:, kk * 128:(kk + 1) * 128],
                              xn[:, kk * TP + ncz * CH: kk * TP + ncz * CH + CH])
                             for kk in range(NF)]
                    mc = k.chain(pb[b][:, 0:CH], pairs, waits=[(LD, lc)])
                    k.w(v, MM, mc)
                    k.w(v, EV, max(ce_bc_ev, k.cnt[EV]))
                    v.tensor_mul(scr[:, 512:1024], pb[b][:, 0:CH],
                                 ybuf[:, jt * TP + ncz * CH: jt * TP + ncz * CH + CH])
                    vc = k.inc(v.tensor_mul(ybuf[:, jt * TP + ncz * CH:
                                                 jt * TP + ncz * CH + CH],
                                            scr[:, 512:1024],
                                            cebc[:, ncz * CH:ncz * CH + CH]), VV)
                    k.bank_free[b] = (VV, vc)
                k.stage_consumed(MM, k.cnt[MM])
            he_vv = k.cnt[VV]
            wde = w_b["wedn"].ap()[e * 1408:(e + 1) * 1408, :].rearrange(
                "(a p) c -> p a c", p=128)
            for mg2 in range(8):
                stg, lc = k.stage([(lambda sg: st3(sg, 11, 256),
                                    wde[:, :, mg2 * 256:(mg2 + 1) * 256])],
                                  extra_wait=[(WD, conv_done["wedn"])])
                for m2 in range(2):
                    mglob = mg2 * 2 + m2
                    for ncz in range(2):
                        b = k.bank()
                        pairs = [(stg[:, kk * 256 + m2 * 128:
                                      kk * 256 + m2 * 128 + 128],
                                  ybuf[:, kk * TP + ncz * CH: kk * TP + ncz * CH + CH])
                                 for kk in range(11)]
                        mc = k.chain(pb[b][:, 0:CH], pairs,
                                     waits=[(LD, lc), (VV, he_vv)])
                        k.w(v, MM, mc)
                        dst = bigF[:, mglob * TP + ncz * CH: mglob * TP + ncz * CH + CH]
                        vc = k.inc(v.tensor_add(dst, dst, pb[b][:, 0:CH]), VV)
                        k.bank_free[b] = (VV, vc)
                k.stage_consumed(MM, k.cnt[MM])
        f_vv = k.cnt[VV]
        if DBG:
            for mf in range(NF):
                k.st(dbg_F[p * D + mf * 128: p * D + (mf + 1) * 128, :],
                     bigF[:, mf * TP:(mf + 1) * TP], waits=[(VV, f_vv)])
        for mf in range(NF):
            sl = spl_slot(s)
            k.w(s, VV, f_vv)
            ec = k.inc(s.activation(spl[:, sl * 1024:sl * 1024 + 1024],
                                    bigF[:, mf * TP:(mf + 1) * TP], AF.Silu), EV)
            sc = k.st(sp_g[p * D + mf * 128: p * D + (mf + 1) * 128, :],
                      spl[:, sl * 1024:sl * 1024 + 1024], waits=[(EV, ec)])
            spl_free[sl] = (ST, sc)
        marks["ybuf"] = (MM, k.cnt[MM])
        return (EV, k.cnt[EV])

    def phase_CC():
        sc = k.st(cc_in[:, :], msc[:, MTOT:MTOT + 32], waits=[(VV, k.cnt[VV])])
        k.w(g, ST, sc)
        cc = k.inc(g.collective_compute(
            "AllGather", ALU.bypass,
            replica_groups=[[0, 1], [2, 3], [4, 5], [6, 7]],
            ins=[cc_in.ap()], outs=[cc_out.ap()]), CC, 1)
        k.w(sy, CC, cc)
        lcc = k.ld(msc[:, MCAR:MCAR + 32], cc_out[0:128, :])
        k.w(v, LD, lcc)
        vc = k.inc(v.tensor_scalar_mul(msc[:, MCAR:MCAR + 32],
                                       msc[:, MCAR:MCAR + 32],
                                       msc[:, MMK:MMK + 1]), VV)
        return vc

    def phase_J(p, car_vv, big2_mark):
        sn = k.snap()
        yb_free = [(None, 0)] * 8
        for mf in range(NF):
            r0 = p * 2 * D + mf * 128
            sls = [(mf % 2) * 4 + i for i in range(4)]
            lc = 0
            for i, (sp, pl) in enumerate([(sp_xh, 0), (sp_xh, 1),
                                          (sp_sf, 0), (sp_sf, 1)]):
                ysl = sls[i]
                wts = [(ST, sn[ST])]
                if yb_free[ysl][0]:
                    wts.append(yb_free[ysl])
                elif mf < 2:
                    wts += [(MM, sn[MM]), (VV, sn[VV])]
                lc = k.ld(ybuf[:, ysl * 1024:(ysl + 1) * 1024],
                          sp[r0 + pl * D:r0 + pl * D + 128, :], waits=wts)
            xhR = ybuf[:, sls[0] * 1024:sls[0] * 1024 + 1024]
            xhI = ybuf[:, sls[1] * 1024:sls[1] * 1024 + 1024]
            sR = ybuf[:, sls[2] * 1024:sls[2] * 1024 + 1024]
            sI = ybuf[:, sls[3] * 1024:sls[3] * 1024 + 1024]
            carR = msc[:, MCAR + mf:MCAR + mf + 1]
            carI = msc[:, MCAR + 16 + mf:MCAR + 16 + mf + 1]
            k.w(v, LD, lc)
            k.w(v, VV, car_vv)
            if big2_mark[0]:
                k.w(v, big2_mark[0], big2_mark[1])
            x1 = scr[:, 0:1024]
            x2 = scr[:, 1024:2048]
            t1 = scr[:, 2048:3072]
            t2 = scr[:, 3072:4096]
            v.tensor_scalar(x1, xhR, carR, None, ALU.add)
            v.tensor_scalar(x2, xhI, carI, None, ALU.add)
            taR = big2[:, mf * TP:(mf + 1) * TP]
            taI = big2[:, NF * TP + mf * TP:NF * TP + (mf + 1) * TP]
            v.tensor_mul(t1, sR, x1)
            v.tensor_mul(t2, sI, x2)
            v.tensor_add(taR, t1, t2)
            v.tensor_mul(t1, sI, x1)
            v.tensor_mul(t2, sR, x2)
            v.tensor_sub(taI, t1, t2)
            v.tensor_mul(t1, sR, sR)
            v.tensor_mul(t2, sI, sI)
            v.tensor_add(t1, t1, t2)
            v.tensor_scalar(t1, t1, -1.0 / D, 1.0 / D, ALU.mult, ALU.add)
            slR = spl_slot(v)
            slI = spl_slot(v)
            v.tensor_mul(spl[:, slR * 1024:slR * 1024 + 1024], t1, x1)
            vc = k.inc(v.tensor_mul(spl[:, slI * 1024:slI * 1024 + 1024], t1, x2), VV)
            for ysl in sls:
                yb_free[ysl] = (VV, vc)
            sc = k.st(sp_z0[r0:r0 + 128, :], spl[:, slR * 1024:slR * 1024 + 1024],
                      waits=[(VV, vc)])
            sc = k.st(sp_z0[r0 + D:r0 + D + 128, :],
                      spl[:, slI * 1024:slI * 1024 + 1024])
            spl_free[slR] = (ST, sc)
            spl_free[slI] = (ST, sc)
            if DBG:
                k.st(sp_ta[r0:r0 + 128, :], taR, waits=[(VV, vc)])
                k.st(sp_ta[r0 + D:r0 + D + 128, :], taI)
        return (VV, k.cnt[VV])

    def phase_M1(p, ta_vv):
        h1_st = [0] * 64
        for jg in range(64):
            stg, lc = k.stage([(lambda sg: st3(sg, 32, 128),
                                r3(w_b["w1"])[:, :, jg * 128:(jg + 1) * 128])],
                              extra_wait=[(WD, conv_done["w1"])])
            sl = spl_slot(s)
            last = 0
            for ncz in range(2):
                b = k.bank()
                pairs = [(stg[:, kk * 128:(kk + 1) * 128],
                          big2[:, kk * TP + ncz * CH: kk * TP + ncz * CH + CH])
                         for kk in range(32)]
                mc = k.chain(pb[b][:, 0:CH], pairs, waits=[(LD, lc), ta_vv])
                k.w(s, MM, mc)
                last = k.inc(s.activation(spl[:, sl * 1024 + ncz * CH:
                                              sl * 1024 + ncz * CH + CH],
                                          pb[b][:, 0:CH], AF.Relu,
                                          bias=msc[:, MB1 + jg:MB1 + jg + 1]), EV)
                k.bank_free[b] = (EV, last)
            k.stage_consumed(MM, k.cnt[MM])
            sc = k.st(sp_h1[jg * 128:(jg + 1) * 128, :],
                      spl[:, sl * 1024:sl * 1024 + 1024], waits=[(EV, last)])
            h1_st[jg] = sc
            spl_free[sl] = (ST, sc)
        return k.cnt[MM], h1_st

    def phase_M2(p, h1_st):
        rW2 = r3(w_b["w2"])
        rH1 = sp_h1.ap().rearrange("(a p) c -> p a c", p=128)
        xn_half_free = [marks["xnreg"], marks["xnreg"]]
        for og in range(16):
            banks = [k.bank(), k.bank(), k.bank(), k.bank()]
            for ri in range(8):
                ka, kb_ = ri * 8, (ri + 1) * 8
                stg, lc = k.stage([(lambda sg: st3(sg, 8, 256),
                                    rW2[:, ka:kb_, og * 256:(og + 1) * 256])],
                                  extra_wait=[(WD, conv_done["w2"])])
                xh_ = ri % 2
                hlc = k.ld(xn[:, xh_ * 8192:xh_ * 8192 + 8 * TP]
                           .rearrange("p (a c) -> p a c", c=TP),
                           rH1[:, ka:kb_, :],
                           waits=[(ST, h1_st[ri * 8 + 7])] +
                                 ([xn_half_free[xh_]] if xn_half_free[xh_][0] else []))
                for m2 in range(2):
                    for ncz in range(2):
                        bi = banks[m2 * 2 + ncz]
                        pairs = [(stg[:, kk * 256 + m2 * 128:
                                      kk * 256 + m2 * 128 + 128],
                                  xn[:, xh_ * 8192 + kk * TP + ncz * CH:
                                     xh_ * 8192 + kk * TP + ncz * CH + CH])
                                 for kk in range(8)]
                        k.chain(pb[bi][:, 0:CH], pairs,
                                waits=[(LD, max(lc, hlc))],
                                start=(ri == 0), stop=(ri == 7))
                xn_half_free[xh_] = (MM, k.cnt[MM])
                k.stage_consumed(MM, k.cnt[MM])
            for m2 in range(2):
                mglob = og * 2 + m2
                ssl = spl_slot(s)
                last = 0
                for ncz in range(2):
                    bi = banks[m2 * 2 + ncz]
                    k.w(s, MM, k.cnt[MM])
                    last = k.inc(s.activation(spl[:, ssl * 1024 + ncz * CH:
                                                  ssl * 1024 + ncz * CH + CH],
                                              pb[bi][:, 0:CH], AF.Identity,
                                              bias=msc[:, MB2 + mglob:
                                                       MB2 + mglob + 1]), EV)
                    k.bank_free[bi] = (EV, last)
                sc = k.st(sp_h[mglob * 128:(mglob + 1) * 128, :],
                          spl[:, ssl * 1024:ssl * 1024 + 1024], waits=[(EV, last)])
                spl_free[ssl] = (ST, sc)
        marks["xnreg"] = (MM, k.cnt[MM])

    def phase_L(p, ta_done_mm):
        sn = k.snap()
        yb_free = [(None, 0)] * 12
        for mf in range(NF):
            r0 = p * 2 * D + mf * 128
            sls = [(mf % 2) * 6 + i for i in range(6)]
            srcs = [(sp_h, mf * 128), (sp_h, D + mf * 128),
                    (sp_tf, r0), (sp_tf, r0 + D),
                    (sp_z0, r0), (sp_z0, r0 + D)]
            lc = 0
            for i, (sp, rr) in enumerate(srcs):
                ysl = sls[i]
                wts = [(ST, sn[ST])]
                if yb_free[ysl][0]:
                    wts.append(yb_free[ysl])
                elif mf < 2:
                    wts += [(MM, sn[MM]), (VV, sn[VV])]
                lc = k.ld(ybuf[:, ysl * 1024:(ysl + 1) * 1024],
                          sp[rr:rr + 128, :], waits=wts)
            h1c = ybuf[:, sls[0] * 1024:sls[0] * 1024 + 1024]
            h2c = ybuf[:, sls[1] * 1024:sls[1] * 1024 + 1024]
            tR = ybuf[:, sls[2] * 1024:sls[2] * 1024 + 1024]
            tI = ybuf[:, sls[3] * 1024:sls[3] * 1024 + 1024]
            z0R = ybuf[:, sls[4] * 1024:sls[4] * 1024 + 1024]
            z0I = ybuf[:, sls[5] * 1024:sls[5] * 1024 + 1024]
            k.w(v, LD, lc)
            k.w(v, MM, ta_done_mm)
            t1 = scr[:, 0:1024]
            t2 = scr[:, 1024:2048]
            zR = big2[:, mf * TP:(mf + 1) * TP]
            zP = big2[:, NF * TP + mf * TP:NF * TP + (mf + 1) * TP]
            v.tensor_mul(t1, tR, h1c)
            v.tensor_mul(t2, tI, h2c)
            v.tensor_add(t1, t1, t2)
            v.scalar_tensor_tensor(zR, t1, 1.0 / D, z0R, ALU.mult, ALU.add)
            v.tensor_mul(t1, tI, h1c)
            v.tensor_mul(t2, tR, h2c)
            v.tensor_sub(t1, t1, t2)
            vc = k.inc(v.scalar_tensor_tensor(zP, t1, 1.0 / D, z0I,
                                              ALU.mult, ALU.add), VV)
            for ysl in sls:
                yb_free[ysl] = (VV, vc)
        return (VV, k.cnt[VV])

    def phase_M(p, z_vv):
        sn = k.snap()
        g_free = [(None, 0)] * 2
        for mf in range(NF):
            stg, lc = k.stage([
                (lambda sg_: sg_[:, 0:NF * 128].rearrange("p (a c) -> p a c", c=128),
                 r3(cmat)[:, :, mf * 128:(mf + 1) * 128]),
                (lambda sg_: sg_[:, NF * 128:2 * NF * 128].rearrange(
                    "p (a c) -> p a c", c=128),
                 r3(smat)[:, :, mf * 128:(mf + 1) * 128])])
            gsl = mf % 2
            wts = [(ST, sn[ST])]
            if g_free[gsl][0]:
                wts.append(g_free[gsl])
            else:
                wts.append((MM, marks["xnreg"][1]))
            glc = k.ld(xn[:, gsl * 1024:(gsl + 1) * 1024],
                       sp_g[p * D + mf * 128: p * D + (mf + 1) * 128, :], waits=wts)
            for ncz in range(2):
                b = k.bank()
                pairs = ([(stg[:, kk * 128:(kk + 1) * 128],
                           big2[:, kk * TP + ncz * CH: kk * TP + ncz * CH + CH])
                          for kk in range(NF)] +
                         [(stg[:, NF * 128 + kk * 128: NF * 128 + (kk + 1) * 128],
                           big2[:, NF * TP + kk * TP + ncz * CH:
                                NF * TP + kk * TP + ncz * CH + CH])
                          for kk in range(NF)])
                mc = k.chain(pb[b][:, 0:CH], pairs, waits=[(LD, lc), z_vv])
                k.w(v, MM, mc)
                k.w(v, LD, glc)
                if mf == 0 and ncz == 0:
                    k.wsnap(v, sn, keys=(MM,))
                vc = k.inc(v.tensor_mul(ybuf[:, mf * TP + ncz * CH:
                                             mf * TP + ncz * CH + CH],
                                        pb[b][:, 0:CH],
                                        xn[:, gsl * 1024 + ncz * CH:
                                           gsl * 1024 + ncz * CH + CH]), VV)
                k.bank_free[b] = (VV, vc)
                g_free[gsl] = (VV, vc)
            k.stage_consumed(MM, k.cnt[MM])
        sg_vv = k.cnt[VV]
        x_free = (None, 0)
        prev_store = 0
        for tt in range(8):
            row0 = p * TP + tt * 128
            wts = [(VV, sn[VV]), (EV, sn[EV])] if tt == 0 else []
            if x_free[0]:
                wts.append(x_free)
            xlc = k.ld(scr[:, 2048:4096], x_in[row0:row0 + 128, :], waits=wts)
            lastv = 0
            for mf in range(NF):
                b = k.ptbank()
                k.w(t, VV, sg_vv)
                mc = k.inc(t.transpose(pt[b][:, 0:128],
                                       ybuf[:, mf * TP + tt * 128:
                                            mf * TP + (tt + 1) * 128],
                                       idn[:, :]), MM)
                k.w(v, MM, mc)
                k.w(v, LD, xlc)
                if mf == 0:
                    k.w(v, ST, prev_store)
                lastv = k.inc(v.tensor_add(scr[:, mf * 128:(mf + 1) * 128],
                                           scr[:, 2048 + mf * 128:
                                               2048 + (mf + 1) * 128],
                                           pt[b][:, 0:128]), VV)
                k.pt_free[b] = (VV, lastv)
            sc = k.st(out[row0:row0 + 128, :], scr[:, 0:2048], waits=[(VV, lastv)])
            prev_store = sc
            x_free = (VV, lastv)

    b2_mark = (None, 0)
    for p in range(NP):
        xn_ev = phase_A(p)
        kf_ev = phase_C(xn_ev, b2_mark)
        kv_vv = phase_D(xn_ev, kf_ev)
        e_vv = phase_E(p, xn_ev, kv_vv)
        phase_GH(p, "ws", sp_sf, xn_ev)
        phase_GH(p, "wt", sp_tf, xn_ev)
        b2_mark = phase_B(p, xn_ev, e_vv)
    car_vv = phase_CC()
    marks["xnreg"] = (MM, k.cnt[MM])
    l_done_mm = 0
    for p in range(NP):
        big2_mark = b2_mark if p == 0 else (MM, l_done_mm)
        ta_vv = phase_J(p, car_vv, big2_mark)
        m1_mm, h1_st = phase_M1(p, ta_vv)
        phase_M2(p, h1_st)
        zvv = phase_L(p, m1_mm)
        phase_M(p, zvv)
        l_done_mm = k.cnt[MM]
    sy.wait_ge(k.sem[ST], k.cnt[ST])
    nc._kb_stack = stack
    return nc


_HOST = {}


def _host_constants():
    if not _HOST:
        n = np.arange(D, dtype=np.float64)
        th = 2.0 * np.pi * np.outer(n, n) / D
        _HOST["cmat"] = np.cos(th).astype(ml_dtypes.bfloat16)
        _HOST["smat"] = np.sin(th).astype(ml_dtypes.bfloat16)
        _HOST["idn"] = np.eye(128, dtype=ml_dtypes.bfloat16)
        s8 = np.zeros((8, 8 * 128), dtype=ml_dtypes.bfloat16)
        for e in range(8):
            s8[e, e * 128:(e + 1) * 128] = 1.0
        _HOST["sel8"] = s8
    return _HOST


def kernel(**inputs):
    hc = _host_constants()
    f32 = lambda a: np.ascontiguousarray(np.asarray(a, np.float32))
    x = f32(inputs["L0_attn_x"]).reshape(8 * T, D)
    shared = {
        "wq": f32(inputs["L0_attn_Wq"]), "wk": f32(inputs["L0_attn_Wk"]),
        "wv": f32(inputs["L0_attn_Wv"]), "ws": f32(inputs["L0_attn_Ws"]),
        "wt": f32(inputs["L0_attn_Wt"]),
        "w1": f32(inputs["L0_attn_W1"]), "w2": f32(inputs["L0_attn_W2"]),
        "wsgu": f32(inputs["L0_ffn_Ws_gate_up"]),
        "wsdn": f32(inputs["L0_ffn_Ws_down"]),
        "wegu": f32(inputs["L0_ffn_We_gate_up"]).reshape(8 * D, 2816),
        "wedn": f32(inputs["L0_ffn_We_down"]).reshape(8 * 1408, D),
        "wrt": np.ascontiguousarray(
            f32(inputs["L0_ffn_W_router"]).reshape(8, 16, 128)
            .transpose(2, 1, 0).reshape(128, 128)),
        "wsgt": np.ascontiguousarray(
            f32(inputs["L0_ffn_W_sgate"]).reshape(16, 128).T),
        "b1t": np.ascontiguousarray(f32(inputs["L0_attn_b1"]).reshape(64, 128).T),
        "b2t": np.ascontiguousarray(f32(inputs["L0_attn_b2"]).reshape(32, 128).T),
        "cmat": hc["cmat"], "smat": hc["smat"], "idn": hc["idn"],
        "sel8": hc["sel8"],
    }
    in_maps = []
    for c in range(8):
        m = dict(shared)
        m["x"] = np.ascontiguousarray(x[c * T:(c + 1) * T])
        m["mask"] = np.full((128, 1), float(c % 2), np.float32)
        in_maps.append(m)
    nc = build_nc()
    res = run_bass_kernel_spmd(nc, in_maps, core_ids=list(range(8)))
    out = np.concatenate([np.asarray(r["out"]) for r in res.results], axis=0)
    if DBG:
        kernel.dbg = res.results
    return out.reshape(4, 4096, D).astype(np.float32)


# version 04
# speedup vs baseline: 1.1948x; 1.1948x over previous
import numpy as np
import ml_dtypes
from contextlib import ExitStack
import concourse.bass as bass
import concourse.mybir as mybir
from concourse.bass_utils import run_bass_kernel_spmd

F32 = mybir.dt.float32
BF16 = mybir.dt.bfloat16
AF = mybir.ActivationFunctionType
ALU = mybir.AluOpType
AX = mybir.AxisListType

D = 2048
NF = 16
TP = 1024
NP = 2
T = 2048
CH = 512

DBG = False

EV, VV, MM, LD, ST, WD, GG, CC = "ev", "vv", "mm", "ld", "st", "wd", "gg", "cc"


class KB:
    def __init__(self, nc, stack):
        self.nc = nc
        self.cnt = {kk: 0 for kk in (EV, VV, MM, LD, ST, WD, GG, CC)}
        self.sem = {kk: stack.enter_context(nc.semaphore("s_" + kk))
                    for kk in (EV, VV, MM, LD, ST, WD, GG, CC)}
        self.waited = {}
        self.bank_free = [(None, 0)] * 6
        self.bidx = 0
        self.pt_free = [(None, 0)] * 2
        self.pti = 0
        self.stg_free = [(None, 0)] * 2
        self.stg_i = 0

    def w(self, eng, key, val):
        if not key or val is None or val <= 0:
            return
        kk = (eng.engine, key)
        if self.waited.get(kk, 0) >= val:
            return
        eng.wait_ge(self.sem[key], val)
        self.waited[kk] = val

    def snap(self):
        return {kk: self.cnt[kk] for kk in (EV, VV, MM, LD, ST, WD)}

    def wsnap(self, eng, sn, keys=(EV, VV, MM, ST)):
        for kk in keys:
            self.w(eng, kk, sn[kk])

    def inc(self, instr, key, n=1):
        instr.then_inc(self.sem[key], n)
        self.cnt[key] += n
        return self.cnt[key]

    def ld(self, out, in_, waits=()):
        for kk, vv in waits:
            self.w(self.nc.sync, kk, vv)
        return self.inc(self.nc.sync.dma_start(out=out, in_=in_), LD, 16)

    def st(self, out, in_, waits=()):
        for kk, vv in waits:
            self.w(self.nc.sync, kk, vv)
        return self.inc(self.nc.sync.dma_start(out=out, in_=in_), ST, 16)

    def bank(self):
        b = self.bidx % 6
        self.bidx += 1
        kk, vv = self.bank_free[b]
        self.w(self.nc.tensor, kk, vv)
        return b

    def ptbank(self):
        b = self.pti % 2
        self.pti += 1
        kk, vv = self.pt_free[b]
        self.w(self.nc.tensor, kk, vv)
        return b

    def chain(self, bank_ap, pairs, waits=(), start=True, stop=True):
        t = self.nc.tensor
        for kk, vv in waits:
            self.w(t, kk, vv)
        n = len(pairs)
        mm = None
        for i, (l, r) in enumerate(pairs):
            mm = t.matmul(bank_ap, l, r,
                          start=(start and i == 0), stop=(stop and i == n - 1))
        return self.inc(mm, MM, 1)

    def chain_pair(self, bA, bB, trips, waits=(), start=True, stop=True):
        t = self.nc.tensor
        for kk, vv in waits:
            self.w(t, kk, vv)
        n = len(trips)
        mm = None
        for i, (l, ra, rb) in enumerate(trips):
            st_, sp_ = (start and i == 0), (stop and i == n - 1)
            t.matmul(bA, l, ra, start=st_, stop=sp_)
            mm = t.matmul(bB, l, rb, start=st_, stop=sp_)
        return self.inc(mm, MM, 1)

    def stage(self, loads, extra_wait=(), cast=False):
        h = self.stg_i % 2
        self.stg_i += 1
        kk, vv = self.stg_free[h]
        eng = self.nc.gpsimd if cast else self.nc.sync
        key = WD if cast else LD
        waits = list(extra_wait) + ([(kk, vv)] if kk else [])
        stg = self.stgv[h]
        lc = 0
        for dvf, src in loads:
            for kx, vx in waits:
                self.w(eng, kx, vx)
            waits = []
            lc = self.inc(eng.dma_start(out=dvf(stg), in_=src), key, 16)
        self._last_half = h
        return stg, lc, key

    def stage_consumed(self, key, val):
        self.stg_free[self._last_half] = (key, val)


def build_nc():
    nc = bass.Bass("TRN2", target_bir_lowering=False)
    EI, EO, IN = "ExternalInput", "ExternalOutput", "Internal"
    dt = nc.dram_tensor
    x_in = dt("x", [T, D], F32, kind=EI)
    w_f = {n: dt(n, sh, F32, kind=EI) for n, sh in [
        ("wq", [D, D]), ("wk", [D, D]), ("wv", [D, D]), ("ws", [D, D]), ("wt", [D, D]),
        ("w1", [2 * D, 4 * D]), ("w2", [4 * D, 2 * D]),
        ("wsgu", [D, 11264]), ("wsdn", [5632, D]),
        ("wegu", [8 * D, 2816]), ("wedn", [8 * 1408, D])]}
    wrt = dt("wrt", [128, 128], F32, kind=EI)
    wsgt = dt("wsgt", [128, 16], F32, kind=EI)
    b1t_in = dt("b1t", [128, 64], F32, kind=EI)
    b2t_in = dt("b2t", [128, 32], F32, kind=EI)
    cmat = dt("cmat", [D, D], BF16, kind=EI)
    smat = dt("smat", [D, D], BF16, kind=EI)
    idn_in = dt("idn", [128, 128], BF16, kind=EI)
    sel8_in = dt("sel8", [8, 8 * 128], BF16, kind=EI)
    mask_in = dt("mask", [128, 1], F32, kind=EI)
    out = dt("out", [T, D], F32, kind=EO)

    spk = EO if DBG else IN
    sp_xh = dt("sp_xh", [NP * 2 * D, TP], BF16, kind=spk)
    sp_sf = dt("sp_sf", [NP * 2 * D, TP], BF16, kind=spk)
    sp_tf = dt("sp_tf", [NP * 2 * D, TP], BF16, kind=spk)
    sp_g = dt("sp_g", [NP * D, TP], BF16, kind=spk)
    sp_z0 = dt("sp_z0", [NP * 2 * D, TP], BF16, kind=spk)
    sp_h1 = dt("sp_h1", [4 * D, TP], BF16, kind=spk)
    sp_h = dt("sp_h", [2 * D, TP], BF16, kind=spk)
    sp_hg = dt("sp_hg", [5632, TP], BF16, kind=spk)
    sp_hsh = dt("sp_hsh", [5632, TP], BF16, kind=spk)
    sp_ta = dt("sp_ta", [NP * 2 * D, TP], BF16, kind=spk)
    sp_xn = dt("sp_xn", [NP * D, TP], BF16, kind=spk)
    dbg_ce = dt("dbg_ce", [8, TP], BF16, kind=spk)
    dbg_lg = dt("dbg_lg", [8, TP], F32, kind=spk)
    dbg_sg = dt("dbg_sg", [128, TP], BF16, kind=spk)
    dbg_F = dt("dbg_F", [NP * D, TP], F32, kind=spk)
    cc_in = dt("cc_in", [128, 32], F32, kind=IN)
    cc_out = dt("cc_out", [256, 32], F32, kind=IN)

    stack = ExitStack()
    sb = lambda n, sh, d: stack.enter_context(nc.sbuf_tensor(n, sh, d))
    psal = lambda n, sh, d: stack.enter_context(nc.psum_tensor(n, sh, d))
    xn = sb("xn", [128, NF * TP], BF16)
    ybuf = sb("ybuf", [128, NF * TP], BF16)
    big2 = sb("big2", [128, 2 * NF * TP], BF16)
    bigF = big2.bitcast(F32)
    stg0 = sb("stg0", [128, 4096], BF16)
    stg1 = sb("stg1", [128, 4096], BF16)
    scr = sb("scr", [128, 4096], F32)
    spl = sb("spl", [128, 4096], BF16)
    msc = sb("msc", [128, 1024], F32)
    idn = sb("idnb", [128, 128], BF16)
    idnf = sb("idnf", [128, 128], F32)
    ones1 = sb("ones1", [1, 128], BF16)
    sel8 = sb("sel8b", [8, 8 * 128], BF16)
    wrtb = sb("wrtb", [128, 128], BF16)
    wsgtb = sb("wsgtb", [128, 16], BF16)
    lgf = sb("lgf", [8, TP], F32)
    cebf = sb("cebf", [8, TP], BF16)
    sgf = sb("sgf", [1, TP], F32)
    sgb = sb("sgb", [1, TP], BF16)
    sgbc = sb("sgbc", [128, TP], BF16)
    cebc = sb("cebc", [128, TP], BF16)
    pb = [psal(f"pb{i}", [128, 512], F32) for i in range(6)]
    pt = [psal(f"pt{i}", [128, 512], BF16) for i in range(2)]

    MB1, MB2, MMK = 0, 64, 96
    MLN = 100
    MTOT = 200
    MCAR = 232
    MRT = 264

    k = KB(nc, stack)
    k.stgv = [stg0, stg1]
    sy, g, t, s, v = nc.sync, nc.gpsimd, nc.tensor, nc.scalar, nc.vector

    r3 = lambda w: w.ap().rearrange("(a p) c -> p a c", p=128)
    st3 = lambda stg, n, c: stg[:, 0:n * c].rearrange("p (a c) -> p a c", c=c)

    k.ld(idn[:, :], idn_in[:, :])
    k.ld(sel8[:, :], sel8_in[:, :])
    k.ld(msc[:, MB1:MB1 + 64], b1t_in[:, :])
    k.ld(msc[:, MB2:MB2 + 32], b2t_in[:, :])
    k.ld(msc[:, MMK:MMK + 1], mask_in[:, :])
    lc = k.ld(scr[:, 0:128], wrt[:, :])
    lc = k.ld(scr[:, 128:144], wsgt[:, :])
    k.w(v, LD, lc)
    v.tensor_copy(wrtb[:, :], scr[:, 0:128])
    v.tensor_copy(wsgtb[:, :], scr[:, 128:144])
    v.tensor_copy(idnf[:, :], idn[:, :])
    k.inc(v.memset(msc[:, MTOT:MTOT + 32], 0.0), VV)
    k.inc(g.memset(ones1[:, :], 1.0), GG)

    spl_free = [(None, 0)] * 4
    spl_ii = [0]

    def spl_slot(prod_eng):
        i = spl_ii[0] % 4
        spl_ii[0] += 1
        kk, vv = spl_free[i]
        if kk:
            k.w(prod_eng, kk, vv)
        return i

    marks = {"ybuf": (None, 0), "xnreg": (None, 0)}

    def phase_A(p):
        sn = k.snap()
        for tt in range(8):
            row0 = p * TP + tt * 128
            wts = ([(VV, sn[VV]), (EV, sn[EV]), (MM, sn[MM])] if tt == 0
                   else [(VV, k.cnt[VV]), (EV, k.cnt[EV])])
            lc = k.ld(scr[:, 0:2048], x_in[row0:row0 + 128, :], waits=wts)
            k.w(v, LD, lc)
            vc = k.inc(v.reduce_sum(msc[:, MLN:MLN + 1], scr[:, 0:2048], AX.X), VV)
            k.w(v, VV, vc)
            vc = k.inc(v.tensor_scalar_mul(msc[:, MLN:MLN + 1],
                                           msc[:, MLN:MLN + 1], 1.0 / D), VV)
            k.w(v, VV, vc)
            vc = k.inc(v.tensor_scalar(scr[:, 2048:4096], scr[:, 0:2048],
                                       msc[:, MLN:MLN + 1], None, ALU.subtract), VV)
            k.w(s, VV, vc)
            ec = k.inc(s.activation(scr[:, 0:2048], scr[:, 2048:4096], AF.Square,
                                    accum_out=msc[:, MLN + 2:MLN + 3]), EV)
            k.w(v, EV, ec)
            vc = k.inc(v.tensor_scalar(msc[:, MLN + 2:MLN + 3], msc[:, MLN + 2:MLN + 3],
                                       1.0 / D, 1e-5, ALU.mult, ALU.add), VV)
            k.w(v, VV, vc)
            vc = k.inc(v.reciprocal(msc[:, MLN + 1:MLN + 2], msc[:, MLN + 2:MLN + 3]), VV)
            k.w(s, VV, vc)
            ec = k.inc(s.activation(msc[:, MLN + 1:MLN + 2],
                                    msc[:, MLN + 1:MLN + 2], AF.Sqrt), EV)
            k.w(v, EV, ec)
            for kk, vv in (spl_free[0], spl_free[1]):
                if kk:
                    k.w(v, kk, vv)
            vc = k.inc(v.tensor_scalar(spl[:, 0:2048], scr[:, 2048:4096],
                                       msc[:, MLN + 1:MLN + 2], None, ALU.mult), VV)
            for f in range(NF):
                b = k.ptbank()
                k.w(t, VV, vc)
                if tt == 0 and f == 0:
                    k.w(t, MM, marks["xnreg"][1] if marks["xnreg"][0] else 0)
                mc = k.inc(t.transpose(pt[b][:, 0:128],
                                       spl[:, f * 128:(f + 1) * 128], idn[:, :]), MM)
                k.w(s, MM, mc)
                if tt == 0 and f == 0:
                    k.wsnap(s, sn, keys=(MM, VV))
                ec = k.inc(s.copy(xn[:, f * TP + tt * 128: f * TP + (tt + 1) * 128],
                                  pt[b][:, 0:128]), EV)
                k.pt_free[b] = (EV, ec)
            spl_free[0] = (MM, k.cnt[MM])
            spl_free[1] = (MM, k.cnt[MM])
        if DBG:
            for f in range(NF):
                k.st(sp_xn[p * D + f * 128: p * D + (f + 1) * 128, :],
                     xn[:, f * TP:(f + 1) * TP], waits=[(EV, k.cnt[EV])])
        return k.cnt[EV]

    def proj(wname, xn_ev):
        ymark = marks["ybuf"]
        for mg in range(8):
            stg, lc, lkey = k.stage([(lambda sg: st3(sg, NF, 256),
                                      r3(w_f[wname])[:, :, mg * 256:(mg + 1) * 256])],
                                    cast=True)
            for m2 in range(2):
                mglob = mg * 2 + m2
                bA, bB = k.bank(), k.bank()
                trips = [(stg[:, kk * 256 + m2 * 128: kk * 256 + m2 * 128 + 128],
                          xn[:, kk * TP: kk * TP + CH],
                          xn[:, kk * TP + CH: kk * TP + 2 * CH])
                         for kk in range(NF)]
                mc = k.chain_pair(pb[bA][:, 0:CH], pb[bB][:, 0:CH], trips,
                                  waits=[(lkey, lc), (EV, xn_ev)])
                k.w(s, MM, mc)
                if ymark[0]:
                    k.w(s, ymark[0], ymark[1])
                ec = k.inc(s.copy(ybuf[:, mglob * TP: mglob * TP + CH],
                                  pb[bA][:, 0:CH]), EV)
                ec = k.inc(s.copy(ybuf[:, mglob * TP + CH: (mglob + 1) * TP],
                                  pb[bB][:, 0:CH]), EV)
                k.bank_free[bA] = (EV, ec)
                k.bank_free[bB] = (EV, ec)
            k.stage_consumed(MM, k.cnt[MM])
        return k.cnt[EV]

    def fft_mtile(mf, rhs_tensor, rhs_ready):
        stg, lc, lkey = k.stage([
            (lambda sg: sg[:, 0:NF * 128].rearrange("p (a c) -> p a c", c=128),
             r3(cmat)[:, :, mf * 128:(mf + 1) * 128]),
            (lambda sg: sg[:, NF * 128:2 * NF * 128].rearrange("p (a c) -> p a c", c=128),
             r3(smat)[:, :, mf * 128:(mf + 1) * 128])])
        res = []
        for plane in range(2):
            off = plane * NF * 128
            bA, bB = k.bank(), k.bank()
            trips = [(stg[:, off + kk * 128: off + (kk + 1) * 128],
                      rhs_tensor[:, kk * TP: kk * TP + CH],
                      rhs_tensor[:, kk * TP + CH: (kk + 1) * TP])
                     for kk in range(NF)]
            mc = k.chain_pair(pb[bA][:, 0:CH], pb[bB][:, 0:CH], trips,
                              waits=[(lkey, lc)] + ([rhs_ready] if rhs_ready else []))
            res.append((bA, bB, mc))
        k.stage_consumed(MM, res[-1][2])
        return res

    def phase_C(xn_ev, b2_mark):
        yev = proj("wk", xn_ev)
        for mf in range(NF):
            res = fft_mtile(mf, ybuf, (EV, yev))
            for plane in range(2):
                bA, bB, mc = res[plane]
                k.w(s, MM, mc)
                if mf == 0 and plane == 0 and b2_mark[0]:
                    k.w(s, b2_mark[0], b2_mark[1])
                c0 = plane * NF * TP + mf * TP
                ec = k.inc(s.copy(big2[:, c0:c0 + CH], pb[bA][:, 0:CH]), EV)
                ec = k.inc(s.copy(big2[:, c0 + CH:c0 + TP], pb[bB][:, 0:CH]), EV)
                k.bank_free[bA] = (EV, ec)
                k.bank_free[bB] = (EV, ec)
        marks["ybuf"] = (MM, k.cnt[MM])
        return k.cnt[EV]

    def phase_D(xn_ev, kf_ev):
        yev = proj("wv", xn_ev)
        for mf in range(NF):
            res = fft_mtile(mf, ybuf, (EV, yev))
            (bR0, bR1, mR), (bI0, bI1, mI) = res
            for ncz in range(2):
                bR = bR0 if ncz == 0 else bR1
                bI = bI0 if ncz == 0 else bI1
                c0 = mf * TP + ncz * CH
                kR = big2[:, c0:c0 + CH]
                kI = big2[:, NF * TP + c0:NF * TP + c0 + CH]
                k.w(v, MM, max(mR, mI))
                k.w(v, EV, kf_ev)
                v.tensor_mul(scr[:, 0:512], pb[bR][:, 0:CH], kR)
                v.tensor_mul(scr[:, 512:1024], pb[bI][:, 0:CH], kI)
                v.tensor_sub(scr[:, 1024:1536], scr[:, 0:512], scr[:, 512:1024])
                v.tensor_mul(scr[:, 0:512], pb[bI][:, 0:CH], kR)
                v.tensor_mul(scr[:, 512:1024], pb[bR][:, 0:CH], kI)
                v.tensor_add(kI, scr[:, 0:512], scr[:, 512:1024])
                vc = k.inc(v.tensor_copy(kR, scr[:, 1024:1536]), VV)
                k.bank_free[bR] = (VV, vc)
                k.bank_free[bI] = (VV, vc)
        marks["ybuf"] = (MM, k.cnt[MM])
        return k.cnt[VV]

    def phase_E(p, xn_ev, kv_vv):
        yev = proj("wq", xn_ev)
        for mf in range(NF):
            res = fft_mtile(mf, ybuf, (EV, yev))
            (bR0, bR1, mR), (bI0, bI1, mI) = res
            for ncz in range(2):
                bR = bR0 if ncz == 0 else bR1
                bI = bI0 if ncz == 0 else bI1
                c0 = mf * TP + ncz * CH
                kvR = big2[:, c0:c0 + CH]
                kvI = big2[:, NF * TP + c0:NF * TP + c0 + CH]
                k.w(v, MM, max(mR, mI))
                k.w(v, VV, kv_vv)
                tA = scr[:, 2048 + ncz * CH:2048 + ncz * CH + CH]
                tB = scr[:, 3072 + ncz * CH:3072 + ncz * CH + CH]
                v.tensor_mul(tA, pb[bR][:, 0:CH], kvR)
                v.tensor_mul(tB, pb[bI][:, 0:CH], kvI)
                v.tensor_add(scr[:, ncz * CH:ncz * CH + CH], tA, tB)
                v.tensor_mul(tA, pb[bR][:, 0:CH], kvI)
                v.tensor_mul(tB, pb[bI][:, 0:CH], kvR)
                vc = k.inc(v.tensor_sub(scr[:, 1024 + ncz * CH:1024 + ncz * CH + CH],
                                        tA, tB), VV)
                k.bank_free[bR] = (VV, vc)
                k.bank_free[bI] = (VV, vc)
            res2 = fft_mtile(mf, xn, (EV, xn_ev))
            totR = msc[:, MTOT + mf:MTOT + mf + 1]
            totI = msc[:, MTOT + 16 + mf:MTOT + 16 + mf + 1]
            k.w(v, VV, k.cnt[VV])
            init_r = 0.0 if p == 0 else totR
            init_i = 0.0 if p == 0 else totI
            vc = k.inc(v.tensor_tensor_scan(scr[:, 2048:3072], scr[:, 0:1024],
                                            scr[:, 0:1024], init_r,
                                            ALU.add, ALU.bypass), VV)
            vc = k.inc(v.tensor_tensor_scan(scr[:, 3072:4096], scr[:, 1024:2048],
                                            scr[:, 1024:2048], init_i,
                                            ALU.add, ALU.bypass), VV)
            k.w(v, VV, vc)
            v.tensor_copy(totR, scr[:, 3071:3072])
            vc = k.inc(v.tensor_copy(totI, scr[:, 4095:4096]), VV)
            slR = spl_slot(v)
            slI = spl_slot(v)
            (fR0, fR1, fmR), (fI0, fI1, fmI) = res2
            lastv = 0
            for ncz in range(2):
                fR = fR0 if ncz == 0 else fR1
                fI = fI0 if ncz == 0 else fI1
                k.w(v, MM, max(fmR, fmI))
                v.tensor_add(spl[:, slR * 1024 + ncz * CH: slR * 1024 + ncz * CH + CH],
                             scr[:, 2048 + ncz * CH:2048 + ncz * CH + CH],
                             pb[fR][:, 0:CH])
                lastv = k.inc(v.tensor_add(spl[:, slI * 1024 + ncz * CH:
                                               slI * 1024 + ncz * CH + CH],
                                           scr[:, 3072 + ncz * CH:3072 + ncz * CH + CH],
                                           pb[fI][:, 0:CH]), VV)
                k.bank_free[fR] = (VV, lastv)
                k.bank_free[fI] = (VV, lastv)
            r0 = p * 2 * D + mf * 128
            sc = k.st(sp_xh[r0:r0 + 128, :], spl[:, slR * 1024:slR * 1024 + 1024],
                      waits=[(VV, lastv)])
            sc = k.st(sp_xh[r0 + D:r0 + D + 128, :],
                      spl[:, slI * 1024:slI * 1024 + 1024])
            spl_free[slR] = (ST, sc)
            spl_free[slI] = (ST, sc)
        marks["ybuf"] = (MM, k.cnt[MM])
        return k.cnt[VV]

    def phase_GH(p, wname, sp, xn_ev):
        yev = proj(wname, xn_ev)
        for mf in range(NF):
            res = fft_mtile(mf, ybuf, (EV, yev))
            slR = spl_slot(s)
            slI = spl_slot(s)
            last = 0
            for plane, sl in ((0, slR), (1, slI)):
                bA, bB, mc = res[plane]
                k.w(s, MM, mc)
                last = k.inc(s.copy(spl[:, sl * 1024:sl * 1024 + CH],
                                    pb[bA][:, 0:CH]), EV)
                last = k.inc(s.copy(spl[:, sl * 1024 + CH:sl * 1024 + TP],
                                    pb[bB][:, 0:CH]), EV)
                k.bank_free[bA] = (EV, last)
                k.bank_free[bB] = (EV, last)
            r0 = p * 2 * D + mf * 128
            sc = k.st(sp[r0:r0 + 128, :], spl[:, slR * 1024:slR * 1024 + 1024],
                      waits=[(EV, last)])
            sc = k.st(sp[r0 + D:r0 + D + 128, :], spl[:, slI * 1024:slI * 1024 + 1024])
            spl_free[slR] = (ST, sc)
            spl_free[slI] = (ST, sc)
        marks["ybuf"] = (MM, k.cnt[MM])

    def phase_B(p, xn_ev, e_vv):
        bA, bB = k.bank(), k.bank()
        trips = [(wrtb[:, kk * 8:(kk + 1) * 8],
                  xn[:, kk * TP: kk * TP + CH],
                  xn[:, kk * TP + CH: (kk + 1) * TP]) for kk in range(NF)]
        mc = k.chain_pair(pb[bA][0:8, 0:CH], pb[bB][0:8, 0:CH], trips,
                          waits=[(EV, xn_ev), (VV, 1)])
        k.w(s, MM, mc)
        ec = k.inc(s.copy(lgf[0:8, 0:CH], pb[bA][0:8, 0:CH]), EV)
        ec = k.inc(s.copy(lgf[0:8, CH:TP], pb[bB][0:8, 0:CH]), EV)
        k.bank_free[bA] = (EV, ec)
        k.bank_free[bB] = (EV, ec)
        bA, bB = k.bank(), k.bank()
        trips = [(wsgtb[:, kk:kk + 1],
                  xn[:, kk * TP: kk * TP + CH],
                  xn[:, kk * TP + CH: (kk + 1) * TP]) for kk in range(NF)]
        mc = k.chain_pair(pb[bA][0:1, 0:CH], pb[bB][0:1, 0:CH], trips)
        k.w(s, MM, mc)
        ec = k.inc(s.activation(sgf[0:1, 0:CH], pb[bA][0:1, 0:CH], AF.Sigmoid), EV)
        ec = k.inc(s.activation(sgf[0:1, CH:TP], pb[bB][0:1, 0:CH], AF.Sigmoid), EV)
        k.bank_free[bA] = (EV, ec)
        k.bank_free[bB] = (EV, ec)
        lg_ev = k.cnt[EV]
        prev_ce_mm = 0
        for tt in range(8):
            b = k.bank()
            k.w(t, EV, lg_ev)
            mc = k.inc(t.transpose(pb[b][:, 0:8],
                                   lgf[0:8, tt * 128:(tt + 1) * 128],
                                   idnf[0:8, 0:8]), MM)
            k.w(s, MM, mc)
            ec = k.inc(s.copy(scr[:, 0:8], pb[b][:, 0:8]), EV)
            k.bank_free[b] = (EV, ec)
            k.w(v, EV, ec)
            vc = k.inc(v.reduce_max(msc[:, MRT:MRT + 1], scr[:, 0:8], AX.X,
                                    negate=True), VV)
            k.w(s, VV, vc)
            ec = k.inc(s.activation(scr[:, 8:16], scr[:, 0:8], AF.Exp,
                                    bias=msc[:, MRT:MRT + 1],
                                    accum_out=msc[:, MRT + 1:MRT + 2]), EV)
            k.w(v, EV, ec)
            vc = k.inc(v.reciprocal(msc[:, MRT + 2:MRT + 3],
                                    msc[:, MRT + 1:MRT + 2]), VV)
            k.w(v, VV, vc)
            vc = k.inc(v.tensor_scalar_mul(scr[:, 8:16], scr[:, 8:16],
                                           msc[:, MRT + 2:MRT + 3]), VV)
            k.w(v, VV, vc)
            vc = k.inc(v.reduce_max(msc[:, MRT + 3:MRT + 4], scr[:, 8:16], AX.X), VV)
            k.w(v, VV, vc)
            k.w(v, MM, prev_ce_mm)
            vc = k.inc(v.tensor_scalar(scr[:, 16:24], scr[:, 8:16],
                                       msc[:, MRT + 3:MRT + 4], None, ALU.is_lt), VV)
            k.w(v, VV, vc)
            vc = k.inc(v.tensor_mul(scr[:, 16:24], scr[:, 16:24], scr[:, 8:16]), VV)
            k.w(v, VV, vc)
            vc = k.inc(v.reduce_max(msc[:, MRT + 4:MRT + 5], scr[:, 16:24], AX.X), VV)
            k.w(v, VV, vc)
            vc = k.inc(v.tensor_scalar(scr[:, 16:24], scr[:, 8:16],
                                       msc[:, MRT + 4:MRT + 5], None, ALU.is_ge), VV)
            k.w(v, VV, vc)
            vc = k.inc(v.tensor_mul(scr[:, 16:24], scr[:, 16:24], scr[:, 8:16]), VV)
            b = k.bank()
            k.w(t, VV, vc)
            mc = k.inc(t.transpose(pb[b][0:8, 0:128], scr[:, 16:24], idnf[:, :]), MM)
            prev_ce_mm = mc
            k.w(s, MM, mc)
            ec = k.inc(s.copy(cebf[0:8, tt * 128:(tt + 1) * 128],
                              pb[b][0:8, 0:128]), EV)
            k.bank_free[b] = (EV, ec)
        ce_ev = k.cnt[EV]
        k.w(v, EV, lg_ev)
        vc = k.inc(v.tensor_copy(sgb[0:1, :], sgf[0:1, :]), VV)
        bA, bB = k.bank(), k.bank()
        k.w(t, GG, 1)
        mc = k.chain_pair(pb[bA][:, 0:CH], pb[bB][:, 0:CH],
                          [(ones1[0:1, 0:128], sgb[0:1, 0:CH], sgb[0:1, CH:TP])],
                          waits=[(VV, vc)])
        k.w(s, MM, mc)
        ec = k.inc(s.copy(sgbc[:, 0:CH], pb[bA][:, 0:CH]), EV)
        ec = k.inc(s.copy(sgbc[:, CH:TP], pb[bB][:, 0:CH]), EV)
        k.bank_free[bA] = (EV, ec)
        k.bank_free[bB] = (EV, ec)
        sg_ev = k.cnt[EV]
        if DBG:
            k.st(dbg_ce[:, :], cebf[0:8, :], waits=[(EV, max(ce_ev, sg_ev))])
            k.st(dbg_sg[:, :], sgbc[:, :])
            k.st(dbg_lg[:, :], lgf[0:8, :])

        gate_st = [0] * 44
        for jt in range(44):
            stg, lc, lkey = k.stage([(lambda sg: st3(sg, NF, 128),
                                      r3(w_f["wsgu"])[:, :, jt * 128:(jt + 1) * 128])],
                                    cast=True)
            sl = spl_slot(s)
            bA, bB = k.bank(), k.bank()
            trips = [(stg[:, kk * 128:(kk + 1) * 128],
                      xn[:, kk * TP: kk * TP + CH],
                      xn[:, kk * TP + CH: (kk + 1) * TP]) for kk in range(NF)]
            mc = k.chain_pair(pb[bA][:, 0:CH], pb[bB][:, 0:CH], trips,
                              waits=[(lkey, lc)])
            k.w(s, MM, mc)
            last = k.inc(s.activation(spl[:, sl * 1024:sl * 1024 + CH],
                                      pb[bA][:, 0:CH], AF.Silu), EV)
            last = k.inc(s.activation(spl[:, sl * 1024 + CH:sl * 1024 + TP],
                                      pb[bB][:, 0:CH], AF.Silu), EV)
            k.bank_free[bA] = (EV, last)
            k.bank_free[bB] = (EV, last)
            k.stage_consumed(MM, k.cnt[MM])
            sc = k.st(sp_hg[jt * 128:(jt + 1) * 128, :],
                      spl[:, sl * 1024:sl * 1024 + 1024], waits=[(EV, last)])
            gate_st[jt] = sc
            spl_free[sl] = (ST, sc)
        ymark = marks["ybuf"]
        yb_free = [ymark if ymark[0] else (None, 0)] * 4
        for jt in range(44):
            stg, lc, lkey = k.stage([(lambda sg: st3(sg, NF, 128),
                                      r3(w_f["wsgu"])[:, :, (44 + jt) * 128:(45 + jt) * 128])],
                                    cast=True)
            ysl = jt % 4
            glc = k.ld(ybuf[:, ysl * 1024:(ysl + 1) * 1024],
                       sp_hg[jt * 128:(jt + 1) * 128, :],
                       waits=[(ST, gate_st[jt])] +
                             ([yb_free[ysl]] if yb_free[ysl][0] else []))
            sl = spl_slot(v)
            bA, bB = k.bank(), k.bank()
            trips = [(stg[:, kk * 128:(kk + 1) * 128],
                      xn[:, kk * TP: kk * TP + CH],
                      xn[:, kk * TP + CH: (kk + 1) * TP]) for kk in range(NF)]
            mc = k.chain_pair(pb[bA][:, 0:CH], pb[bB][:, 0:CH], trips,
                              waits=[(lkey, lc)])
            k.w(v, MM, mc)
            k.w(v, LD, glc)
            k.w(v, EV, sg_ev)
            lastv = 0
            for ncz, bx in ((0, bA), (1, bB)):
                v.tensor_mul(scr[:, 512:1024], pb[bx][:, 0:CH],
                             ybuf[:, ysl * 1024 + ncz * CH: ysl * 1024 + ncz * CH + CH])
                lastv = k.inc(v.tensor_mul(spl[:, sl * 1024 + ncz * CH:
                                               sl * 1024 + ncz * CH + CH],
                                           scr[:, 512:1024],
                                           sgbc[:, ncz * CH:ncz * CH + CH]), VV)
                k.bank_free[bx] = (VV, lastv)
            k.stage_consumed(MM, k.cnt[MM])
            yb_free[ysl] = (VV, lastv)
            sc = k.st(sp_hsh[jt * 128:(jt + 1) * 128, :],
                      spl[:, sl * 1024:sl * 1024 + 1024], waits=[(VV, lastv)])
            spl_free[sl] = (ST, sc)
        hsh_st = k.cnt[ST]
        yb_last = (VV, k.cnt[VV])

        rW = r3(w_f["wsdn"])
        rH = sp_hsh.ap().rearrange("(a p) c -> p a c", p=128)
        yb_half_free = [yb_last, yb_last]
        for mg2 in range(8):
            banks = [k.bank(), k.bank(), k.bank(), k.bank()]
            runs = [(0, 8), (8, 16), (16, 24), (24, 32), (32, 40), (40, 44)]
            for ri, (ka, kb_) in enumerate(runs):
                nk = kb_ - ka
                stg, lc, lkey = k.stage([(lambda sg, nk=nk: st3(sg, nk, 256),
                                          rW[:, ka:kb_, mg2 * 256:(mg2 + 1) * 256])],
                                        cast=True)
                yh = ri % 2
                ylc = k.ld(ybuf[:, yh * 8192:yh * 8192 + nk * TP]
                           .rearrange("p (a c) -> p a c", c=TP),
                           rH[:, ka:kb_, :],
                           waits=[(ST, hsh_st)] +
                                 ([yb_half_free[yh]] if yb_half_free[yh][0] else []))
                for m2 in range(2):
                    bA = banks[m2 * 2]
                    bB = banks[m2 * 2 + 1]
                    trips = [(stg[:, kk * 256 + m2 * 128: kk * 256 + m2 * 128 + 128],
                              ybuf[:, yh * 8192 + kk * TP: yh * 8192 + kk * TP + CH],
                              ybuf[:, yh * 8192 + kk * TP + CH:
                                   yh * 8192 + (kk + 1) * TP])
                             for kk in range(nk)]
                    k.chain_pair(pb[bA][:, 0:CH], pb[bB][:, 0:CH], trips,
                                 waits=[(lkey, lc), (LD, ylc)],
                                 start=(ri == 0), stop=(ri == len(runs) - 1))
                yb_half_free[yh] = (MM, k.cnt[MM])
                k.stage_consumed(MM, k.cnt[MM])
            for m2 in range(2):
                for ncz in range(2):
                    bi = banks[m2 * 2 + ncz]
                    mglob = mg2 * 2 + m2
                    k.w(s, MM, k.cnt[MM])
                    k.w(s, VV, e_vv)
                    ec = k.inc(s.copy(bigF[:, mglob * TP + ncz * CH:
                                           mglob * TP + ncz * CH + CH],
                                      pb[bi][:, 0:CH]), EV)
                    k.bank_free[bi] = (EV, ec)

        for e in range(8):
            ex_mm = k.cnt[MM]
            bA, bB = k.bank(), k.bank()
            mc = k.chain_pair(pb[bA][:, 0:CH], pb[bB][:, 0:CH],
                              [(sel8[:, e * 128:(e + 1) * 128],
                                cebf[0:8, 0:CH], cebf[0:8, CH:TP])],
                              waits=[(EV, ce_ev), (LD, 32)])
            k.w(s, MM, mc)
            ec = k.inc(s.copy(cebc[:, 0:CH], pb[bA][:, 0:CH]), EV)
            ec = k.inc(s.copy(cebc[:, CH:TP], pb[bB][:, 0:CH]), EV)
            k.bank_free[bA] = (EV, ec)
            k.bank_free[bB] = (EV, ec)
            ce_bc_ev = k.cnt[EV]
            wge = w_f["wegu"].ap()[e * D:(e + 1) * D, :].rearrange(
                "(a p) c -> p a c", p=128)
            for jt in range(11):
                stg, lc, lkey = k.stage([(lambda sg: st3(sg, NF, 128),
                                          wge[:, :, jt * 128:(jt + 1) * 128])],
                                        cast=True)
                bA, bB = k.bank(), k.bank()
                trips = [(stg[:, kk * 128:(kk + 1) * 128],
                          xn[:, kk * TP: kk * TP + CH],
                          xn[:, kk * TP + CH: (kk + 1) * TP]) for kk in range(NF)]
                mc = k.chain_pair(pb[bA][:, 0:CH], pb[bB][:, 0:CH], trips,
                                  waits=[(lkey, lc)])
                k.w(s, MM, max(mc, ex_mm))
                ec = k.inc(s.activation(ybuf[:, jt * TP: jt * TP + CH],
                                        pb[bA][:, 0:CH], AF.Silu), EV)
                ec = k.inc(s.activation(ybuf[:, jt * TP + CH: (jt + 1) * TP],
                                        pb[bB][:, 0:CH], AF.Silu), EV)
                k.bank_free[bA] = (EV, ec)
                k.bank_free[bB] = (EV, ec)
                k.stage_consumed(MM, k.cnt[MM])
            for jt in range(11):
                stg, lc, lkey = k.stage([(lambda sg: st3(sg, NF, 128),
                                          wge[:, :, (11 + jt) * 128:(12 + jt) * 128])],
                                        cast=True)
                bA, bB = k.bank(), k.bank()
                trips = [(stg[:, kk * 128:(kk + 1) * 128],
                          xn[:, kk * TP: kk * TP + CH],
                          xn[:, kk * TP + CH: (kk + 1) * TP]) for kk in range(NF)]
                mc = k.chain_pair(pb[bA][:, 0:CH], pb[bB][:, 0:CH], trips,
                                  waits=[(lkey, lc)])
                k.w(v, MM, mc)
                k.w(v, EV, max(ce_bc_ev, k.cnt[EV]))
                lastv = 0
                for ncz, bx in ((0, bA), (1, bB)):
                    v.tensor_mul(scr[:, 512:1024], pb[bx][:, 0:CH],
                                 ybuf[:, jt * TP + ncz * CH: jt * TP + ncz * CH + CH])
                    lastv = k.inc(v.tensor_mul(ybuf[:, jt * TP + ncz * CH:
                                                    jt * TP + ncz * CH + CH],
                                               scr[:, 512:1024],
                                               cebc[:, ncz * CH:ncz * CH + CH]), VV)
                    k.bank_free[bx] = (VV, lastv)
                k.stage_consumed(MM, k.cnt[MM])
            he_vv = k.cnt[VV]
            wde = w_f["wedn"].ap()[e * 1408:(e + 1) * 1408, :].rearrange(
                "(a p) c -> p a c", p=128)
            for mg2 in range(8):
                stg, lc, lkey = k.stage([(lambda sg: st3(sg, 11, 256),
                                          wde[:, :, mg2 * 256:(mg2 + 1) * 256])],
                                        cast=True)
                for m2 in range(2):
                    mglob = mg2 * 2 + m2
                    bA, bB = k.bank(), k.bank()
                    trips = [(stg[:, kk * 256 + m2 * 128: kk * 256 + m2 * 128 + 128],
                              ybuf[:, kk * TP: kk * TP + CH],
                              ybuf[:, kk * TP + CH: (kk + 1) * TP])
                             for kk in range(11)]
                    mc = k.chain_pair(pb[bA][:, 0:CH], pb[bB][:, 0:CH], trips,
                                      waits=[(lkey, lc), (VV, he_vv)])
                    k.w(v, MM, mc)
                    for ncz, bx in ((0, bA), (1, bB)):
                        dst = bigF[:, mglob * TP + ncz * CH: mglob * TP + ncz * CH + CH]
                        vc = k.inc(v.tensor_add(dst, dst, pb[bx][:, 0:CH]), VV)
                        k.bank_free[bx] = (VV, vc)
                k.stage_consumed(MM, k.cnt[MM])
        f_vv = k.cnt[VV]
        if DBG:
            for mf in range(NF):
                k.st(dbg_F[p * D + mf * 128: p * D + (mf + 1) * 128, :],
                     bigF[:, mf * TP:(mf + 1) * TP], waits=[(VV, f_vv)])
        for mf in range(NF):
            sl = spl_slot(s)
            k.w(s, VV, f_vv)
            ec = k.inc(s.activation(spl[:, sl * 1024:sl * 1024 + 1024],
                                    bigF[:, mf * TP:(mf + 1) * TP], AF.Silu), EV)
            sc = k.st(sp_g[p * D + mf * 128: p * D + (mf + 1) * 128, :],
                      spl[:, sl * 1024:sl * 1024 + 1024], waits=[(EV, ec)])
            spl_free[sl] = (ST, sc)
        marks["ybuf"] = (MM, k.cnt[MM])
        return (EV, k.cnt[EV])

    def phase_CC():
        sc = k.st(cc_in[:, :], msc[:, MTOT:MTOT + 32], waits=[(VV, k.cnt[VV])])
        k.w(g, ST, sc)
        cc = k.inc(g.collective_compute(
            "AllGather", ALU.bypass,
            replica_groups=[[0, 1], [2, 3], [4, 5], [6, 7]],
            ins=[cc_in.ap()], outs=[cc_out.ap()]), CC, 1)
        k.w(sy, CC, cc)
        lcc = k.ld(msc[:, MCAR:MCAR + 32], cc_out[0:128, :])
        k.w(v, LD, lcc)
        vc = k.inc(v.tensor_scalar_mul(msc[:, MCAR:MCAR + 32],
                                       msc[:, MCAR:MCAR + 32],
                                       msc[:, MMK:MMK + 1]), VV)
        return vc

    def phase_J(p, car_vv, big2_mark):
        sn = k.snap()
        yb_free = [(None, 0)] * 8
        for mf in range(NF):
            r0 = p * 2 * D + mf * 128
            sls = [(mf % 2) * 4 + i for i in range(4)]
            lc = 0
            for i, (sp, pl) in enumerate([(sp_xh, 0), (sp_xh, 1),
                                          (sp_sf, 0), (sp_sf, 1)]):
                ysl = sls[i]
                wts = [(ST, sn[ST])]
                if yb_free[ysl][0]:
                    wts.append(yb_free[ysl])
                elif mf < 2:
                    wts += [(MM, sn[MM]), (VV, sn[VV])]
                lc = k.ld(ybuf[:, ysl * 1024:(ysl + 1) * 1024],
                          sp[r0 + pl * D:r0 + pl * D + 128, :], waits=wts)
            xhR = ybuf[:, sls[0] * 1024:sls[0] * 1024 + 1024]
            xhI = ybuf[:, sls[1] * 1024:sls[1] * 1024 + 1024]
            sR = ybuf[:, sls[2] * 1024:sls[2] * 1024 + 1024]
            sI = ybuf[:, sls[3] * 1024:sls[3] * 1024 + 1024]
            carR = msc[:, MCAR + mf:MCAR + mf + 1]
            carI = msc[:, MCAR + 16 + mf:MCAR + 16 + mf + 1]
            k.w(v, LD, lc)
            k.w(v, VV, car_vv)
            if big2_mark[0]:
                k.w(v, big2_mark[0], big2_mark[1])
            x1 = scr[:, 0:1024]
            x2 = scr[:, 1024:2048]
            t1 = scr[:, 2048:3072]
            t2 = scr[:, 3072:4096]
            v.tensor_scalar(x1, xhR, carR, None, ALU.add)
            v.tensor_scalar(x2, xhI, carI, None, ALU.add)
            taR = big2[:, mf * TP:(mf + 1) * TP]
            taI = big2[:, NF * TP + mf * TP:NF * TP + (mf + 1) * TP]
            v.tensor_mul(t1, sR, x1)
            v.tensor_mul(t2, sI, x2)
            v.tensor_add(taR, t1, t2)
            v.tensor_mul(t1, sI, x1)
            v.tensor_mul(t2, sR, x2)
            v.tensor_sub(taI, t1, t2)
            v.tensor_mul(t1, sR, sR)
            v.tensor_mul(t2, sI, sI)
            v.tensor_add(t1, t1, t2)
            v.tensor_scalar(t1, t1, -1.0 / D, 1.0 / D, ALU.mult, ALU.add)
            slR = spl_slot(v)
            slI = spl_slot(v)
            v.tensor_mul(spl[:, slR * 1024:slR * 1024 + 1024], t1, x1)
            vc = k.inc(v.tensor_mul(spl[:, slI * 1024:slI * 1024 + 1024], t1, x2), VV)
            for ysl in sls:
                yb_free[ysl] = (VV, vc)
            sc = k.st(sp_z0[r0:r0 + 128, :], spl[:, slR * 1024:slR * 1024 + 1024],
                      waits=[(VV, vc)])
            sc = k.st(sp_z0[r0 + D:r0 + D + 128, :],
                      spl[:, slI * 1024:slI * 1024 + 1024])
            spl_free[slR] = (ST, sc)
            spl_free[slI] = (ST, sc)
            if DBG:
                k.st(sp_ta[r0:r0 + 128, :], taR, waits=[(VV, vc)])
                k.st(sp_ta[r0 + D:r0 + D + 128, :], taI)
        return (VV, k.cnt[VV])

    def phase_M1(p, ta_vv):
        h1_st = [0] * 64
        for jg in range(64):
            stg, lc, lkey = k.stage([(lambda sg: st3(sg, 32, 128),
                                      r3(w_f["w1"])[:, :, jg * 128:(jg + 1) * 128])],
                                    cast=True)
            sl = spl_slot(s)
            bA, bB = k.bank(), k.bank()
            trips = [(stg[:, kk * 128:(kk + 1) * 128],
                      big2[:, kk * TP: kk * TP + CH],
                      big2[:, kk * TP + CH: (kk + 1) * TP]) for kk in range(32)]
            mc = k.chain_pair(pb[bA][:, 0:CH], pb[bB][:, 0:CH], trips,
                              waits=[(lkey, lc), ta_vv])
            k.w(s, MM, mc)
            last = k.inc(s.activation(spl[:, sl * 1024:sl * 1024 + CH],
                                      pb[bA][:, 0:CH], AF.Relu,
                                      bias=msc[:, MB1 + jg:MB1 + jg + 1]), EV)
            last = k.inc(s.activation(spl[:, sl * 1024 + CH:sl * 1024 + TP],
                                      pb[bB][:, 0:CH], AF.Relu,
                                      bias=msc[:, MB1 + jg:MB1 + jg + 1]), EV)
            k.bank_free[bA] = (EV, last)
            k.bank_free[bB] = (EV, last)
            k.stage_consumed(MM, k.cnt[MM])
            sc = k.st(sp_h1[jg * 128:(jg + 1) * 128, :],
                      spl[:, sl * 1024:sl * 1024 + 1024], waits=[(EV, last)])
            h1_st[jg] = sc
            spl_free[sl] = (ST, sc)
        return k.cnt[MM], h1_st

    def phase_M2(p, h1_st):
        rW2 = r3(w_f["w2"])
        rH1 = sp_h1.ap().rearrange("(a p) c -> p a c", p=128)
        xn_half_free = [marks["xnreg"], marks["xnreg"]]
        for og in range(16):
            banks = [k.bank(), k.bank(), k.bank(), k.bank()]
            for ri in range(8):
                ka, kb_ = ri * 8, (ri + 1) * 8
                stg, lc, lkey = k.stage([(lambda sg: st3(sg, 8, 256),
                                          rW2[:, ka:kb_, og * 256:(og + 1) * 256])],
                                        cast=True)
                xh_ = ri % 2
                hlc = k.ld(xn[:, xh_ * 8192:xh_ * 8192 + 8 * TP]
                           .rearrange("p (a c) -> p a c", c=TP),
                           rH1[:, ka:kb_, :],
                           waits=[(ST, h1_st[ri * 8 + 7])] +
                                 ([xn_half_free[xh_]] if xn_half_free[xh_][0] else []))
                for m2 in range(2):
                    bA = banks[m2 * 2]
                    bB = banks[m2 * 2 + 1]
                    trips = [(stg[:, kk * 256 + m2 * 128: kk * 256 + m2 * 128 + 128],
                              xn[:, xh_ * 8192 + kk * TP: xh_ * 8192 + kk * TP + CH],
                              xn[:, xh_ * 8192 + kk * TP + CH:
                                 xh_ * 8192 + (kk + 1) * TP])
                             for kk in range(8)]
                    k.chain_pair(pb[bA][:, 0:CH], pb[bB][:, 0:CH], trips,
                                 waits=[(lkey, lc), (LD, hlc)],
                                 start=(ri == 0), stop=(ri == 7))
                xn_half_free[xh_] = (MM, k.cnt[MM])
                k.stage_consumed(MM, k.cnt[MM])
            for m2 in range(2):
                mglob = og * 2 + m2
                ssl = spl_slot(s)
                last = 0
                for ncz in range(2):
                    bi = banks[m2 * 2 + ncz]
                    k.w(s, MM, k.cnt[MM])
                    last = k.inc(s.activation(spl[:, ssl * 1024 + ncz * CH:
                                                  ssl * 1024 + ncz * CH + CH],
                                              pb[bi][:, 0:CH], AF.Identity,
                                              bias=msc[:, MB2 + mglob:
                                                       MB2 + mglob + 1]), EV)
                    k.bank_free[bi] = (EV, last)
                sc = k.st(sp_h[mglob * 128:(mglob + 1) * 128, :],
                          spl[:, ssl * 1024:ssl * 1024 + 1024], waits=[(EV, last)])
                spl_free[ssl] = (ST, sc)
        marks["xnreg"] = (MM, k.cnt[MM])

    def phase_L(p, ta_done_mm):
        sn = k.snap()
        yb_free = [(None, 0)] * 12
        for mf in range(NF):
            r0 = p * 2 * D + mf * 128
            sls = [(mf % 2) * 6 + i for i in range(6)]
            srcs = [(sp_h, mf * 128), (sp_h, D + mf * 128),
                    (sp_tf, r0), (sp_tf, r0 + D),
                    (sp_z0, r0), (sp_z0, r0 + D)]
            lc = 0
            for i, (sp, rr) in enumerate(srcs):
                ysl = sls[i]
                wts = [(ST, sn[ST])]
                if yb_free[ysl][0]:
                    wts.append(yb_free[ysl])
                elif mf < 2:
                    wts += [(MM, sn[MM]), (VV, sn[VV])]
                lc = k.ld(ybuf[:, ysl * 1024:(ysl + 1) * 1024],
                          sp[rr:rr + 128, :], waits=wts)
            h1c = ybuf[:, sls[0] * 1024:sls[0] * 1024 + 1024]
            h2c = ybuf[:, sls[1] * 1024:sls[1] * 1024 + 1024]
            tR = ybuf[:, sls[2] * 1024:sls[2] * 1024 + 1024]
            tI = ybuf[:, sls[3] * 1024:sls[3] * 1024 + 1024]
            z0R = ybuf[:, sls[4] * 1024:sls[4] * 1024 + 1024]
            z0I = ybuf[:, sls[5] * 1024:sls[5] * 1024 + 1024]
            k.w(v, LD, lc)
            k.w(v, MM, ta_done_mm)
            t1 = scr[:, 0:1024]
            t2 = scr[:, 1024:2048]
            zR = big2[:, mf * TP:(mf + 1) * TP]
            zP = big2[:, NF * TP + mf * TP:NF * TP + (mf + 1) * TP]
            v.tensor_mul(t1, tR, h1c)
            v.tensor_mul(t2, tI, h2c)
            v.tensor_add(t1, t1, t2)
            v.scalar_tensor_tensor(zR, t1, 1.0 / D, z0R, ALU.mult, ALU.add)
            v.tensor_mul(t1, tI, h1c)
            v.tensor_mul(t2, tR, h2c)
            v.tensor_sub(t1, t1, t2)
            vc = k.inc(v.scalar_tensor_tensor(zP, t1, 1.0 / D, z0I,
                                              ALU.mult, ALU.add), VV)
            for ysl in sls:
                yb_free[ysl] = (VV, vc)
        return (VV, k.cnt[VV])

    def phase_M(p, z_vv):
        sn = k.snap()
        g_free = [(None, 0)] * 2
        for mf in range(NF):
            stg, lc, lkey = k.stage([
                (lambda sg_: sg_[:, 0:NF * 128].rearrange("p (a c) -> p a c", c=128),
                 r3(cmat)[:, :, mf * 128:(mf + 1) * 128]),
                (lambda sg_: sg_[:, NF * 128:2 * NF * 128].rearrange(
                    "p (a c) -> p a c", c=128),
                 r3(smat)[:, :, mf * 128:(mf + 1) * 128])])
            gsl = mf % 2
            wts = [(ST, sn[ST])]
            if g_free[gsl][0]:
                wts.append(g_free[gsl])
            else:
                wts.append((MM, marks["xnreg"][1]))
            glc = k.ld(xn[:, gsl * 1024:(gsl + 1) * 1024],
                       sp_g[p * D + mf * 128: p * D + (mf + 1) * 128, :], waits=wts)
            bA, bB = k.bank(), k.bank()
            trips = ([(stg[:, kk * 128:(kk + 1) * 128],
                       big2[:, kk * TP: kk * TP + CH],
                       big2[:, kk * TP + CH: (kk + 1) * TP]) for kk in range(NF)] +
                     [(stg[:, NF * 128 + kk * 128: NF * 128 + (kk + 1) * 128],
                       big2[:, NF * TP + kk * TP: NF * TP + kk * TP + CH],
                       big2[:, NF * TP + kk * TP + CH: NF * TP + (kk + 1) * TP])
                      for kk in range(NF)])
            mc = k.chain_pair(pb[bA][:, 0:CH], pb[bB][:, 0:CH], trips,
                              waits=[(lkey, lc), z_vv])
            k.w(v, MM, mc)
            k.w(v, LD, glc)
            if mf == 0:
                k.wsnap(v, sn, keys=(MM,))
            lastv = 0
            for ncz, bx in ((0, bA), (1, bB)):
                lastv = k.inc(v.tensor_mul(ybuf[:, mf * TP + ncz * CH:
                                                mf * TP + ncz * CH + CH],
                                           pb[bx][:, 0:CH],
                                           xn[:, gsl * 1024 + ncz * CH:
                                              gsl * 1024 + ncz * CH + CH]), VV)
                k.bank_free[bx] = (VV, lastv)
            g_free[gsl] = (VV, lastv)
            k.stage_consumed(MM, k.cnt[MM])
        sg_vv = k.cnt[VV]
        x_free = (None, 0)
        prev_store = 0
        for tt in range(8):
            row0 = p * TP + tt * 128
            wts = [(VV, sn[VV]), (EV, sn[EV])] if tt == 0 else []
            if x_free[0]:
                wts.append(x_free)
            xlc = k.ld(scr[:, 2048:4096], x_in[row0:row0 + 128, :], waits=wts)
            lastv = 0
            for mf in range(NF):
                b = k.ptbank()
                k.w(t, VV, sg_vv)
                mc = k.inc(t.transpose(pt[b][:, 0:128],
                                       ybuf[:, mf * TP + tt * 128:
                                            mf * TP + (tt + 1) * 128],
                                       idn[:, :]), MM)
                k.w(v, MM, mc)
                k.w(v, LD, xlc)
                if mf == 0:
                    k.w(v, ST, prev_store)
                lastv = k.inc(v.tensor_add(scr[:, mf * 128:(mf + 1) * 128],
                                           scr[:, 2048 + mf * 128:
                                               2048 + (mf + 1) * 128],
                                           pt[b][:, 0:128]), VV)
                k.pt_free[b] = (VV, lastv)
            sc = k.st(out[row0:row0 + 128, :], scr[:, 0:2048], waits=[(VV, lastv)])
            prev_store = sc
            x_free = (VV, lastv)

    b2_mark = (None, 0)
    for p in range(NP):
        xn_ev = phase_A(p)
        kf_ev = phase_C(xn_ev, b2_mark)
        kv_vv = phase_D(xn_ev, kf_ev)
        e_vv = phase_E(p, xn_ev, kv_vv)
        phase_GH(p, "ws", sp_sf, xn_ev)
        phase_GH(p, "wt", sp_tf, xn_ev)
        b2_mark = phase_B(p, xn_ev, e_vv)
    car_vv = phase_CC()
    marks["xnreg"] = (MM, k.cnt[MM])
    l_done_mm = 0
    for p in range(NP):
        big2_mark = b2_mark if p == 0 else (MM, l_done_mm)
        ta_vv = phase_J(p, car_vv, big2_mark)
        m1_mm, h1_st = phase_M1(p, ta_vv)
        phase_M2(p, h1_st)
        zvv = phase_L(p, m1_mm)
        phase_M(p, zvv)
        l_done_mm = k.cnt[MM]
    sy.wait_ge(k.sem[ST], k.cnt[ST])
    nc._kb_stack = stack
    return nc


_HOST = {}


def _host_constants():
    if not _HOST:
        n = np.arange(D, dtype=np.float64)
        th = 2.0 * np.pi * np.outer(n, n) / D
        _HOST["cmat"] = np.cos(th).astype(ml_dtypes.bfloat16)
        _HOST["smat"] = np.sin(th).astype(ml_dtypes.bfloat16)
        _HOST["idn"] = np.eye(128, dtype=ml_dtypes.bfloat16)
        s8 = np.zeros((8, 8 * 128), dtype=ml_dtypes.bfloat16)
        for e in range(8):
            s8[e, e * 128:(e + 1) * 128] = 1.0
        _HOST["sel8"] = s8
    return _HOST


def kernel(**inputs):
    hc = _host_constants()
    f32 = lambda a: np.ascontiguousarray(np.asarray(a, np.float32))
    x = f32(inputs["L0_attn_x"]).reshape(8 * T, D)
    shared = {
        "wq": f32(inputs["L0_attn_Wq"]), "wk": f32(inputs["L0_attn_Wk"]),
        "wv": f32(inputs["L0_attn_Wv"]), "ws": f32(inputs["L0_attn_Ws"]),
        "wt": f32(inputs["L0_attn_Wt"]),
        "w1": f32(inputs["L0_attn_W1"]), "w2": f32(inputs["L0_attn_W2"]),
        "wsgu": f32(inputs["L0_ffn_Ws_gate_up"]),
        "wsdn": f32(inputs["L0_ffn_Ws_down"]),
        "wegu": f32(inputs["L0_ffn_We_gate_up"]).reshape(8 * D, 2816),
        "wedn": f32(inputs["L0_ffn_We_down"]).reshape(8 * 1408, D),
        "wrt": np.ascontiguousarray(
            f32(inputs["L0_ffn_W_router"]).reshape(8, 16, 128)
            .transpose(2, 1, 0).reshape(128, 128)),
        "wsgt": np.ascontiguousarray(
            f32(inputs["L0_ffn_W_sgate"]).reshape(16, 128).T),
        "b1t": np.ascontiguousarray(f32(inputs["L0_attn_b1"]).reshape(64, 128).T),
        "b2t": np.ascontiguousarray(f32(inputs["L0_attn_b2"]).reshape(32, 128).T),
        "cmat": hc["cmat"], "smat": hc["smat"], "idn": hc["idn"],
        "sel8": hc["sel8"],
    }
    in_maps = []
    for c in range(8):
        m = dict(shared)
        m["x"] = np.ascontiguousarray(x[c * T:(c + 1) * T])
        m["mask"] = np.full((128, 1), float(c % 2), np.float32)
        in_maps.append(m)
    nc = build_nc()
    res = run_bass_kernel_spmd(nc, in_maps, core_ids=list(range(8)))
    out = np.concatenate([np.asarray(r["out"]) for r in res.results], axis=0)
    if DBG:
        kernel.dbg = res.results
    return out.reshape(4, 4096, D).astype(np.float32)


# version 05
# speedup vs baseline: 1.2019x; 1.0060x over previous
import numpy as np
import ml_dtypes
from contextlib import ExitStack
import concourse.bass as bass
import concourse.mybir as mybir
from concourse.bass_utils import run_bass_kernel_spmd

F32 = mybir.dt.float32
BF16 = mybir.dt.bfloat16
AF = mybir.ActivationFunctionType
ALU = mybir.AluOpType
AX = mybir.AxisListType

D = 2048
NF = 16
TP = 1024
NP = 2
T = 2048
CH = 512

DBG = False

EV, VV, MM, LD, ST, WD, GG, CC = "ev", "vv", "mm", "ld", "st", "wd", "gg", "cc"


class KB:
    def __init__(self, nc, stack):
        self.nc = nc
        self.cnt = {kk: 0 for kk in (EV, VV, MM, LD, ST, WD, GG, CC)}
        self.sem = {kk: stack.enter_context(nc.semaphore("s_" + kk))
                    for kk in (EV, VV, MM, LD, ST, WD, GG, CC)}
        self.waited = {}
        self.bank_free = [(None, 0)] * 6
        self.bidx = 0
        self.pt_free = [(None, 0)] * 2
        self.pti = 0
        self.stg_free = [(None, 0)] * 2
        self.stg_i = 0

    def w(self, eng, key, val):
        if not key or val is None or val <= 0:
            return
        kk = (eng.engine, key)
        if self.waited.get(kk, 0) >= val:
            return
        eng.wait_ge(self.sem[key], val)
        self.waited[kk] = val

    def snap(self):
        return {kk: self.cnt[kk] for kk in (EV, VV, MM, LD, ST, WD)}

    def wsnap(self, eng, sn, keys=(EV, VV, MM, ST)):
        for kk in keys:
            self.w(eng, kk, sn[kk])

    def inc(self, instr, key, n=1):
        instr.then_inc(self.sem[key], n)
        self.cnt[key] += n
        return self.cnt[key]

    def ld(self, out, in_, waits=()):
        for kk, vv in waits:
            self.w(self.nc.sync, kk, vv)
        return self.inc(self.nc.sync.dma_start(out=out, in_=in_), LD, 16)

    def st(self, out, in_, waits=()):
        for kk, vv in waits:
            self.w(self.nc.sync, kk, vv)
        return self.inc(self.nc.sync.dma_start(out=out, in_=in_), ST, 16)

    def bank(self):
        b = self.bidx % 6
        self.bidx += 1
        kk, vv = self.bank_free[b]
        self.w(self.nc.tensor, kk, vv)
        return b

    def ptbank(self):
        b = self.pti % 2
        self.pti += 1
        kk, vv = self.pt_free[b]
        self.w(self.nc.tensor, kk, vv)
        return b

    def chain(self, bank_ap, pairs, waits=(), start=True, stop=True):
        t = self.nc.tensor
        for kk, vv in waits:
            self.w(t, kk, vv)
        n = len(pairs)
        mm = None
        for i, (l, r) in enumerate(pairs):
            mm = t.matmul(bank_ap, l, r,
                          start=(start and i == 0), stop=(stop and i == n - 1))
        return self.inc(mm, MM, 1)

    def chain_pair(self, bA, bB, trips, waits=(), start=True, stop=True):
        t = self.nc.tensor
        for kk, vv in waits:
            self.w(t, kk, vv)
        n = len(trips)
        mm = None
        for i, (l, ra, rb) in enumerate(trips):
            st_, sp_ = (start and i == 0), (stop and i == n - 1)
            t.matmul(bA, l, ra, start=st_, stop=sp_)
            mm = t.matmul(bB, l, rb, start=st_, stop=sp_)
        return self.inc(mm, MM, 1)

    def stage(self, loads, extra_wait=(), cast=False):
        h = self.stg_i % 2
        self.stg_i += 1
        kk, vv = self.stg_free[h]
        eng = self.nc.gpsimd if cast else self.nc.sync
        key = WD if cast else LD
        waits = list(extra_wait) + ([(kk, vv)] if kk else [])
        stg = self.stgv[h]
        lc = 0
        for dvf, src in loads:
            for kx, vx in waits:
                self.w(eng, kx, vx)
            waits = []
            lc = self.inc(eng.dma_start(out=dvf(stg), in_=src), key, 16)
        self._last_half = h
        return stg, lc, key

    def stage_consumed(self, key, val):
        self.stg_free[self._last_half] = (key, val)


def build_nc():
    nc = bass.Bass("TRN2", target_bir_lowering=False)
    EI, EO, IN = "ExternalInput", "ExternalOutput", "Internal"
    dt = nc.dram_tensor
    x_in = dt("x", [T, D], F32, kind=EI)
    w_f = {n: dt(n, sh, F32, kind=EI) for n, sh in [
        ("wq", [D, D]), ("wk", [D, D]), ("wv", [D, D]), ("ws", [D, D]), ("wt", [D, D]),
        ("w1", [2 * D, 4 * D]), ("w2", [4 * D, 2 * D]),
        ("wsgu", [D, 11264]), ("wsdn", [5632, D]),
        ("wegu", [8 * D, 2816]), ("wedn", [8 * 1408, D])]}
    wrt = dt("wrt", [128, 128], F32, kind=EI)
    wsgt = dt("wsgt", [128, 16], F32, kind=EI)
    b1t_in = dt("b1t", [128, 64], F32, kind=EI)
    b2t_in = dt("b2t", [128, 32], F32, kind=EI)
    cmat = dt("cmat", [D, D], BF16, kind=EI)
    smat = dt("smat", [D, D], BF16, kind=EI)
    idn_in = dt("idn", [128, 128], BF16, kind=EI)
    sel8_in = dt("sel8", [8, 8 * 128], BF16, kind=EI)
    mask_in = dt("mask", [128, 1], F32, kind=EI)
    out = dt("out", [T, D], F32, kind=EO)

    spk = EO if DBG else IN
    sp_xh = dt("sp_xh", [NP * 2 * D, TP], BF16, kind=spk)
    sp_sf = dt("sp_sf", [NP * 2 * D, TP], BF16, kind=spk)
    sp_tf = dt("sp_tf", [NP * 2 * D, TP], BF16, kind=spk)
    sp_g = dt("sp_g", [NP * D, TP], BF16, kind=spk)
    sp_z0 = dt("sp_z0", [NP * 2 * D, TP], BF16, kind=spk)
    sp_h1 = dt("sp_h1", [4 * D, TP], BF16, kind=spk)
    sp_h = dt("sp_h", [2 * D, TP], BF16, kind=spk)
    sp_hg = dt("sp_hg", [5632, TP], BF16, kind=spk)
    sp_hsh = dt("sp_hsh", [5632, TP], BF16, kind=spk)
    sp_ta = dt("sp_ta", [NP * 2 * D, TP], BF16, kind=spk)
    sp_xn = dt("sp_xn", [NP * D, TP], BF16, kind=spk)
    dbg_ce = dt("dbg_ce", [8, TP], BF16, kind=spk)
    dbg_lg = dt("dbg_lg", [8, TP], F32, kind=spk)
    dbg_sg = dt("dbg_sg", [128, TP], BF16, kind=spk)
    dbg_F = dt("dbg_F", [NP * D, TP], F32, kind=spk)
    cc_in = dt("cc_in", [128, 32], F32, kind=IN)
    cc_out = dt("cc_out", [256, 32], F32, kind=IN)

    stack = ExitStack()
    sb = lambda n, sh, d: stack.enter_context(nc.sbuf_tensor(n, sh, d))
    psal = lambda n, sh, d: stack.enter_context(nc.psum_tensor(n, sh, d))
    xn = sb("xn", [128, NF * TP], BF16)
    ybuf = sb("ybuf", [128, NF * TP], BF16)
    big2 = sb("big2", [128, 2 * NF * TP], BF16)
    bigF = big2.bitcast(F32)
    stg0 = sb("stg0", [128, 4096], BF16)
    stg1 = sb("stg1", [128, 4096], BF16)
    scr = sb("scr", [128, 4096], F32)
    spl = sb("spl", [128, 4096], BF16)
    msc = sb("msc", [128, 1024], F32)
    idn = sb("idnb", [128, 128], BF16)
    idnf = sb("idnf", [128, 128], F32)
    ones1 = sb("ones1", [1, 128], BF16)
    sel8 = sb("sel8b", [8, 8 * 128], BF16)
    wrtb = sb("wrtb", [128, 128], BF16)
    wsgtb = sb("wsgtb", [128, 16], BF16)
    lgf = sb("lgf", [8, TP], F32)
    cebf = sb("cebf", [8, TP], BF16)
    sgf = sb("sgf", [1, TP], F32)
    sgb = sb("sgb", [1, TP], BF16)
    sgbc = sb("sgbc", [128, TP], BF16)
    cebc = sb("cebc", [128, TP], BF16)
    pb = [psal(f"pb{i}", [128, 512], F32) for i in range(6)]
    pt = [psal(f"pt{i}", [128, 512], BF16) for i in range(2)]

    MB1, MB2, MMK = 0, 64, 96
    MLN = 100
    MTOT = 200
    MCAR = 232
    MRT = 264

    k = KB(nc, stack)
    k.stgv = [stg0, stg1]
    sy, g, t, s, v = nc.sync, nc.gpsimd, nc.tensor, nc.scalar, nc.vector

    r3 = lambda w: w.ap().rearrange("(a p) c -> p a c", p=128)
    st3 = lambda stg, n, c: stg[:, 0:n * c].rearrange("p (a c) -> p a c", c=c)

    k.ld(idn[:, :], idn_in[:, :])
    k.ld(sel8[:, :], sel8_in[:, :])
    k.ld(msc[:, MB1:MB1 + 64], b1t_in[:, :])
    k.ld(msc[:, MB2:MB2 + 32], b2t_in[:, :])
    k.ld(msc[:, MMK:MMK + 1], mask_in[:, :])
    lc = k.ld(scr[:, 0:128], wrt[:, :])
    lc = k.ld(scr[:, 128:144], wsgt[:, :])
    k.w(v, LD, lc)
    v.tensor_copy(wrtb[:, :], scr[:, 0:128])
    v.tensor_copy(wsgtb[:, :], scr[:, 128:144])
    v.tensor_copy(idnf[:, :], idn[:, :])
    k.inc(v.memset(msc[:, MTOT:MTOT + 32], 0.0), VV)
    k.inc(g.memset(ones1[:, :], 1.0), GG)

    spl_free = [(None, 0)] * 4
    spl_ii = [0]

    def spl_slot(prod_eng):
        i = spl_ii[0] % 4
        spl_ii[0] += 1
        kk, vv = spl_free[i]
        if kk:
            k.w(prod_eng, kk, vv)
        return i

    marks = {"ybuf": (None, 0), "xnreg": (None, 0)}

    def phase_A(p):
        sn = k.snap()
        for tt in range(8):
            row0 = p * TP + tt * 128
            wts = ([(VV, sn[VV]), (EV, sn[EV]), (MM, sn[MM])] if tt == 0
                   else [(VV, k.cnt[VV]), (EV, k.cnt[EV])])
            lc = k.ld(scr[:, 0:2048], x_in[row0:row0 + 128, :], waits=wts)
            k.w(v, LD, lc)
            vc = k.inc(v.reduce_sum(msc[:, MLN:MLN + 1], scr[:, 0:2048], AX.X), VV)
            k.w(v, VV, vc)
            vc = k.inc(v.tensor_scalar_mul(msc[:, MLN:MLN + 1],
                                           msc[:, MLN:MLN + 1], 1.0 / D), VV)
            k.w(v, VV, vc)
            vc = k.inc(v.tensor_scalar(scr[:, 2048:4096], scr[:, 0:2048],
                                       msc[:, MLN:MLN + 1], None, ALU.subtract), VV)
            k.w(s, VV, vc)
            ec = k.inc(s.activation(scr[:, 0:2048], scr[:, 2048:4096], AF.Square,
                                    accum_out=msc[:, MLN + 2:MLN + 3]), EV)
            k.w(v, EV, ec)
            vc = k.inc(v.tensor_scalar(msc[:, MLN + 2:MLN + 3], msc[:, MLN + 2:MLN + 3],
                                       1.0 / D, 1e-5, ALU.mult, ALU.add), VV)
            k.w(v, VV, vc)
            vc = k.inc(v.reciprocal(msc[:, MLN + 1:MLN + 2], msc[:, MLN + 2:MLN + 3]), VV)
            k.w(s, VV, vc)
            ec = k.inc(s.activation(msc[:, MLN + 1:MLN + 2],
                                    msc[:, MLN + 1:MLN + 2], AF.Sqrt), EV)
            k.w(v, EV, ec)
            for kk, vv in (spl_free[0], spl_free[1]):
                if kk:
                    k.w(v, kk, vv)
            vc = k.inc(v.tensor_scalar(spl[:, 0:2048], scr[:, 2048:4096],
                                       msc[:, MLN + 1:MLN + 2], None, ALU.mult), VV)
            for f in range(NF):
                b = k.ptbank()
                k.w(t, VV, vc)
                if tt == 0 and f == 0:
                    k.w(t, MM, marks["xnreg"][1] if marks["xnreg"][0] else 0)
                mc = k.inc(t.transpose(pt[b][:, 0:128],
                                       spl[:, f * 128:(f + 1) * 128], idn[:, :]), MM)
                k.w(s, MM, mc)
                if tt == 0 and f == 0:
                    k.wsnap(s, sn, keys=(MM, VV))
                ec = k.inc(s.copy(xn[:, f * TP + tt * 128: f * TP + (tt + 1) * 128],
                                  pt[b][:, 0:128]), EV)
                k.pt_free[b] = (EV, ec)
            spl_free[0] = (MM, k.cnt[MM])
            spl_free[1] = (MM, k.cnt[MM])
        if DBG:
            for f in range(NF):
                k.st(sp_xn[p * D + f * 128: p * D + (f + 1) * 128, :],
                     xn[:, f * TP:(f + 1) * TP], waits=[(EV, k.cnt[EV])])
        return k.cnt[EV]

    def proj(wname, xn_ev):
        ymark = marks["ybuf"]
        for mg in range(8):
            stg, lc, lkey = k.stage([(lambda sg: st3(sg, NF, 256),
                                      r3(w_f[wname])[:, :, mg * 256:(mg + 1) * 256])],
                                    cast=True)
            for m2 in range(2):
                mglob = mg * 2 + m2
                bA, bB = k.bank(), k.bank()
                trips = [(stg[:, kk * 256 + m2 * 128: kk * 256 + m2 * 128 + 128],
                          xn[:, kk * TP: kk * TP + CH],
                          xn[:, kk * TP + CH: kk * TP + 2 * CH])
                         for kk in range(NF)]
                mc = k.chain_pair(pb[bA][:, 0:CH], pb[bB][:, 0:CH], trips,
                                  waits=[(lkey, lc), (EV, xn_ev)])
                k.w(s, MM, mc)
                if ymark[0]:
                    k.w(s, ymark[0], ymark[1])
                ec = k.inc(s.copy(ybuf[:, mglob * TP: mglob * TP + CH],
                                  pb[bA][:, 0:CH]), EV)
                ec = k.inc(s.copy(ybuf[:, mglob * TP + CH: (mglob + 1) * TP],
                                  pb[bB][:, 0:CH]), EV)
                k.bank_free[bA] = (EV, ec)
                k.bank_free[bB] = (EV, ec)
            k.stage_consumed(MM, k.cnt[MM])
        return k.cnt[EV]

    def fft_mtile(mf, rhs_tensor, rhs_ready):
        stg, lc, lkey = k.stage([
            (lambda sg: sg[:, 0:NF * 128].rearrange("p (a c) -> p a c", c=128),
             r3(cmat)[:, :, mf * 128:(mf + 1) * 128]),
            (lambda sg: sg[:, NF * 128:2 * NF * 128].rearrange("p (a c) -> p a c", c=128),
             r3(smat)[:, :, mf * 128:(mf + 1) * 128])])
        res = []
        for plane in range(2):
            off = plane * NF * 128
            bA, bB = k.bank(), k.bank()
            trips = [(stg[:, off + kk * 128: off + (kk + 1) * 128],
                      rhs_tensor[:, kk * TP: kk * TP + CH],
                      rhs_tensor[:, kk * TP + CH: (kk + 1) * TP])
                     for kk in range(NF)]
            mc = k.chain_pair(pb[bA][:, 0:CH], pb[bB][:, 0:CH], trips,
                              waits=[(lkey, lc)] + ([rhs_ready] if rhs_ready else []))
            res.append((bA, bB, mc))
        k.stage_consumed(MM, res[-1][2])
        return res

    def phase_C(xn_ev, b2_mark):
        yev = proj("wk", xn_ev)
        for mf in range(NF):
            res = fft_mtile(mf, ybuf, (EV, yev))
            for plane in range(2):
                bA, bB, mc = res[plane]
                k.w(s, MM, mc)
                if mf == 0 and plane == 0 and b2_mark[0]:
                    k.w(s, b2_mark[0], b2_mark[1])
                c0 = plane * NF * TP + mf * TP
                ec = k.inc(s.copy(big2[:, c0:c0 + CH], pb[bA][:, 0:CH]), EV)
                ec = k.inc(s.copy(big2[:, c0 + CH:c0 + TP], pb[bB][:, 0:CH]), EV)
                k.bank_free[bA] = (EV, ec)
                k.bank_free[bB] = (EV, ec)
        marks["ybuf"] = (MM, k.cnt[MM])
        return k.cnt[EV]

    def phase_D(xn_ev, kf_ev):
        yev = proj("wv", xn_ev)
        for mf in range(NF):
            res = fft_mtile(mf, ybuf, (EV, yev))
            (bR0, bR1, mR), (bI0, bI1, mI) = res
            for ncz in range(2):
                bR = bR0 if ncz == 0 else bR1
                bI = bI0 if ncz == 0 else bI1
                c0 = mf * TP + ncz * CH
                kR = big2[:, c0:c0 + CH]
                kI = big2[:, NF * TP + c0:NF * TP + c0 + CH]
                k.w(v, MM, max(mR, mI))
                k.w(v, EV, kf_ev)
                v.tensor_mul(scr[:, 0:512], pb[bR][:, 0:CH], kR)
                v.tensor_mul(scr[:, 512:1024], pb[bI][:, 0:CH], kI)
                v.tensor_sub(scr[:, 1024:1536], scr[:, 0:512], scr[:, 512:1024])
                v.tensor_mul(scr[:, 0:512], pb[bI][:, 0:CH], kR)
                v.tensor_mul(scr[:, 512:1024], pb[bR][:, 0:CH], kI)
                v.tensor_add(kI, scr[:, 0:512], scr[:, 512:1024])
                vc = k.inc(v.tensor_copy(kR, scr[:, 1024:1536]), VV)
                k.bank_free[bR] = (VV, vc)
                k.bank_free[bI] = (VV, vc)
        marks["ybuf"] = (MM, k.cnt[MM])
        return k.cnt[VV]

    def phase_E(p, xn_ev, kv_vv):
        yev = proj("wq", xn_ev)
        for mf in range(NF):
            res = fft_mtile(mf, ybuf, (EV, yev))
            (bR0, bR1, mR), (bI0, bI1, mI) = res
            for ncz in range(2):
                bR = bR0 if ncz == 0 else bR1
                bI = bI0 if ncz == 0 else bI1
                c0 = mf * TP + ncz * CH
                kvR = big2[:, c0:c0 + CH]
                kvI = big2[:, NF * TP + c0:NF * TP + c0 + CH]
                k.w(v, MM, max(mR, mI))
                k.w(v, VV, kv_vv)
                tA = scr[:, 2048 + ncz * CH:2048 + ncz * CH + CH]
                tB = scr[:, 3072 + ncz * CH:3072 + ncz * CH + CH]
                v.tensor_mul(tA, pb[bR][:, 0:CH], kvR)
                v.tensor_mul(tB, pb[bI][:, 0:CH], kvI)
                v.tensor_add(scr[:, ncz * CH:ncz * CH + CH], tA, tB)
                v.tensor_mul(tA, pb[bR][:, 0:CH], kvI)
                v.tensor_mul(tB, pb[bI][:, 0:CH], kvR)
                vc = k.inc(v.tensor_sub(scr[:, 1024 + ncz * CH:1024 + ncz * CH + CH],
                                        tA, tB), VV)
                k.bank_free[bR] = (VV, vc)
                k.bank_free[bI] = (VV, vc)
            res2 = fft_mtile(mf, xn, (EV, xn_ev))
            totR = msc[:, MTOT + mf:MTOT + mf + 1]
            totI = msc[:, MTOT + 16 + mf:MTOT + 16 + mf + 1]
            k.w(v, VV, k.cnt[VV])
            init_r = 0.0 if p == 0 else totR
            init_i = 0.0 if p == 0 else totI
            vc = k.inc(v.tensor_tensor_scan(scr[:, 2048:3072], scr[:, 0:1024],
                                            scr[:, 0:1024], init_r,
                                            ALU.add, ALU.bypass), VV)
            vc = k.inc(v.tensor_tensor_scan(scr[:, 3072:4096], scr[:, 1024:2048],
                                            scr[:, 1024:2048], init_i,
                                            ALU.add, ALU.bypass), VV)
            k.w(v, VV, vc)
            v.tensor_copy(totR, scr[:, 3071:3072])
            vc = k.inc(v.tensor_copy(totI, scr[:, 4095:4096]), VV)
            slR = spl_slot(v)
            slI = spl_slot(v)
            (fR0, fR1, fmR), (fI0, fI1, fmI) = res2
            lastv = 0
            for ncz in range(2):
                fR = fR0 if ncz == 0 else fR1
                fI = fI0 if ncz == 0 else fI1
                k.w(v, MM, max(fmR, fmI))
                v.tensor_add(spl[:, slR * 1024 + ncz * CH: slR * 1024 + ncz * CH + CH],
                             scr[:, 2048 + ncz * CH:2048 + ncz * CH + CH],
                             pb[fR][:, 0:CH])
                lastv = k.inc(v.tensor_add(spl[:, slI * 1024 + ncz * CH:
                                               slI * 1024 + ncz * CH + CH],
                                           scr[:, 3072 + ncz * CH:3072 + ncz * CH + CH],
                                           pb[fI][:, 0:CH]), VV)
                k.bank_free[fR] = (VV, lastv)
                k.bank_free[fI] = (VV, lastv)
            r0 = p * 2 * D + mf * 128
            sc = k.st(sp_xh[r0:r0 + 128, :], spl[:, slR * 1024:slR * 1024 + 1024],
                      waits=[(VV, lastv)])
            sc = k.st(sp_xh[r0 + D:r0 + D + 128, :],
                      spl[:, slI * 1024:slI * 1024 + 1024])
            spl_free[slR] = (ST, sc)
            spl_free[slI] = (ST, sc)
        marks["ybuf"] = (MM, k.cnt[MM])
        return k.cnt[VV]

    def phase_GH(p, wname, sp, xn_ev):
        yev = proj(wname, xn_ev)
        for mf in range(NF):
            res = fft_mtile(mf, ybuf, (EV, yev))
            slR = spl_slot(s)
            slI = spl_slot(s)
            last = 0
            for plane, sl in ((0, slR), (1, slI)):
                bA, bB, mc = res[plane]
                k.w(s, MM, mc)
                last = k.inc(s.copy(spl[:, sl * 1024:sl * 1024 + CH],
                                    pb[bA][:, 0:CH]), EV)
                last = k.inc(s.copy(spl[:, sl * 1024 + CH:sl * 1024 + TP],
                                    pb[bB][:, 0:CH]), EV)
                k.bank_free[bA] = (EV, last)
                k.bank_free[bB] = (EV, last)
            r0 = p * 2 * D + mf * 128
            sc = k.st(sp[r0:r0 + 128, :], spl[:, slR * 1024:slR * 1024 + 1024],
                      waits=[(EV, last)])
            sc = k.st(sp[r0 + D:r0 + D + 128, :], spl[:, slI * 1024:slI * 1024 + 1024])
            spl_free[slR] = (ST, sc)
            spl_free[slI] = (ST, sc)
        marks["ybuf"] = (MM, k.cnt[MM])

    def phase_B(p, xn_ev, e_vv):
        bA, bB = k.bank(), k.bank()
        trips = [(wrtb[:, kk * 8:(kk + 1) * 8],
                  xn[:, kk * TP: kk * TP + CH],
                  xn[:, kk * TP + CH: (kk + 1) * TP]) for kk in range(NF)]
        mc = k.chain_pair(pb[bA][0:8, 0:CH], pb[bB][0:8, 0:CH], trips,
                          waits=[(EV, xn_ev), (VV, 1)])
        k.w(s, MM, mc)
        ec = k.inc(s.copy(lgf[0:8, 0:CH], pb[bA][0:8, 0:CH]), EV)
        ec = k.inc(s.copy(lgf[0:8, CH:TP], pb[bB][0:8, 0:CH]), EV)
        k.bank_free[bA] = (EV, ec)
        k.bank_free[bB] = (EV, ec)
        bA, bB = k.bank(), k.bank()
        trips = [(wsgtb[:, kk:kk + 1],
                  xn[:, kk * TP: kk * TP + CH],
                  xn[:, kk * TP + CH: (kk + 1) * TP]) for kk in range(NF)]
        mc = k.chain_pair(pb[bA][0:1, 0:CH], pb[bB][0:1, 0:CH], trips)
        k.w(s, MM, mc)
        ec = k.inc(s.activation(sgf[0:1, 0:CH], pb[bA][0:1, 0:CH], AF.Sigmoid), EV)
        ec = k.inc(s.activation(sgf[0:1, CH:TP], pb[bB][0:1, 0:CH], AF.Sigmoid), EV)
        k.bank_free[bA] = (EV, ec)
        k.bank_free[bB] = (EV, ec)
        lg_ev = k.cnt[EV]

        rt_vc = [0] * 8

        def router_a(tt):
            c0 = tt * 32
            m0 = MRT + tt * 8
            b = k.bank()
            k.w(t, EV, lg_ev)
            mc = k.inc(t.transpose(pb[b][:, 0:8],
                                   lgf[0:8, tt * 128:(tt + 1) * 128],
                                   idnf[0:8, 0:8]), MM)
            k.w(s, MM, mc)
            ec = k.inc(s.copy(scr[:, c0:c0 + 8], pb[b][:, 0:8]), EV)
            k.bank_free[b] = (EV, ec)
            k.w(v, EV, ec)
            vc = k.inc(v.reduce_max(msc[:, m0:m0 + 1], scr[:, c0:c0 + 8], AX.X,
                                    negate=True), VV)
            k.w(s, VV, vc)
            ec = k.inc(s.activation(scr[:, c0 + 8:c0 + 16], scr[:, c0:c0 + 8],
                                    AF.Exp, bias=msc[:, m0:m0 + 1],
                                    accum_out=msc[:, m0 + 1:m0 + 2]), EV)
            k.w(v, EV, ec)
            vc = k.inc(v.reciprocal(msc[:, m0 + 2:m0 + 3],
                                    msc[:, m0 + 1:m0 + 2]), VV)
            k.w(v, VV, vc)
            vc = k.inc(v.tensor_scalar_mul(scr[:, c0 + 8:c0 + 16],
                                           scr[:, c0 + 8:c0 + 16],
                                           msc[:, m0 + 2:m0 + 3]), VV)
            k.w(v, VV, vc)
            vc = k.inc(v.reduce_max(msc[:, m0 + 3:m0 + 4],
                                    scr[:, c0 + 8:c0 + 16], AX.X), VV)
            k.w(v, VV, vc)
            vc = k.inc(v.tensor_scalar(scr[:, c0 + 16:c0 + 24],
                                       scr[:, c0 + 8:c0 + 16],
                                       msc[:, m0 + 3:m0 + 4], None, ALU.is_lt), VV)
            k.w(v, VV, vc)
            vc = k.inc(v.tensor_mul(scr[:, c0 + 16:c0 + 24],
                                    scr[:, c0 + 16:c0 + 24],
                                    scr[:, c0 + 8:c0 + 16]), VV)
            k.w(v, VV, vc)
            vc = k.inc(v.reduce_max(msc[:, m0 + 4:m0 + 5],
                                    scr[:, c0 + 16:c0 + 24], AX.X), VV)
            k.w(v, VV, vc)
            vc = k.inc(v.tensor_scalar(scr[:, c0 + 16:c0 + 24],
                                       scr[:, c0 + 8:c0 + 16],
                                       msc[:, m0 + 4:m0 + 5], None, ALU.is_ge), VV)
            k.w(v, VV, vc)
            vc = k.inc(v.tensor_mul(scr[:, c0 + 16:c0 + 24],
                                    scr[:, c0 + 16:c0 + 24],
                                    scr[:, c0 + 8:c0 + 16]), VV)
            rt_vc[tt] = vc

        def router_b(tt):
            c0 = tt * 32
            b = k.bank()
            k.w(t, VV, rt_vc[tt])
            mc = k.inc(t.transpose(pb[b][0:8, 0:128], scr[:, c0 + 16:c0 + 24],
                                   idnf[:, :]), MM)
            k.w(s, MM, mc)
            ec = k.inc(s.copy(cebf[0:8, tt * 128:(tt + 1) * 128],
                              pb[b][0:8, 0:128]), EV)
            k.bank_free[b] = (EV, ec)

        def sg_bcast():
            k.w(v, EV, lg_ev)
            vc = k.inc(v.tensor_copy(sgb[0:1, :], sgf[0:1, :]), VV)
            bA, bB = k.bank(), k.bank()
            k.w(t, GG, 1)
            mc = k.chain_pair(pb[bA][:, 0:CH], pb[bB][:, 0:CH],
                              [(ones1[0:1, 0:128], sgb[0:1, 0:CH], sgb[0:1, CH:TP])],
                              waits=[(VV, vc)])
            k.w(s, MM, mc)
            ec = k.inc(s.copy(sgbc[:, 0:CH], pb[bA][:, 0:CH]), EV)
            ec = k.inc(s.copy(sgbc[:, CH:TP], pb[bB][:, 0:CH]), EV)
            k.bank_free[bA] = (EV, ec)
            k.bank_free[bB] = (EV, ec)

        ce_ev = 0
        sg_ev = 0
        gate_st = [0] * 44
        for jt in range(44):
            stg, lc, lkey = k.stage([(lambda sg: st3(sg, NF, 128),
                                      r3(w_f["wsgu"])[:, :, jt * 128:(jt + 1) * 128])],
                                    cast=True)
            sl = spl_slot(s)
            bA, bB = k.bank(), k.bank()
            trips = [(stg[:, kk * 128:(kk + 1) * 128],
                      xn[:, kk * TP: kk * TP + CH],
                      xn[:, kk * TP + CH: (kk + 1) * TP]) for kk in range(NF)]
            mc = k.chain_pair(pb[bA][:, 0:CH], pb[bB][:, 0:CH], trips,
                              waits=[(lkey, lc)])
            k.w(s, MM, mc)
            last = k.inc(s.activation(spl[:, sl * 1024:sl * 1024 + CH],
                                      pb[bA][:, 0:CH], AF.Silu), EV)
            last = k.inc(s.activation(spl[:, sl * 1024 + CH:sl * 1024 + TP],
                                      pb[bB][:, 0:CH], AF.Silu), EV)
            k.bank_free[bA] = (EV, last)
            k.bank_free[bB] = (EV, last)
            k.stage_consumed(MM, k.cnt[MM])
            sc = k.st(sp_hg[jt * 128:(jt + 1) * 128, :],
                      spl[:, sl * 1024:sl * 1024 + 1024], waits=[(EV, last)])
            gate_st[jt] = sc
            spl_free[sl] = (ST, sc)
            if jt < 8:
                router_a(jt)
                if jt >= 1:
                    router_b(jt - 1)
            elif jt == 8:
                router_b(7)
                ce_ev = k.cnt[EV]
                sg_bcast()
                sg_ev = k.cnt[EV]
        ymark = marks["ybuf"]
        yb_free = [ymark if ymark[0] else (None, 0)] * 4
        for jt in range(44):
            stg, lc, lkey = k.stage([(lambda sg: st3(sg, NF, 128),
                                      r3(w_f["wsgu"])[:, :, (44 + jt) * 128:(45 + jt) * 128])],
                                    cast=True)
            ysl = jt % 4
            glc = k.ld(ybuf[:, ysl * 1024:(ysl + 1) * 1024],
                       sp_hg[jt * 128:(jt + 1) * 128, :],
                       waits=[(ST, gate_st[jt])] +
                             ([yb_free[ysl]] if yb_free[ysl][0] else []))
            sl = spl_slot(v)
            bA, bB = k.bank(), k.bank()
            trips = [(stg[:, kk * 128:(kk + 1) * 128],
                      xn[:, kk * TP: kk * TP + CH],
                      xn[:, kk * TP + CH: (kk + 1) * TP]) for kk in range(NF)]
            mc = k.chain_pair(pb[bA][:, 0:CH], pb[bB][:, 0:CH], trips,
                              waits=[(lkey, lc)])
            k.w(v, MM, mc)
            k.w(v, LD, glc)
            k.w(v, EV, sg_ev)
            lastv = 0
            for ncz, bx in ((0, bA), (1, bB)):
                v.tensor_mul(scr[:, 512:1024], pb[bx][:, 0:CH],
                             ybuf[:, ysl * 1024 + ncz * CH: ysl * 1024 + ncz * CH + CH])
                lastv = k.inc(v.tensor_mul(spl[:, sl * 1024 + ncz * CH:
                                               sl * 1024 + ncz * CH + CH],
                                           scr[:, 512:1024],
                                           sgbc[:, ncz * CH:ncz * CH + CH]), VV)
                k.bank_free[bx] = (VV, lastv)
            k.stage_consumed(MM, k.cnt[MM])
            yb_free[ysl] = (VV, lastv)
            sc = k.st(sp_hsh[jt * 128:(jt + 1) * 128, :],
                      spl[:, sl * 1024:sl * 1024 + 1024], waits=[(VV, lastv)])
            spl_free[sl] = (ST, sc)
        hsh_st = k.cnt[ST]
        yb_last = (VV, k.cnt[VV])

        rW = r3(w_f["wsdn"])
        rH = sp_hsh.ap().rearrange("(a p) c -> p a c", p=128)
        yb_half_free = [yb_last, yb_last]
        for mg2 in range(8):
            banks = [k.bank(), k.bank(), k.bank(), k.bank()]
            runs = [(0, 8), (8, 16), (16, 24), (24, 32), (32, 40), (40, 44)]
            for ri, (ka, kb_) in enumerate(runs):
                nk = kb_ - ka
                stg, lc, lkey = k.stage([(lambda sg, nk=nk: st3(sg, nk, 256),
                                          rW[:, ka:kb_, mg2 * 256:(mg2 + 1) * 256])],
                                        cast=True)
                yh = ri % 2
                ylc = k.ld(ybuf[:, yh * 8192:yh * 8192 + nk * TP]
                           .rearrange("p (a c) -> p a c", c=TP),
                           rH[:, ka:kb_, :],
                           waits=[(ST, hsh_st)] +
                                 ([yb_half_free[yh]] if yb_half_free[yh][0] else []))
                for m2 in range(2):
                    bA = banks[m2 * 2]
                    bB = banks[m2 * 2 + 1]
                    trips = [(stg[:, kk * 256 + m2 * 128: kk * 256 + m2 * 128 + 128],
                              ybuf[:, yh * 8192 + kk * TP: yh * 8192 + kk * TP + CH],
                              ybuf[:, yh * 8192 + kk * TP + CH:
                                   yh * 8192 + (kk + 1) * TP])
                             for kk in range(nk)]
                    k.chain_pair(pb[bA][:, 0:CH], pb[bB][:, 0:CH], trips,
                                 waits=[(lkey, lc), (LD, ylc)],
                                 start=(ri == 0), stop=(ri == len(runs) - 1))
                yb_half_free[yh] = (MM, k.cnt[MM])
                k.stage_consumed(MM, k.cnt[MM])
            for m2 in range(2):
                for ncz in range(2):
                    bi = banks[m2 * 2 + ncz]
                    mglob = mg2 * 2 + m2
                    k.w(s, MM, k.cnt[MM])
                    k.w(s, VV, e_vv)
                    ec = k.inc(s.copy(bigF[:, mglob * TP + ncz * CH:
                                           mglob * TP + ncz * CH + CH],
                                      pb[bi][:, 0:CH]), EV)
                    k.bank_free[bi] = (EV, ec)

        for e in range(8):
            ex_mm = k.cnt[MM]
            bA, bB = k.bank(), k.bank()
            mc = k.chain_pair(pb[bA][:, 0:CH], pb[bB][:, 0:CH],
                              [(sel8[:, e * 128:(e + 1) * 128],
                                cebf[0:8, 0:CH], cebf[0:8, CH:TP])],
                              waits=[(EV, ce_ev), (LD, 32)])
            k.w(s, MM, mc)
            ec = k.inc(s.copy(cebc[:, 0:CH], pb[bA][:, 0:CH]), EV)
            ec = k.inc(s.copy(cebc[:, CH:TP], pb[bB][:, 0:CH]), EV)
            k.bank_free[bA] = (EV, ec)
            k.bank_free[bB] = (EV, ec)
            ce_bc_ev = k.cnt[EV]
            wge = w_f["wegu"].ap()[e * D:(e + 1) * D, :].rearrange(
                "(a p) c -> p a c", p=128)
            for jt in range(11):
                stg, lc, lkey = k.stage([(lambda sg: st3(sg, NF, 128),
                                          wge[:, :, jt * 128:(jt + 1) * 128])],
                                        cast=True)
                bA, bB = k.bank(), k.bank()
                trips = [(stg[:, kk * 128:(kk + 1) * 128],
                          xn[:, kk * TP: kk * TP + CH],
                          xn[:, kk * TP + CH: (kk + 1) * TP]) for kk in range(NF)]
                mc = k.chain_pair(pb[bA][:, 0:CH], pb[bB][:, 0:CH], trips,
                                  waits=[(lkey, lc)])
                k.w(s, MM, max(mc, ex_mm))
                ec = k.inc(s.activation(ybuf[:, jt * TP: jt * TP + CH],
                                        pb[bA][:, 0:CH], AF.Silu), EV)
                ec = k.inc(s.activation(ybuf[:, jt * TP + CH: (jt + 1) * TP],
                                        pb[bB][:, 0:CH], AF.Silu), EV)
                k.bank_free[bA] = (EV, ec)
                k.bank_free[bB] = (EV, ec)
                k.stage_consumed(MM, k.cnt[MM])
            for jt in range(11):
                stg, lc, lkey = k.stage([(lambda sg: st3(sg, NF, 128),
                                          wge[:, :, (11 + jt) * 128:(12 + jt) * 128])],
                                        cast=True)
                bA, bB = k.bank(), k.bank()
                trips = [(stg[:, kk * 128:(kk + 1) * 128],
                          xn[:, kk * TP: kk * TP + CH],
                          xn[:, kk * TP + CH: (kk + 1) * TP]) for kk in range(NF)]
                mc = k.chain_pair(pb[bA][:, 0:CH], pb[bB][:, 0:CH], trips,
                                  waits=[(lkey, lc)])
                k.w(v, MM, mc)
                k.w(v, EV, max(ce_bc_ev, k.cnt[EV]))
                lastv = 0
                for ncz, bx in ((0, bA), (1, bB)):
                    v.tensor_mul(scr[:, 512:1024], pb[bx][:, 0:CH],
                                 ybuf[:, jt * TP + ncz * CH: jt * TP + ncz * CH + CH])
                    lastv = k.inc(v.tensor_mul(ybuf[:, jt * TP + ncz * CH:
                                                    jt * TP + ncz * CH + CH],
                                               scr[:, 512:1024],
                                               cebc[:, ncz * CH:ncz * CH + CH]), VV)
                    k.bank_free[bx] = (VV, lastv)
                k.stage_consumed(MM, k.cnt[MM])
            he_vv = k.cnt[VV]
            wde = w_f["wedn"].ap()[e * 1408:(e + 1) * 1408, :].rearrange(
                "(a p) c -> p a c", p=128)
            for mg2 in range(8):
                stg, lc, lkey = k.stage([(lambda sg: st3(sg, 11, 256),
                                          wde[:, :, mg2 * 256:(mg2 + 1) * 256])],
                                        cast=True)
                for m2 in range(2):
                    mglob = mg2 * 2 + m2
                    bA, bB = k.bank(), k.bank()
                    trips = [(stg[:, kk * 256 + m2 * 128: kk * 256 + m2 * 128 + 128],
                              ybuf[:, kk * TP: kk * TP + CH],
                              ybuf[:, kk * TP + CH: (kk + 1) * TP])
                             for kk in range(11)]
                    mc = k.chain_pair(pb[bA][:, 0:CH], pb[bB][:, 0:CH], trips,
                                      waits=[(lkey, lc), (VV, he_vv)])
                    k.w(v, MM, mc)
                    for ncz, bx in ((0, bA), (1, bB)):
                        dst = bigF[:, mglob * TP + ncz * CH: mglob * TP + ncz * CH + CH]
                        vc = k.inc(v.tensor_add(dst, dst, pb[bx][:, 0:CH]), VV)
                        k.bank_free[bx] = (VV, vc)
                k.stage_consumed(MM, k.cnt[MM])
        f_vv = k.cnt[VV]
        if DBG:
            for mf in range(NF):
                k.st(dbg_F[p * D + mf * 128: p * D + (mf + 1) * 128, :],
                     bigF[:, mf * TP:(mf + 1) * TP], waits=[(VV, f_vv)])
        for mf in range(NF):
            sl = spl_slot(s)
            k.w(s, VV, f_vv)
            ec = k.inc(s.activation(spl[:, sl * 1024:sl * 1024 + 1024],
                                    bigF[:, mf * TP:(mf + 1) * TP], AF.Silu), EV)
            sc = k.st(sp_g[p * D + mf * 128: p * D + (mf + 1) * 128, :],
                      spl[:, sl * 1024:sl * 1024 + 1024], waits=[(EV, ec)])
            spl_free[sl] = (ST, sc)
        marks["ybuf"] = (MM, k.cnt[MM])
        return (EV, k.cnt[EV])

    def phase_CC_issue():
        sc = k.st(cc_in[:, :], msc[:, MTOT:MTOT + 32], waits=[(VV, k.cnt[VV])])
        k.w(g, ST, sc)
        cc = k.inc(g.collective_compute(
            "AllGather", ALU.bypass,
            replica_groups=[[0, 1], [2, 3], [4, 5], [6, 7]],
            ins=[cc_in.ap()], outs=[cc_out.ap()]), CC, 1)
        return cc

    def phase_CC_fin(cc):
        k.w(sy, CC, cc)
        lcc = k.ld(msc[:, MCAR:MCAR + 32], cc_out[0:128, :])
        k.w(v, LD, lcc)
        vc = k.inc(v.tensor_scalar_mul(msc[:, MCAR:MCAR + 32],
                                       msc[:, MCAR:MCAR + 32],
                                       msc[:, MMK:MMK + 1]), VV)
        return vc

    def phase_J(p, car_vv, big2_mark):
        sn = k.snap()
        yb_free = [(None, 0)] * 8
        for mf in range(NF):
            r0 = p * 2 * D + mf * 128
            sls = [(mf % 2) * 4 + i for i in range(4)]
            lc = 0
            for i, (sp, pl) in enumerate([(sp_xh, 0), (sp_xh, 1),
                                          (sp_sf, 0), (sp_sf, 1)]):
                ysl = sls[i]
                wts = [(ST, sn[ST])]
                if yb_free[ysl][0]:
                    wts.append(yb_free[ysl])
                elif mf < 2:
                    wts += [(MM, sn[MM]), (VV, sn[VV])]
                lc = k.ld(ybuf[:, ysl * 1024:(ysl + 1) * 1024],
                          sp[r0 + pl * D:r0 + pl * D + 128, :], waits=wts)
            xhR = ybuf[:, sls[0] * 1024:sls[0] * 1024 + 1024]
            xhI = ybuf[:, sls[1] * 1024:sls[1] * 1024 + 1024]
            sR = ybuf[:, sls[2] * 1024:sls[2] * 1024 + 1024]
            sI = ybuf[:, sls[3] * 1024:sls[3] * 1024 + 1024]
            carR = msc[:, MCAR + mf:MCAR + mf + 1]
            carI = msc[:, MCAR + 16 + mf:MCAR + 16 + mf + 1]
            k.w(v, LD, lc)
            k.w(v, VV, car_vv)
            if big2_mark[0]:
                k.w(v, big2_mark[0], big2_mark[1])
            x1 = scr[:, 0:1024]
            x2 = scr[:, 1024:2048]
            t1 = scr[:, 2048:3072]
            t2 = scr[:, 3072:4096]
            v.tensor_scalar(x1, xhR, carR, None, ALU.add)
            v.tensor_scalar(x2, xhI, carI, None, ALU.add)
            taR = big2[:, mf * TP:(mf + 1) * TP]
            taI = big2[:, NF * TP + mf * TP:NF * TP + (mf + 1) * TP]
            v.tensor_mul(t1, sR, x1)
            v.tensor_mul(t2, sI, x2)
            v.tensor_add(taR, t1, t2)
            v.tensor_mul(t1, sI, x1)
            v.tensor_mul(t2, sR, x2)
            v.tensor_sub(taI, t1, t2)
            v.tensor_mul(t1, sR, sR)
            v.tensor_mul(t2, sI, sI)
            v.tensor_add(t1, t1, t2)
            v.tensor_scalar(t1, t1, -1.0 / D, 1.0 / D, ALU.mult, ALU.add)
            slR = spl_slot(v)
            slI = spl_slot(v)
            v.tensor_mul(spl[:, slR * 1024:slR * 1024 + 1024], t1, x1)
            vc = k.inc(v.tensor_mul(spl[:, slI * 1024:slI * 1024 + 1024], t1, x2), VV)
            for ysl in sls:
                yb_free[ysl] = (VV, vc)
            sc = k.st(sp_z0[r0:r0 + 128, :], spl[:, slR * 1024:slR * 1024 + 1024],
                      waits=[(VV, vc)])
            sc = k.st(sp_z0[r0 + D:r0 + D + 128, :],
                      spl[:, slI * 1024:slI * 1024 + 1024])
            spl_free[slR] = (ST, sc)
            spl_free[slI] = (ST, sc)
            if DBG:
                k.st(sp_ta[r0:r0 + 128, :], taR, waits=[(VV, vc)])
                k.st(sp_ta[r0 + D:r0 + D + 128, :], taI)
        return (VV, k.cnt[VV])

    def phase_M1(p, ta_vv):
        h1_st = [0] * 64
        for jg in range(64):
            stg, lc, lkey = k.stage([(lambda sg: st3(sg, 32, 128),
                                      r3(w_f["w1"])[:, :, jg * 128:(jg + 1) * 128])],
                                    cast=True)
            sl = spl_slot(s)
            bA, bB = k.bank(), k.bank()
            trips = [(stg[:, kk * 128:(kk + 1) * 128],
                      big2[:, kk * TP: kk * TP + CH],
                      big2[:, kk * TP + CH: (kk + 1) * TP]) for kk in range(32)]
            mc = k.chain_pair(pb[bA][:, 0:CH], pb[bB][:, 0:CH], trips,
                              waits=[(lkey, lc), ta_vv])
            k.w(s, MM, mc)
            last = k.inc(s.activation(spl[:, sl * 1024:sl * 1024 + CH],
                                      pb[bA][:, 0:CH], AF.Relu,
                                      bias=msc[:, MB1 + jg:MB1 + jg + 1]), EV)
            last = k.inc(s.activation(spl[:, sl * 1024 + CH:sl * 1024 + TP],
                                      pb[bB][:, 0:CH], AF.Relu,
                                      bias=msc[:, MB1 + jg:MB1 + jg + 1]), EV)
            k.bank_free[bA] = (EV, last)
            k.bank_free[bB] = (EV, last)
            k.stage_consumed(MM, k.cnt[MM])
            sc = k.st(sp_h1[jg * 128:(jg + 1) * 128, :],
                      spl[:, sl * 1024:sl * 1024 + 1024], waits=[(EV, last)])
            h1_st[jg] = sc
            spl_free[sl] = (ST, sc)
        return k.cnt[MM], h1_st

    def phase_M2(p, h1_st, l_tile=None):
        rW2 = r3(w_f["w2"])
        rH1 = sp_h1.ap().rearrange("(a p) c -> p a c", p=128)
        xn_half_free = [marks["xnreg"], marks["xnreg"]]
        sph_st = {}
        for og in range(16):
            banks = [k.bank(), k.bank(), k.bank(), k.bank()]
            for ri in range(8):
                ka, kb_ = ri * 8, (ri + 1) * 8
                stg, lc, lkey = k.stage([(lambda sg: st3(sg, 8, 256),
                                          rW2[:, ka:kb_, og * 256:(og + 1) * 256])],
                                        cast=True)
                xh_ = ri % 2
                hlc = k.ld(xn[:, xh_ * 8192:xh_ * 8192 + 8 * TP]
                           .rearrange("p (a c) -> p a c", c=TP),
                           rH1[:, ka:kb_, :],
                           waits=[(ST, h1_st[ri * 8 + 7])] +
                                 ([xn_half_free[xh_]] if xn_half_free[xh_][0] else []))
                for m2 in range(2):
                    bA = banks[m2 * 2]
                    bB = banks[m2 * 2 + 1]
                    trips = [(stg[:, kk * 256 + m2 * 128: kk * 256 + m2 * 128 + 128],
                              xn[:, xh_ * 8192 + kk * TP: xh_ * 8192 + kk * TP + CH],
                              xn[:, xh_ * 8192 + kk * TP + CH:
                                 xh_ * 8192 + (kk + 1) * TP])
                             for kk in range(8)]
                    k.chain_pair(pb[bA][:, 0:CH], pb[bB][:, 0:CH], trips,
                                 waits=[(lkey, lc), (LD, hlc)],
                                 start=(ri == 0), stop=(ri == 7))
                xn_half_free[xh_] = (MM, k.cnt[MM])
                k.stage_consumed(MM, k.cnt[MM])
            for m2 in range(2):
                mglob = og * 2 + m2
                ssl = spl_slot(s)
                last = 0
                for ncz in range(2):
                    bi = banks[m2 * 2 + ncz]
                    k.w(s, MM, k.cnt[MM])
                    last = k.inc(s.activation(spl[:, ssl * 1024 + ncz * CH:
                                                  ssl * 1024 + ncz * CH + CH],
                                              pb[bi][:, 0:CH], AF.Identity,
                                              bias=msc[:, MB2 + mglob:
                                                       MB2 + mglob + 1]), EV)
                    k.bank_free[bi] = (EV, last)
                sc = k.st(sp_h[mglob * 128:(mglob + 1) * 128, :],
                          spl[:, ssl * 1024:ssl * 1024 + 1024], waits=[(EV, last)])
                sph_st[mglob] = sc
                spl_free[ssl] = (ST, sc)
            if l_tile is not None and og >= 8:
                for j_ in range(2):
                    mf = 2 * (og - 8) + j_
                    l_tile(mf, sph_st[16 + mf])
        marks["xnreg"] = (MM, k.cnt[MM])

    def make_L(p, ta_done_mm):
        sn = k.snap()
        yb_free = [(None, 0)] * 12
        nrun = [0]

        def L_tile(mf, h_st_val):
            r0 = p * 2 * D + mf * 128
            sls = [(mf % 2) * 6 + i for i in range(6)]
            srcs = [(sp_h, mf * 128, True), (sp_h, D + mf * 128, True),
                    (sp_tf, r0, False), (sp_tf, r0 + D, False),
                    (sp_z0, r0, False), (sp_z0, r0 + D, False)]
            lc = 0
            for i, (sp, rr, ish) in enumerate(srcs):
                ysl = sls[i]
                wts = [(ST, h_st_val if ish else sn[ST])]
                if yb_free[ysl][0]:
                    wts.append(yb_free[ysl])
                elif nrun[0] < 2:
                    wts += [(MM, sn[MM]), (VV, sn[VV])]
                lc = k.ld(ybuf[:, ysl * 1024:(ysl + 1) * 1024],
                          sp[rr:rr + 128, :], waits=wts)
            nrun[0] += 1
            h1c = ybuf[:, sls[0] * 1024:sls[0] * 1024 + 1024]
            h2c = ybuf[:, sls[1] * 1024:sls[1] * 1024 + 1024]
            tR = ybuf[:, sls[2] * 1024:sls[2] * 1024 + 1024]
            tI = ybuf[:, sls[3] * 1024:sls[3] * 1024 + 1024]
            z0R = ybuf[:, sls[4] * 1024:sls[4] * 1024 + 1024]
            z0I = ybuf[:, sls[5] * 1024:sls[5] * 1024 + 1024]
            k.w(v, LD, lc)
            k.w(v, MM, ta_done_mm)
            t1 = scr[:, 0:1024]
            t2 = scr[:, 1024:2048]
            zR = big2[:, mf * TP:(mf + 1) * TP]
            zP = big2[:, NF * TP + mf * TP:NF * TP + (mf + 1) * TP]
            v.tensor_mul(t1, tR, h1c)
            v.tensor_mul(t2, tI, h2c)
            v.tensor_add(t1, t1, t2)
            v.scalar_tensor_tensor(zR, t1, 1.0 / D, z0R, ALU.mult, ALU.add)
            v.tensor_mul(t1, tI, h1c)
            v.tensor_mul(t2, tR, h2c)
            v.tensor_sub(t1, t1, t2)
            vc = k.inc(v.scalar_tensor_tensor(zP, t1, 1.0 / D, z0I,
                                              ALU.mult, ALU.add), VV)
            for ysl in sls:
                yb_free[ysl] = (VV, vc)

        return L_tile

    def phase_M(p, z_vv):
        sn = k.snap()
        g_free = [(None, 0)] * 2
        for mf in range(NF):
            stg, lc, lkey = k.stage([
                (lambda sg_: sg_[:, 0:NF * 128].rearrange("p (a c) -> p a c", c=128),
                 r3(cmat)[:, :, mf * 128:(mf + 1) * 128]),
                (lambda sg_: sg_[:, NF * 128:2 * NF * 128].rearrange(
                    "p (a c) -> p a c", c=128),
                 r3(smat)[:, :, mf * 128:(mf + 1) * 128])])
            gsl = mf % 2
            wts = [(ST, sn[ST])]
            if g_free[gsl][0]:
                wts.append(g_free[gsl])
            else:
                wts.append((MM, marks["xnreg"][1]))
            glc = k.ld(xn[:, gsl * 1024:(gsl + 1) * 1024],
                       sp_g[p * D + mf * 128: p * D + (mf + 1) * 128, :], waits=wts)
            bA, bB = k.bank(), k.bank()
            trips = ([(stg[:, kk * 128:(kk + 1) * 128],
                       big2[:, kk * TP: kk * TP + CH],
                       big2[:, kk * TP + CH: (kk + 1) * TP]) for kk in range(NF)] +
                     [(stg[:, NF * 128 + kk * 128: NF * 128 + (kk + 1) * 128],
                       big2[:, NF * TP + kk * TP: NF * TP + kk * TP + CH],
                       big2[:, NF * TP + kk * TP + CH: NF * TP + (kk + 1) * TP])
                      for kk in range(NF)])
            mc = k.chain_pair(pb[bA][:, 0:CH], pb[bB][:, 0:CH], trips,
                              waits=[(lkey, lc), z_vv])
            k.w(v, MM, mc)
            k.w(v, LD, glc)
            if mf == 0:
                k.wsnap(v, sn, keys=(MM,))
            lastv = 0
            for ncz, bx in ((0, bA), (1, bB)):
                lastv = k.inc(v.tensor_mul(ybuf[:, mf * TP + ncz * CH:
                                                mf * TP + ncz * CH + CH],
                                           pb[bx][:, 0:CH],
                                           xn[:, gsl * 1024 + ncz * CH:
                                              gsl * 1024 + ncz * CH + CH]), VV)
                k.bank_free[bx] = (VV, lastv)
            g_free[gsl] = (VV, lastv)
            k.stage_consumed(MM, k.cnt[MM])
        sg_vv = k.cnt[VV]
        x_free = (None, 0)
        prev_store = 0
        for tt in range(8):
            row0 = p * TP + tt * 128
            wts = [(VV, sn[VV]), (EV, sn[EV])] if tt == 0 else []
            if x_free[0]:
                wts.append(x_free)
            xlc = k.ld(scr[:, 2048:4096], x_in[row0:row0 + 128, :], waits=wts)
            lastv = 0
            for mf in range(NF):
                b = k.ptbank()
                k.w(t, VV, sg_vv)
                mc = k.inc(t.transpose(pt[b][:, 0:128],
                                       ybuf[:, mf * TP + tt * 128:
                                            mf * TP + (tt + 1) * 128],
                                       idn[:, :]), MM)
                k.w(v, MM, mc)
                k.w(v, LD, xlc)
                if mf == 0:
                    k.w(v, ST, prev_store)
                lastv = k.inc(v.tensor_add(scr[:, mf * 128:(mf + 1) * 128],
                                           scr[:, 2048 + mf * 128:
                                               2048 + (mf + 1) * 128],
                                           pt[b][:, 0:128]), VV)
                k.pt_free[b] = (VV, lastv)
            sc = k.st(out[row0:row0 + 128, :], scr[:, 0:2048], waits=[(VV, lastv)])
            prev_store = sc
            x_free = (VV, lastv)

    b2_mark = (None, 0)
    cc_id = 0
    for p in range(NP):
        xn_ev = phase_A(p)
        kf_ev = phase_C(xn_ev, b2_mark)
        kv_vv = phase_D(xn_ev, kf_ev)
        e_vv = phase_E(p, xn_ev, kv_vv)
        if p == NP - 1:
            cc_id = phase_CC_issue()
        phase_GH(p, "ws", sp_sf, xn_ev)
        phase_GH(p, "wt", sp_tf, xn_ev)
        b2_mark = phase_B(p, xn_ev, e_vv)
    car_vv = phase_CC_fin(cc_id)
    marks["xnreg"] = (MM, k.cnt[MM])
    l_done_mm = 0
    for p in range(NP):
        big2_mark = b2_mark if p == 0 else (MM, l_done_mm)
        ta_vv = phase_J(p, car_vv, big2_mark)
        m1_mm, h1_st = phase_M1(p, ta_vv)
        l_tile = make_L(p, m1_mm)
        phase_M2(p, h1_st, l_tile)
        zvv = (VV, k.cnt[VV])
        phase_M(p, zvv)
        l_done_mm = k.cnt[MM]
    sy.wait_ge(k.sem[ST], k.cnt[ST])
    nc._kb_stack = stack
    return nc


_HOST = {}


def _host_constants():
    if not _HOST:
        n = np.arange(D, dtype=np.float64)
        th = 2.0 * np.pi * np.outer(n, n) / D
        _HOST["cmat"] = np.cos(th).astype(ml_dtypes.bfloat16)
        _HOST["smat"] = np.sin(th).astype(ml_dtypes.bfloat16)
        _HOST["idn"] = np.eye(128, dtype=ml_dtypes.bfloat16)
        s8 = np.zeros((8, 8 * 128), dtype=ml_dtypes.bfloat16)
        for e in range(8):
            s8[e, e * 128:(e + 1) * 128] = 1.0
        _HOST["sel8"] = s8
    return _HOST


def kernel(**inputs):
    hc = _host_constants()
    f32 = lambda a: np.ascontiguousarray(np.asarray(a, np.float32))
    x = f32(inputs["L0_attn_x"]).reshape(8 * T, D)
    shared = {
        "wq": f32(inputs["L0_attn_Wq"]), "wk": f32(inputs["L0_attn_Wk"]),
        "wv": f32(inputs["L0_attn_Wv"]), "ws": f32(inputs["L0_attn_Ws"]),
        "wt": f32(inputs["L0_attn_Wt"]),
        "w1": f32(inputs["L0_attn_W1"]), "w2": f32(inputs["L0_attn_W2"]),
        "wsgu": f32(inputs["L0_ffn_Ws_gate_up"]),
        "wsdn": f32(inputs["L0_ffn_Ws_down"]),
        "wegu": f32(inputs["L0_ffn_We_gate_up"]).reshape(8 * D, 2816),
        "wedn": f32(inputs["L0_ffn_We_down"]).reshape(8 * 1408, D),
        "wrt": np.ascontiguousarray(
            f32(inputs["L0_ffn_W_router"]).reshape(8, 16, 128)
            .transpose(2, 1, 0).reshape(128, 128)),
        "wsgt": np.ascontiguousarray(
            f32(inputs["L0_ffn_W_sgate"]).reshape(16, 128).T),
        "b1t": np.ascontiguousarray(f32(inputs["L0_attn_b1"]).reshape(64, 128).T),
        "b2t": np.ascontiguousarray(f32(inputs["L0_attn_b2"]).reshape(32, 128).T),
        "cmat": hc["cmat"], "smat": hc["smat"], "idn": hc["idn"],
        "sel8": hc["sel8"],
    }
    in_maps = []
    for c in range(8):
        m = dict(shared)
        m["x"] = np.ascontiguousarray(x[c * T:(c + 1) * T])
        m["mask"] = np.full((128, 1), float(c % 2), np.float32)
        in_maps.append(m)
    nc = build_nc()
    res = run_bass_kernel_spmd(nc, in_maps, core_ids=list(range(8)))
    out = np.concatenate([np.asarray(r["out"]) for r in res.results], axis=0)
    if DBG:
        kernel.dbg = res.results
    return out.reshape(4, 4096, D).astype(np.float32)


# version 10 (final)
# speedup vs baseline: 1.4079x; 1.1714x over previous
import numpy as np
import ml_dtypes
from contextlib import ExitStack
import concourse.bass as bass
import concourse.mybir as mybir
from concourse.bass_utils import run_bass_kernel_spmd

F32 = mybir.dt.float32
BF16 = mybir.dt.bfloat16
AF = mybir.ActivationFunctionType
ALU = mybir.AluOpType
AX = mybir.AxisListType

D = 2048
NF = 16
NFH = 9
TP = 1024
NP = 2
T = 2048
CH = 512

DBG = False

EV, VV, MM, LD, ST, WD, GG, CC = "ev", "vv", "mm", "ld", "st", "wd", "gg", "cc"
WD2, LD2 = "wd2", "ld2"


class KB:
    def __init__(self, nc, stack):
        self.nc = nc
        self.cnt = {kk: 0 for kk in (EV, VV, MM, LD, ST, WD, GG, CC, WD2, LD2)}
        self.sem = {kk: stack.enter_context(nc.semaphore("s_" + kk))
                    for kk in (EV, VV, MM, LD, ST, WD, GG, CC, WD2, LD2)}
        self.waited = {}
        self.bank_free = [(None, 0)] * 6
        self.bidx = 0
        self.pt_free = [(None, 0)] * 2
        self.pti = 0
        self.stg_free = [(None, 0)] * 2
        self.stg_i = 0

    def w(self, eng, key, val):
        if not key or val is None or val <= 0:
            return
        kk = (eng.engine, key)
        if self.waited.get(kk, 0) >= val:
            return
        eng.wait_ge(self.sem[key], val)
        self.waited[kk] = val

    def snap(self):
        return {kk: self.cnt[kk] for kk in (EV, VV, MM, LD, ST, WD)}

    def wsnap(self, eng, sn, keys=(EV, VV, MM, ST)):
        for kk in keys:
            self.w(eng, kk, sn[kk])

    def inc(self, instr, key, n=1):
        instr.then_inc(self.sem[key], n)
        self.cnt[key] += n
        return self.cnt[key]

    def ld(self, out, in_, waits=(), eng=None):
        eng = eng or self.nc.sync
        for kk, vv in waits:
            self.w(eng, kk, vv)
        return self.inc(eng.dma_start(out=out, in_=in_), LD, 16)

    def st(self, out, in_, waits=()):
        for kk, vv in waits:
            self.w(self.nc.sync, kk, vv)
        return self.inc(self.nc.sync.dma_start(out=out, in_=in_), ST, 16)

    def bank(self):
        b = self.bidx % 6
        self.bidx += 1
        kk, vv = self.bank_free[b]
        self.w(self.nc.tensor, kk, vv)
        return b

    def ptbank(self):
        b = self.pti % 2
        self.pti += 1
        kk, vv = self.pt_free[b]
        self.w(self.nc.tensor, kk, vv)
        return b

    def chain(self, bank_ap, pairs, waits=(), start=True, stop=True):
        t = self.nc.tensor
        for kk, vv in waits:
            self.w(t, kk, vv)
        n = len(pairs)
        mm = None
        for i, (l, r) in enumerate(pairs):
            mm = t.matmul(bank_ap, l, r,
                          start=(start and i == 0), stop=(stop and i == n - 1))
        return self.inc(mm, MM, 1)

    def chain_pair(self, bA, bB, trips, waits=(), start=True, stop=True):
        t = self.nc.tensor
        for kk, vv in waits:
            self.w(t, kk, vv)
        n = len(trips)
        mm = None
        for i, (l, ra, rb) in enumerate(trips):
            st_, sp_ = (start and i == 0), (stop and i == n - 1)
            t.matmul(bA, l, ra, start=st_, stop=sp_)
            mm = t.matmul(bB, l, rb, start=st_, stop=sp_)
        return self.inc(mm, MM, 1)

    def stage(self, loads, extra_wait=(), cast=False):
        h = self.stg_i % 2
        self.stg_i += 1
        kk, vv = self.stg_free[h]
        eng = self.nc.gpsimd if cast else self.nc.sync
        key = (WD if h == 0 else WD2) if cast else (LD if h == 0 else LD2)
        waits = list(extra_wait) + ([(kk, vv)] if kk else [])
        stg = self.stgv[h]
        lc = 0
        for dvf, src in loads:
            for kx, vx in waits:
                self.w(eng, kx, vx)
            waits = []
            lc = self.inc(eng.dma_start(out=dvf(stg), in_=src), key, 16)
        self._last_half = h
        return stg, lc, key

    def stage_consumed(self, key, val):
        self.stg_free[self._last_half] = (key, val)


def build_nc():
    nc = bass.Bass("TRN2", target_bir_lowering=False)
    EI, EO, IN = "ExternalInput", "ExternalOutput", "Internal"
    dt = nc.dram_tensor
    x_in = dt("x", [T, D], F32, kind=EI)
    w_f = {n: dt(n, sh, F32, kind=EI) for n, sh in [
        ("wq", [D, D]), ("wk", [D, D]), ("wv", [D, D]), ("ws", [D, D]), ("wt", [D, D]),
        ("w1", [2 * D, 4 * D]), ("w2", [4 * D, 2 * D]),
        ("wsgu", [D, 11264]), ("wsdn", [5632, D]),
        ("wegu", [8 * D, 2816]), ("wedn", [8 * 1408, D])]}
    wrt = dt("wrt", [128, 128], F32, kind=EI)
    wsgt = dt("wsgt", [128, 16], F32, kind=EI)
    b1t_in = dt("b1t", [128, 64], F32, kind=EI)
    b2t_in = dt("b2t", [128, 32], F32, kind=EI)
    cmat = dt("cmat", [D, D], BF16, kind=EI)
    smat = dt("smat", [D, D], BF16, kind=EI)
    idn_in = dt("idn", [128, 128], BF16, kind=EI)
    sel8_in = dt("sel8", [8, 8 * 128], BF16, kind=EI)
    mask_in = dt("mask", [128, 1], F32, kind=EI)
    revs_in = dt("revs", [128, 6 * 128], BF16, kind=EI)
    out = dt("out", [T, D], F32, kind=EO)

    spk = EO if DBG else IN
    sp_xh = dt("sp_xh", [NP * 2 * D, TP], BF16, kind=spk)
    sp_sf = dt("sp_sf", [NP * 2 * D, TP], BF16, kind=spk)
    sp_tf = dt("sp_tf", [NP * 2 * D, TP], BF16, kind=spk)
    sp_g = dt("sp_g", [NP * D, TP], BF16, kind=spk)
    sp_mt = dt("sp_mt", [NP * D, TP], BF16, kind=spk)
    sp_z0 = dt("sp_z0", [NP * 2 * D, TP], BF16, kind=spk)
    sp_h1 = dt("sp_h1", [4 * D, TP], BF16, kind=spk)
    sp_h = dt("sp_h", [2 * D, TP], BF16, kind=spk)
    sp_hg = dt("sp_hg", [5632, TP], BF16, kind=spk)
    sp_hsh = dt("sp_hsh", [5632, TP], BF16, kind=spk)
    sp_ta = dt("sp_ta", [NP * 2 * D, TP], BF16, kind=spk)
    sp_xn = dt("sp_xn", [NP * D, TP], BF16, kind=spk)
    dbg_ce = dt("dbg_ce", [8, TP], BF16, kind=spk)
    dbg_lg = dt("dbg_lg", [8, TP], F32, kind=spk)
    dbg_sg = dt("dbg_sg", [128, TP], BF16, kind=spk)
    dbg_F = dt("dbg_F", [NP * D, TP], F32, kind=spk)
    cc_in = dt("cc_in", [128, 32], F32, kind=IN)
    cc_out = dt("cc_out", [256, 32], F32, kind=IN)
    w1c = dt("w1c", [128, 64 * 2048], BF16, kind=IN)
    w2c = dt("w2c", [128, 88 * 3072], BF16, kind=IN)

    stack = ExitStack()
    sb = lambda n, sh, d: stack.enter_context(nc.sbuf_tensor(n, sh, d))
    psal = lambda n, sh, d: stack.enter_context(nc.psum_tensor(n, sh, d))
    xn = sb("xn", [128, NF * TP], BF16)
    ybuf = sb("ybuf", [128, NF * TP], BF16)
    big2 = sb("big2", [128, 2 * NF * TP], BF16)
    bigF = big2.bitcast(F32)
    stg0 = sb("stg0", [128, 4096], BF16)
    stg1 = sb("stg1", [128, 4096], BF16)
    scr = sb("scr", [128, 4096], F32)
    spl = sb("spl", [128, 4096], BF16)
    msc = sb("msc", [128, 1024], F32)
    fldb = sb("fldb", [128, 2 * 2048], BF16)
    revb = sb("revb", [128, 6 * 128], BF16)
    carb = sb("carb", [128, 32], BF16)
    idn = sb("idnb", [128, 128], BF16)
    idnf = sb("idnf", [128, 128], F32)
    ones1 = sb("ones1", [1, 128], BF16)
    sel8 = sb("sel8b", [8, 8 * 128], BF16)
    wrtb = sb("wrtb", [128, 128], BF16)
    wsgtb = sb("wsgtb", [128, 16], BF16)
    lgf = sb("lgf", [8, TP], F32)
    cebf = sb("cebf", [8, TP], BF16)
    sgf = sb("sgf", [1, TP], F32)
    sgb = sb("sgb", [1, TP], BF16)
    sgbc = sb("sgbc", [128, TP], BF16)
    cebc = sb("cebc", [128, TP], BF16)
    pb = [psal(f"pb{i}", [128, 512], F32) for i in range(6)]
    pt = [psal(f"pt{i}", [128, 512], BF16) for i in range(2)]

    MB1, MB2, MMK = 0, 64, 96
    MLN = 100
    MTOT = 200
    MCAR = 232
    MRT = 264

    k = KB(nc, stack)
    k.stgv = [stg0, stg1]
    sy, g, t, s, v = nc.sync, nc.gpsimd, nc.tensor, nc.scalar, nc.vector

    r3 = lambda w: w.ap().rearrange("(a p) c -> p a c", p=128)
    st3 = lambda stg, n, c: stg[:, 0:n * c].rearrange("p (a c) -> p a c", c=c)

    k.ld(idn[:, :], idn_in[:, :])
    k.ld(sel8[:, :], sel8_in[:, :])
    rv_lc = k.ld(revb[:, :], revs_in[:, :])
    RVP = revb[:, 0 * 128:1 * 128]
    RVPN = revb[:, 1 * 128:2 * 128]
    RVA = revb[:, 2 * 128:3 * 128]
    RVAN = revb[:, 3 * 128:4 * 128]
    RVB = revb[:, 4 * 128:5 * 128]
    RVBN = revb[:, 5 * 128:6 * 128]
    k.ld(msc[:, MB1:MB1 + 64], b1t_in[:, :])
    k.ld(msc[:, MB2:MB2 + 32], b2t_in[:, :])
    k.ld(msc[:, MMK:MMK + 1], mask_in[:, :])
    lc = k.ld(scr[:, 0:128], wrt[:, :])
    lc = k.ld(scr[:, 128:144], wsgt[:, :])
    k.w(v, LD, lc)
    v.tensor_copy(wrtb[:, :], scr[:, 0:128])
    v.tensor_copy(wsgtb[:, :], scr[:, 128:144])
    v.tensor_copy(idnf[:, :], idn[:, :])
    k.inc(v.memset(msc[:, MTOT:MTOT + 32], 0.0), VV)
    k.inc(g.memset(ones1[:, :], 1.0), GG)

    spl_free = [(None, 0)] * 4
    spl_ii = [0]

    def spl_slot(prod_eng):
        i = spl_ii[0] % 4
        spl_ii[0] += 1
        kk, vv = spl_free[i]
        if kk:
            k.w(prod_eng, kk, vv)
        return i

    marks = {"ybuf": (None, 0), "xnreg": (None, 0)}
    wcache_st = {"w1": 0, "w2": 0}

    def phase_A(p):
        sn = k.snap()
        for tt in range(8):
            row0 = p * TP + tt * 128
            wts = ([(VV, sn[VV]), (EV, sn[EV]), (MM, sn[MM])] if tt == 0
                   else [(VV, k.cnt[VV]), (EV, k.cnt[EV])])
            lc = k.ld(scr[:, 0:2048], x_in[row0:row0 + 128, :], waits=wts)
            k.w(v, LD, lc)
            vc = k.inc(v.reduce_sum(msc[:, MLN:MLN + 1], scr[:, 0:2048], AX.X), VV)
            k.w(v, VV, vc)
            vc = k.inc(v.tensor_scalar_mul(msc[:, MLN:MLN + 1],
                                           msc[:, MLN:MLN + 1], 1.0 / D), VV)
            k.w(v, VV, vc)
            vc = k.inc(v.tensor_scalar(scr[:, 2048:4096], scr[:, 0:2048],
                                       msc[:, MLN:MLN + 1], None, ALU.subtract), VV)
            k.w(s, VV, vc)
            ec = k.inc(s.activation(scr[:, 0:2048], scr[:, 2048:4096], AF.Square,
                                    accum_out=msc[:, MLN + 2:MLN + 3]), EV)
            k.w(v, EV, ec)
            vc = k.inc(v.tensor_scalar(msc[:, MLN + 2:MLN + 3], msc[:, MLN + 2:MLN + 3],
                                       1.0 / D, 1e-5, ALU.mult, ALU.add), VV)
            k.w(v, VV, vc)
            vc = k.inc(v.reciprocal(msc[:, MLN + 1:MLN + 2], msc[:, MLN + 2:MLN + 3]), VV)
            k.w(s, VV, vc)
            ec = k.inc(s.activation(msc[:, MLN + 1:MLN + 2],
                                    msc[:, MLN + 1:MLN + 2], AF.Sqrt), EV)
            k.w(v, EV, ec)
            for kk, vv in (spl_free[0], spl_free[1]):
                if kk:
                    k.w(v, kk, vv)
            vc = k.inc(v.tensor_scalar(spl[:, 0:2048], scr[:, 2048:4096],
                                       msc[:, MLN + 1:MLN + 2], None, ALU.mult), VV)
            for f in range(NF):
                b = k.ptbank()
                k.w(t, VV, vc)
                if tt == 0 and f == 0:
                    k.w(t, MM, marks["xnreg"][1] if marks["xnreg"][0] else 0)
                mc = k.inc(t.transpose(pt[b][:, 0:128],
                                       spl[:, f * 128:(f + 1) * 128], idn[:, :]), MM)
                k.w(s, MM, mc)
                if tt == 0 and f == 0:
                    k.wsnap(s, sn, keys=(MM, VV))
                ec = k.inc(s.copy(xn[:, f * TP + tt * 128: f * TP + (tt + 1) * 128],
                                  pt[b][:, 0:128]), EV)
                k.pt_free[b] = (EV, ec)
            spl_free[0] = (MM, k.cnt[MM])
            spl_free[1] = (MM, k.cnt[MM])
        if DBG:
            for f in range(NF):
                k.st(sp_xn[p * D + f * 128: p * D + (f + 1) * 128, :],
                     xn[:, f * TP:(f + 1) * TP], waits=[(EV, k.cnt[EV])])
        return k.cnt[EV]

    def proj(wname, xn_ev):
        ymark = marks["ybuf"]
        for mg in range(8):
            stg, lc, lkey = k.stage([(lambda sg: st3(sg, NF, 256),
                                      r3(w_f[wname])[:, :, mg * 256:(mg + 1) * 256])],
                                    cast=True)
            for m2 in range(2):
                mglob = mg * 2 + m2
                bA, bB = k.bank(), k.bank()
                trips = [(stg[:, kk * 256 + m2 * 128: kk * 256 + m2 * 128 + 128],
                          xn[:, kk * TP: kk * TP + CH],
                          xn[:, kk * TP + CH: kk * TP + 2 * CH])
                         for kk in range(NF)]
                mc = k.chain_pair(pb[bA][:, 0:CH], pb[bB][:, 0:CH], trips,
                                  waits=[(lkey, lc), (EV, xn_ev)])
                k.w(s, MM, mc)
                if ymark[0]:
                    k.w(s, ymark[0], ymark[1])
                ec = k.inc(s.copy(ybuf[:, mglob * TP: mglob * TP + CH],
                                  pb[bA][:, 0:CH]), EV)
                ec = k.inc(s.copy(ybuf[:, mglob * TP + CH: (mglob + 1) * TP],
                                  pb[bB][:, 0:CH]), EV)
                k.bank_free[bA] = (EV, ec)
                k.bank_free[bB] = (EV, ec)
            k.stage_consumed(MM, k.cnt[MM])
        return k.cnt[EV]

    def fft_mtile(mf, rhs_tensor, rhs_ready):
        stg, lc, lkey = k.stage([
            (lambda sg: sg[:, 0:NF * 128].rearrange("p (a c) -> p a c", c=128),
             r3(cmat)[:, :, mf * 128:(mf + 1) * 128]),
            (lambda sg: sg[:, NF * 128:2 * NF * 128].rearrange("p (a c) -> p a c", c=128),
             r3(smat)[:, :, mf * 128:(mf + 1) * 128])])
        res = []
        for plane in range(2):
            off = plane * NF * 128
            bA, bB = k.bank(), k.bank()
            trips = [(stg[:, off + kk * 128: off + (kk + 1) * 128],
                      rhs_tensor[:, kk * TP: kk * TP + CH],
                      rhs_tensor[:, kk * TP + CH: (kk + 1) * TP])
                     for kk in range(NF)]
            mc = k.chain_pair(pb[bA][:, 0:CH], pb[bB][:, 0:CH], trips,
                              waits=[(lkey, lc)] + ([rhs_ready] if rhs_ready else []))
            res.append((bA, bB, mc))
        k.stage_consumed(MM, res[-1][2])
        return res

    def phase_C(xn_ev, b2_mark):
        yev = proj("wk", xn_ev)
        for mf in range(NFH):
            res = fft_mtile(mf, ybuf, (EV, yev))
            for plane in range(2):
                bA, bB, mc = res[plane]
                k.w(s, MM, mc)
                if mf == 0 and plane == 0 and b2_mark[0]:
                    k.w(s, b2_mark[0], b2_mark[1])
                c0 = plane * NF * TP + mf * TP
                ec = k.inc(s.copy(big2[:, c0:c0 + CH], pb[bA][:, 0:CH]), EV)
                ec = k.inc(s.copy(big2[:, c0 + CH:c0 + TP], pb[bB][:, 0:CH]), EV)
                k.bank_free[bA] = (EV, ec)
                k.bank_free[bB] = (EV, ec)
        marks["ybuf"] = (MM, k.cnt[MM])
        return k.cnt[EV]

    def phase_D(xn_ev, kf_ev):
        yev = proj("wv", xn_ev)
        for mf in range(NFH):
            res = fft_mtile(mf, ybuf, (EV, yev))
            (bR0, bR1, mR), (bI0, bI1, mI) = res
            for ncz in range(2):
                bR = bR0 if ncz == 0 else bR1
                bI = bI0 if ncz == 0 else bI1
                c0 = mf * TP + ncz * CH
                kR = big2[:, c0:c0 + CH]
                kI = big2[:, NF * TP + c0:NF * TP + c0 + CH]
                k.w(v, MM, max(mR, mI))
                k.w(v, EV, kf_ev)
                v.tensor_mul(scr[:, 0:512], pb[bR][:, 0:CH], kR)
                v.tensor_mul(scr[:, 512:1024], pb[bI][:, 0:CH], kI)
                v.tensor_sub(scr[:, 1024:1536], scr[:, 0:512], scr[:, 512:1024])
                v.tensor_mul(scr[:, 0:512], pb[bI][:, 0:CH], kR)
                v.tensor_mul(scr[:, 512:1024], pb[bR][:, 0:CH], kI)
                v.tensor_add(kI, scr[:, 0:512], scr[:, 512:1024])
                vc = k.inc(v.tensor_copy(kR, scr[:, 1024:1536]), VV)
                k.bank_free[bR] = (VV, vc)
                k.bank_free[bI] = (VV, vc)
        marks["ybuf"] = (MM, k.cnt[MM])
        return k.cnt[VV]

    def phase_E(p, xn_ev, kv_vv):
        yev = proj("wq", xn_ev)
        for mf in range(NFH):
            res = fft_mtile(mf, ybuf, (EV, yev))
            (bR0, bR1, mR), (bI0, bI1, mI) = res
            for ncz in range(2):
                bR = bR0 if ncz == 0 else bR1
                bI = bI0 if ncz == 0 else bI1
                c0 = mf * TP + ncz * CH
                kvR = big2[:, c0:c0 + CH]
                kvI = big2[:, NF * TP + c0:NF * TP + c0 + CH]
                k.w(v, MM, max(mR, mI))
                k.w(v, VV, kv_vv)
                tA = scr[:, 2048 + ncz * CH:2048 + ncz * CH + CH]
                tB = scr[:, 3072 + ncz * CH:3072 + ncz * CH + CH]
                v.tensor_mul(tA, pb[bR][:, 0:CH], kvR)
                v.tensor_mul(tB, pb[bI][:, 0:CH], kvI)
                v.tensor_add(scr[:, ncz * CH:ncz * CH + CH], tA, tB)
                v.tensor_mul(tA, pb[bR][:, 0:CH], kvI)
                v.tensor_mul(tB, pb[bI][:, 0:CH], kvR)
                vc = k.inc(v.tensor_sub(scr[:, 1024 + ncz * CH:1024 + ncz * CH + CH],
                                        tA, tB), VV)
                k.bank_free[bR] = (VV, vc)
                k.bank_free[bI] = (VV, vc)
            res2 = fft_mtile(mf, xn, (EV, xn_ev))
            totR = msc[:, MTOT + mf:MTOT + mf + 1]
            totI = msc[:, MTOT + 16 + mf:MTOT + 16 + mf + 1]
            k.w(v, VV, k.cnt[VV])
            init_r = 0.0 if p == 0 else totR
            init_i = 0.0 if p == 0 else totI
            vc = k.inc(v.tensor_tensor_scan(scr[:, 2048:3072], scr[:, 0:1024],
                                            scr[:, 0:1024], init_r,
                                            ALU.add, ALU.bypass), VV)
            vc = k.inc(v.tensor_tensor_scan(scr[:, 3072:4096], scr[:, 1024:2048],
                                            scr[:, 1024:2048], init_i,
                                            ALU.add, ALU.bypass), VV)
            k.w(v, VV, vc)
            v.tensor_copy(totR, scr[:, 3071:3072])
            vc = k.inc(v.tensor_copy(totI, scr[:, 4095:4096]), VV)
            slR = spl_slot(v)
            slI = spl_slot(v)
            (fR0, fR1, fmR), (fI0, fI1, fmI) = res2
            lastv = 0
            for ncz in range(2):
                fR = fR0 if ncz == 0 else fR1
                fI = fI0 if ncz == 0 else fI1
                k.w(v, MM, max(fmR, fmI))
                v.tensor_add(spl[:, slR * 1024 + ncz * CH: slR * 1024 + ncz * CH + CH],
                             scr[:, 2048 + ncz * CH:2048 + ncz * CH + CH],
                             pb[fR][:, 0:CH])
                lastv = k.inc(v.tensor_add(spl[:, slI * 1024 + ncz * CH:
                                               slI * 1024 + ncz * CH + CH],
                                           scr[:, 3072 + ncz * CH:3072 + ncz * CH + CH],
                                           pb[fI][:, 0:CH]), VV)
                k.bank_free[fR] = (VV, lastv)
                k.bank_free[fI] = (VV, lastv)
            r0 = p * 2 * D + mf * 128
            sc = k.st(sp_xh[r0:r0 + 128, :], spl[:, slR * 1024:slR * 1024 + 1024],
                      waits=[(VV, lastv)])
            sc = k.st(sp_xh[r0 + D:r0 + D + 128, :],
                      spl[:, slI * 1024:slI * 1024 + 1024])
            spl_free[slR] = (ST, sc)
            spl_free[slI] = (ST, sc)
        marks["ybuf"] = (MM, k.cnt[MM])
        return k.cnt[VV]

    def phase_GH(p, wname, sp, xn_ev, mkm=False):
        yev = proj(wname, xn_ev)
        for mf in range(NFH):
            res = fft_mtile(mf, ybuf, (EV, yev))
            slR = spl_slot(s)
            slI = spl_slot(s)
            last = 0
            for plane, sl in ((0, slR), (1, slI)):
                bA, bB, mc = res[plane]
                k.w(s, MM, mc)
                last = k.inc(s.copy(spl[:, sl * 1024:sl * 1024 + CH],
                                    pb[bA][:, 0:CH]), EV)
                last = k.inc(s.copy(spl[:, sl * 1024 + CH:sl * 1024 + TP],
                                    pb[bB][:, 0:CH]), EV)
                k.bank_free[bA] = (EV, last)
                k.bank_free[bB] = (EV, last)
            r0 = p * 2 * D + mf * 128
            sc = k.st(sp[r0:r0 + 128, :], spl[:, slR * 1024:slR * 1024 + 1024],
                      waits=[(EV, last)])
            sc = k.st(sp[r0 + D:r0 + D + 128, :], spl[:, slI * 1024:slI * 1024 + 1024])
            spl_free[slR] = (ST, sc)
            spl_free[slI] = (ST, sc)
            if mkm:
                slM = spl_slot(v)
                k.w(v, EV, last)
                v.tensor_mul(scr[:, 0:1024], spl[:, slR * 1024:slR * 1024 + 1024],
                             spl[:, slR * 1024:slR * 1024 + 1024])
                v.tensor_mul(scr[:, 1024:2048], spl[:, slI * 1024:slI * 1024 + 1024],
                             spl[:, slI * 1024:slI * 1024 + 1024])
                v.tensor_add(scr[:, 0:1024], scr[:, 0:1024], scr[:, 1024:2048])
                vc = k.inc(v.tensor_scalar(spl[:, slM * 1024:slM * 1024 + 1024],
                                           scr[:, 0:1024], -1.0 / D, 1.0 / D,
                                           ALU.mult, ALU.add), VV)
                sc2 = k.st(sp_mt[p * D + mf * 128: p * D + (mf + 1) * 128, :],
                           spl[:, slM * 1024:slM * 1024 + 1024], waits=[(VV, vc)])
                spl_free[slM] = (ST, sc2)
        marks["ybuf"] = (MM, k.cnt[MM])

    def phase_B(p, xn_ev, e_vv):
        bA, bB = k.bank(), k.bank()
        trips = [(wrtb[:, kk * 8:(kk + 1) * 8],
                  xn[:, kk * TP: kk * TP + CH],
                  xn[:, kk * TP + CH: (kk + 1) * TP]) for kk in range(NF)]
        mc = k.chain_pair(pb[bA][0:8, 0:CH], pb[bB][0:8, 0:CH], trips,
                          waits=[(EV, xn_ev), (VV, 1)])
        k.w(s, MM, mc)
        ec = k.inc(s.copy(lgf[0:8, 0:CH], pb[bA][0:8, 0:CH]), EV)
        ec = k.inc(s.copy(lgf[0:8, CH:TP], pb[bB][0:8, 0:CH]), EV)
        k.bank_free[bA] = (EV, ec)
        k.bank_free[bB] = (EV, ec)
        bA, bB = k.bank(), k.bank()
        trips = [(wsgtb[:, kk:kk + 1],
                  xn[:, kk * TP: kk * TP + CH],
                  xn[:, kk * TP + CH: (kk + 1) * TP]) for kk in range(NF)]
        mc = k.chain_pair(pb[bA][0:1, 0:CH], pb[bB][0:1, 0:CH], trips)
        k.w(s, MM, mc)
        ec = k.inc(s.activation(sgf[0:1, 0:CH], pb[bA][0:1, 0:CH], AF.Sigmoid), EV)
        ec = k.inc(s.activation(sgf[0:1, CH:TP], pb[bB][0:1, 0:CH], AF.Sigmoid), EV)
        k.bank_free[bA] = (EV, ec)
        k.bank_free[bB] = (EV, ec)
        lg_ev = k.cnt[EV]

        rt_vc = [0] * 8
        b_vv0 = k.cnt[VV]

        def router_a(tt):
            c0 = tt * 32
            m0 = MRT + tt * 8
            b = k.bank()
            k.w(t, EV, lg_ev)
            mc = k.inc(t.transpose(pb[b][:, 0:8],
                                   lgf[0:8, tt * 128:(tt + 1) * 128],
                                   idnf[0:8, 0:8]), MM)
            k.w(s, MM, mc)
            k.w(s, VV, b_vv0)
            ec = k.inc(s.copy(scr[:, c0:c0 + 8], pb[b][:, 0:8]), EV)
            k.bank_free[b] = (EV, ec)
            k.w(v, EV, ec)
            vc = k.inc(v.reduce_max(msc[:, m0:m0 + 1], scr[:, c0:c0 + 8], AX.X,
                                    negate=True), VV)
            k.w(s, VV, vc)
            ec = k.inc(s.activation(scr[:, c0 + 8:c0 + 16], scr[:, c0:c0 + 8],
                                    AF.Exp, bias=msc[:, m0:m0 + 1],
                                    accum_out=msc[:, m0 + 1:m0 + 2]), EV)
            k.w(v, EV, ec)
            vc = k.inc(v.reciprocal(msc[:, m0 + 2:m0 + 3],
                                    msc[:, m0 + 1:m0 + 2]), VV)
            k.w(v, VV, vc)
            vc = k.inc(v.tensor_scalar_mul(scr[:, c0 + 8:c0 + 16],
                                           scr[:, c0 + 8:c0 + 16],
                                           msc[:, m0 + 2:m0 + 3]), VV)
            k.w(v, VV, vc)
            vc = k.inc(v.reduce_max(msc[:, m0 + 3:m0 + 4],
                                    scr[:, c0 + 8:c0 + 16], AX.X), VV)
            k.w(v, VV, vc)
            vc = k.inc(v.tensor_scalar(scr[:, c0 + 16:c0 + 24],
                                       scr[:, c0 + 8:c0 + 16],
                                       msc[:, m0 + 3:m0 + 4], None, ALU.is_lt), VV)
            k.w(v, VV, vc)
            vc = k.inc(v.tensor_mul(scr[:, c0 + 16:c0 + 24],
                                    scr[:, c0 + 16:c0 + 24],
                                    scr[:, c0 + 8:c0 + 16]), VV)
            k.w(v, VV, vc)
            vc = k.inc(v.reduce_max(msc[:, m0 + 4:m0 + 5],
                                    scr[:, c0 + 16:c0 + 24], AX.X), VV)
            k.w(v, VV, vc)
            vc = k.inc(v.tensor_scalar(scr[:, c0 + 16:c0 + 24],
                                       scr[:, c0 + 8:c0 + 16],
                                       msc[:, m0 + 4:m0 + 5], None, ALU.is_ge), VV)
            k.w(v, VV, vc)
            vc = k.inc(v.tensor_mul(scr[:, c0 + 16:c0 + 24],
                                    scr[:, c0 + 16:c0 + 24],
                                    scr[:, c0 + 8:c0 + 16]), VV)
            rt_vc[tt] = vc

        def router_b(tt):
            c0 = tt * 32
            b = k.bank()
            k.w(t, VV, rt_vc[tt])
            mc = k.inc(t.transpose(pb[b][0:8, 0:128], scr[:, c0 + 16:c0 + 24],
                                   idnf[:, :]), MM)
            k.w(s, MM, mc)
            ec = k.inc(s.copy(cebf[0:8, tt * 128:(tt + 1) * 128],
                              pb[b][0:8, 0:128]), EV)
            k.bank_free[b] = (EV, ec)

        def sg_bcast():
            k.w(v, EV, lg_ev)
            vc = k.inc(v.tensor_copy(sgb[0:1, :], sgf[0:1, :]), VV)
            bA, bB = k.bank(), k.bank()
            k.w(t, GG, 1)
            mc = k.chain_pair(pb[bA][:, 0:CH], pb[bB][:, 0:CH],
                              [(ones1[0:1, 0:128], sgb[0:1, 0:CH], sgb[0:1, CH:TP])],
                              waits=[(VV, vc)])
            k.w(s, MM, mc)
            ec = k.inc(s.copy(sgbc[:, 0:CH], pb[bA][:, 0:CH]), EV)
            ec = k.inc(s.copy(sgbc[:, CH:TP], pb[bB][:, 0:CH]), EV)
            k.bank_free[bA] = (EV, ec)
            k.bank_free[bB] = (EV, ec)

        ce_ev = 0
        sg_ev = 0
        gate_st = [0] * 44
        for jt in range(44):
            stg, lc, lkey = k.stage([(lambda sg: st3(sg, NF, 128),
                                      r3(w_f["wsgu"])[:, :, jt * 128:(jt + 1) * 128])],
                                    cast=True)
            sl = spl_slot(s)
            bA, bB = k.bank(), k.bank()
            trips = [(stg[:, kk * 128:(kk + 1) * 128],
                      xn[:, kk * TP: kk * TP + CH],
                      xn[:, kk * TP + CH: (kk + 1) * TP]) for kk in range(NF)]
            mc = k.chain_pair(pb[bA][:, 0:CH], pb[bB][:, 0:CH], trips,
                              waits=[(lkey, lc)])
            k.w(s, MM, mc)
            last = k.inc(s.activation(spl[:, sl * 1024:sl * 1024 + CH],
                                      pb[bA][:, 0:CH], AF.Silu), EV)
            last = k.inc(s.activation(spl[:, sl * 1024 + CH:sl * 1024 + TP],
                                      pb[bB][:, 0:CH], AF.Silu), EV)
            k.bank_free[bA] = (EV, last)
            k.bank_free[bB] = (EV, last)
            k.stage_consumed(MM, k.cnt[MM])
            sc = k.st(sp_hg[jt * 128:(jt + 1) * 128, :],
                      spl[:, sl * 1024:sl * 1024 + 1024], waits=[(EV, last)])
            gate_st[jt] = sc
            spl_free[sl] = (ST, sc)
            if jt < 8:
                router_a(jt)
                if jt >= 1:
                    router_b(jt - 1)
            elif jt == 8:
                router_b(7)
                ce_ev = k.cnt[EV]
                sg_bcast()
                sg_ev = k.cnt[EV]
        ymark = marks["ybuf"]
        yb_free = [ymark if ymark[0] else (None, 0)] * 4
        for jt in range(44):
            stg, lc, lkey = k.stage([(lambda sg: st3(sg, NF, 128),
                                      r3(w_f["wsgu"])[:, :, (44 + jt) * 128:(45 + jt) * 128])],
                                    cast=True)
            ysl = jt % 4
            glc = k.ld(ybuf[:, ysl * 1024:(ysl + 1) * 1024],
                       sp_hg[jt * 128:(jt + 1) * 128, :],
                       waits=[(ST, gate_st[jt])] +
                             ([yb_free[ysl]] if yb_free[ysl][0] else []))
            sl = spl_slot(v)
            bA, bB = k.bank(), k.bank()
            trips = [(stg[:, kk * 128:(kk + 1) * 128],
                      xn[:, kk * TP: kk * TP + CH],
                      xn[:, kk * TP + CH: (kk + 1) * TP]) for kk in range(NF)]
            mc = k.chain_pair(pb[bA][:, 0:CH], pb[bB][:, 0:CH], trips,
                              waits=[(lkey, lc)])
            k.w(v, MM, mc)
            k.w(v, LD, glc)
            k.w(v, EV, sg_ev)
            lastv = 0
            for ncz, bx in ((0, bA), (1, bB)):
                v.tensor_mul(scr[:, 512:1024], pb[bx][:, 0:CH],
                             ybuf[:, ysl * 1024 + ncz * CH: ysl * 1024 + ncz * CH + CH])
                lastv = k.inc(v.tensor_mul(spl[:, sl * 1024 + ncz * CH:
                                               sl * 1024 + ncz * CH + CH],
                                           scr[:, 512:1024],
                                           sgbc[:, ncz * CH:ncz * CH + CH]), VV)
                k.bank_free[bx] = (VV, lastv)
            k.stage_consumed(MM, k.cnt[MM])
            yb_free[ysl] = (VV, lastv)
            sc = k.st(sp_hsh[jt * 128:(jt + 1) * 128, :],
                      spl[:, sl * 1024:sl * 1024 + 1024], waits=[(VV, lastv)])
            spl_free[sl] = (ST, sc)
        hsh_st = k.cnt[ST]
        yb_last = (VV, k.cnt[VV])

        rW = r3(w_f["wsdn"])
        rH = sp_hsh.ap().rearrange("(a p) c -> p a c", p=128)
        yb_half_free = [yb_last, yb_last]
        for mg2 in range(8):
            banks = [k.bank(), k.bank(), k.bank(), k.bank()]
            runs = [(0, 8), (8, 16), (16, 24), (24, 32), (32, 40), (40, 44)]
            for ri, (ka, kb_) in enumerate(runs):
                nk = kb_ - ka
                stg, lc, lkey = k.stage([(lambda sg, nk=nk: st3(sg, nk, 256),
                                          rW[:, ka:kb_, mg2 * 256:(mg2 + 1) * 256])],
                                        cast=True)
                yh = ri % 2
                ylc = k.ld(ybuf[:, yh * 8192:yh * 8192 + nk * TP]
                           .rearrange("p (a c) -> p a c", c=TP),
                           rH[:, ka:kb_, :],
                           waits=[(ST, hsh_st)] +
                                 ([yb_half_free[yh]] if yb_half_free[yh][0] else []))
                for m2 in range(2):
                    bA = banks[m2 * 2]
                    bB = banks[m2 * 2 + 1]
                    trips = [(stg[:, kk * 256 + m2 * 128: kk * 256 + m2 * 128 + 128],
                              ybuf[:, yh * 8192 + kk * TP: yh * 8192 + kk * TP + CH],
                              ybuf[:, yh * 8192 + kk * TP + CH:
                                   yh * 8192 + (kk + 1) * TP])
                             for kk in range(nk)]
                    k.chain_pair(pb[bA][:, 0:CH], pb[bB][:, 0:CH], trips,
                                 waits=[(lkey, lc), (LD, ylc)],
                                 start=(ri == 0), stop=(ri == len(runs) - 1))
                yb_half_free[yh] = (MM, k.cnt[MM])
                k.stage_consumed(MM, k.cnt[MM])
            for m2 in range(2):
                for ncz in range(2):
                    bi = banks[m2 * 2 + ncz]
                    mglob = mg2 * 2 + m2
                    k.w(s, MM, k.cnt[MM])
                    k.w(s, VV, e_vv)
                    ec = k.inc(s.copy(bigF[:, mglob * TP + ncz * CH:
                                           mglob * TP + ncz * CH + CH],
                                      pb[bi][:, 0:CH]), EV)
                    k.bank_free[bi] = (EV, ec)

        for e in range(8):
            ex_mm = k.cnt[MM]
            bA, bB = k.bank(), k.bank()
            mc = k.chain_pair(pb[bA][:, 0:CH], pb[bB][:, 0:CH],
                              [(sel8[:, e * 128:(e + 1) * 128],
                                cebf[0:8, 0:CH], cebf[0:8, CH:TP])],
                              waits=[(EV, ce_ev), (LD, 32)])
            k.w(s, MM, mc)
            ec = k.inc(s.copy(cebc[:, 0:CH], pb[bA][:, 0:CH]), EV)
            ec = k.inc(s.copy(cebc[:, CH:TP], pb[bB][:, 0:CH]), EV)
            k.bank_free[bA] = (EV, ec)
            k.bank_free[bB] = (EV, ec)
            ce_bc_ev = k.cnt[EV]
            wge = w_f["wegu"].ap()[e * D:(e + 1) * D, :].rearrange(
                "(a p) c -> p a c", p=128)
            for jt in range(11):
                stg, lc, lkey = k.stage([(lambda sg: st3(sg, NF, 128),
                                          wge[:, :, jt * 128:(jt + 1) * 128])],
                                        cast=True)
                bA, bB = k.bank(), k.bank()
                trips = [(stg[:, kk * 128:(kk + 1) * 128],
                          xn[:, kk * TP: kk * TP + CH],
                          xn[:, kk * TP + CH: (kk + 1) * TP]) for kk in range(NF)]
                mc = k.chain_pair(pb[bA][:, 0:CH], pb[bB][:, 0:CH], trips,
                                  waits=[(lkey, lc)])
                k.w(s, MM, max(mc, ex_mm))
                ec = k.inc(s.activation(ybuf[:, jt * TP: jt * TP + CH],
                                        pb[bA][:, 0:CH], AF.Silu), EV)
                ec = k.inc(s.activation(ybuf[:, jt * TP + CH: (jt + 1) * TP],
                                        pb[bB][:, 0:CH], AF.Silu), EV)
                k.bank_free[bA] = (EV, ec)
                k.bank_free[bB] = (EV, ec)
                k.stage_consumed(MM, k.cnt[MM])
            for jt in range(11):
                stg, lc, lkey = k.stage([(lambda sg: st3(sg, NF, 128),
                                          wge[:, :, (11 + jt) * 128:(12 + jt) * 128])],
                                        cast=True)
                bA, bB = k.bank(), k.bank()
                trips = [(stg[:, kk * 128:(kk + 1) * 128],
                          xn[:, kk * TP: kk * TP + CH],
                          xn[:, kk * TP + CH: (kk + 1) * TP]) for kk in range(NF)]
                mc = k.chain_pair(pb[bA][:, 0:CH], pb[bB][:, 0:CH], trips,
                                  waits=[(lkey, lc)])
                k.w(v, MM, mc)
                k.w(v, EV, max(ce_bc_ev, k.cnt[EV]))
                lastv = 0
                for ncz, bx in ((0, bA), (1, bB)):
                    v.tensor_mul(scr[:, 512:1024], pb[bx][:, 0:CH],
                                 ybuf[:, jt * TP + ncz * CH: jt * TP + ncz * CH + CH])
                    lastv = k.inc(v.tensor_mul(ybuf[:, jt * TP + ncz * CH:
                                                    jt * TP + ncz * CH + CH],
                                               scr[:, 512:1024],
                                               cebc[:, ncz * CH:ncz * CH + CH]), VV)
                    k.bank_free[bx] = (VV, lastv)
                k.stage_consumed(MM, k.cnt[MM])
            he_vv = k.cnt[VV]
            wde = w_f["wedn"].ap()[e * 1408:(e + 1) * 1408, :].rearrange(
                "(a p) c -> p a c", p=128)
            for mg2 in range(8):
                stg, lc, lkey = k.stage([(lambda sg: st3(sg, 11, 256),
                                          wde[:, :, mg2 * 256:(mg2 + 1) * 256])],
                                        cast=True)
                for m2 in range(2):
                    mglob = mg2 * 2 + m2
                    bA, bB = k.bank(), k.bank()
                    trips = [(stg[:, kk * 256 + m2 * 128: kk * 256 + m2 * 128 + 128],
                              ybuf[:, kk * TP: kk * TP + CH],
                              ybuf[:, kk * TP + CH: (kk + 1) * TP])
                             for kk in range(11)]
                    mc = k.chain_pair(pb[bA][:, 0:CH], pb[bB][:, 0:CH], trips,
                                      waits=[(lkey, lc), (VV, he_vv)])
                    k.w(v, MM, mc)
                    for ncz, bx in ((0, bA), (1, bB)):
                        dst = bigF[:, mglob * TP + ncz * CH: mglob * TP + ncz * CH + CH]
                        vc = k.inc(v.tensor_add(dst, dst, pb[bx][:, 0:CH]), VV)
                        k.bank_free[bx] = (VV, vc)
                k.stage_consumed(MM, k.cnt[MM])
        f_vv = k.cnt[VV]
        if DBG:
            for mf in range(NF):
                k.st(dbg_F[p * D + mf * 128: p * D + (mf + 1) * 128, :],
                     bigF[:, mf * TP:(mf + 1) * TP], waits=[(VV, f_vv)])
        for mf in range(NF):
            sl = spl_slot(s)
            k.w(s, VV, f_vv)
            ec = k.inc(s.activation(spl[:, sl * 1024:sl * 1024 + 1024],
                                    bigF[:, mf * TP:(mf + 1) * TP], AF.Silu), EV)
            sc = k.st(sp_g[p * D + mf * 128: p * D + (mf + 1) * 128, :],
                      spl[:, sl * 1024:sl * 1024 + 1024], waits=[(EV, ec)])
            spl_free[sl] = (ST, sc)
        marks["ybuf"] = (MM, k.cnt[MM])
        return (EV, k.cnt[EV])

    def phase_CC_issue():
        sc = k.st(cc_in[:, :], msc[:, MTOT:MTOT + 32], waits=[(VV, k.cnt[VV])])
        k.w(g, ST, sc)
        cc = k.inc(g.collective_compute(
            "AllGather", ALU.bypass,
            replica_groups=[[0, 1], [2, 3], [4, 5], [6, 7]],
            ins=[cc_in.ap()], outs=[cc_out.ap()]), CC, 1)
        return cc

    def phase_CC_fin(cc):
        k.w(sy, CC, cc)
        lcc = k.ld(msc[:, MCAR:MCAR + 32], cc_out[0:128, :])
        k.w(v, LD, lcc)
        vc = k.inc(v.tensor_scalar_mul(msc[:, MCAR:MCAR + 32],
                                       msc[:, MCAR:MCAR + 32],
                                       msc[:, MMK:MMK + 1]), VV)
        k.w(v, VV, vc)
        vc = k.inc(v.tensor_copy(carb[:, :], msc[:, MCAR:MCAR + 32]), VV)
        for ap in range(9, 16):
            b = k.bank()
            k.w(t, VV, vc)
            k.w(t, LD, rv_lc)
            t.matmul(pb[b][:, 0:1], RVA, carb[:, 15 - ap:16 - ap],
                     start=True, stop=False)
            t.matmul(pb[b][:, 0:1], RVB, carb[:, 16 - ap:17 - ap],
                     start=False, stop=True)
            t.matmul(pb[b][:, 1:2], RVAN, carb[:, 31 - ap:32 - ap],
                     start=True, stop=False)
            mc = k.inc(t.matmul(pb[b][:, 1:2], RVBN,
                                carb[:, 32 - ap:33 - ap],
                                start=False, stop=True), MM)
            k.w(s, MM, mc)
            ec = k.inc(s.copy(msc[:, MCAR + ap:MCAR + ap + 1], pb[b][:, 0:1]), EV)
            ec = k.inc(s.copy(msc[:, MCAR + 16 + ap:MCAR + 17 + ap],
                              pb[b][:, 1:2]), EV)
            k.bank_free[b] = (EV, ec)
        return vc

    def phase_J(p, car_vv, big2_mark):
        sn = k.snap()
        yb_free = [(None, 0)] * 15
        vc_prev = 0
        for mf in range(NF):
            r0 = p * 2 * D + mf * 128
            carR = msc[:, MCAR + mf:MCAR + mf + 1]
            carI = msc[:, MCAR + 16 + mf:MCAR + 16 + mf + 1]
            x1 = scr[:, 0:1024]
            x2 = scr[:, 1024:2048]
            if mf < NFH:
                sls = [(mf % 3) * 5 + i for i in range(5)]
                srcs = [(sp_xh, r0), (sp_xh, r0 + D), (sp_sf, r0),
                        (sp_sf, r0 + D), (sp_mt, p * D + mf * 128)]
                lc = 0
                for i, (sp, rr) in enumerate(srcs):
                    ysl = sls[i]
                    wts = [(ST, sn[ST])]
                    if yb_free[ysl][0]:
                        wts.append(yb_free[ysl])
                    elif mf < 3:
                        wts += [(MM, sn[MM]), (VV, sn[VV])]
                    lc = k.ld(ybuf[:, ysl * 1024:(ysl + 1) * 1024],
                              sp[rr:rr + 128, :], waits=wts)
                xhR = ybuf[:, sls[0] * 1024:sls[0] * 1024 + 1024]
                xhI = ybuf[:, sls[1] * 1024:sls[1] * 1024 + 1024]
                sR = ybuf[:, sls[2] * 1024:sls[2] * 1024 + 1024]
                sI = ybuf[:, sls[3] * 1024:sls[3] * 1024 + 1024]
                mtt = ybuf[:, sls[4] * 1024:sls[4] * 1024 + 1024]
                t1 = scr[:, 2048:3072]
                t2 = scr[:, 3072:4096]
                k.w(s, LD, lc)
                k.w(s, VV, max(car_vv, vc_prev, sn[VV] if mf == 0 else 0))
                s.activation(x1, xhR, AF.Identity, bias=carR)
                ec = k.inc(s.activation(x2, xhI, AF.Identity, bias=carI), EV)
                k.w(v, EV, ec)
                if big2_mark[0]:
                    k.w(v, big2_mark[0], big2_mark[1])
                taR = big2[:, mf * TP:(mf + 1) * TP]
                taI = big2[:, NF * TP + mf * TP:NF * TP + (mf + 1) * TP]
                v.tensor_mul(t1, sR, x1)
                v.tensor_mul(t2, sI, x2)
                v.tensor_add(taR, t1, t2)
                v.tensor_mul(t1, sI, x1)
                v.tensor_mul(t2, sR, x2)
                v.tensor_sub(taI, t1, t2)
                slR = spl_slot(v)
                slI = spl_slot(v)
                v.tensor_mul(spl[:, slR * 1024:slR * 1024 + 1024], mtt, x1)
                vc = k.inc(v.tensor_mul(spl[:, slI * 1024:slI * 1024 + 1024],
                                        mtt, x2), VV)
            else:
                base = 128 * (15 - mf) + 1
                sls = [(mf % 3) * 5 + i for i in range(3)]
                srcs = [(sp_xh, p * 2 * D + base), (sp_xh, p * 2 * D + D + base),
                        (sp_mt, p * D + base)]
                lc = 0
                for i, (sp, rr) in enumerate(srcs):
                    ysl = sls[i]
                    wts = [(ST, sn[ST])]
                    if yb_free[ysl][0]:
                        wts.append(yb_free[ysl])
                    lc = k.ld(ybuf[:, ysl * 1024:(ysl + 1) * 1024],
                              sp[rr:rr + 128, :], waits=wts)
                banks = []
                for i, rv in enumerate((RVP, RVPN, RVP)):
                    bA, bB = k.bank(), k.bank()
                    src = ybuf[:, sls[i] * 1024:(sls[i] + 1) * 1024]
                    mc = k.chain_pair(pb[bA][:, 0:CH], pb[bB][:, 0:CH],
                                      [(rv, src[:, 0:CH], src[:, CH:TP])],
                                      waits=[(LD, lc), (LD, rv_lc)])
                    banks.append((bA, bB, mc))
                (bR0, bR1, _), (bI0, bI1, _), (bm0, bm1, mM) = banks
                k.w(s, MM, mM)
                k.w(s, VV, max(car_vv, vc_prev))
                s.activation(x1[:, 0:CH], pb[bR0][:, 0:CH], AF.Identity, bias=carR)
                s.activation(x1[:, CH:TP], pb[bR1][:, 0:CH], AF.Identity, bias=carR)
                s.activation(x2[:, 0:CH], pb[bI0][:, 0:CH], AF.Identity, bias=carI)
                ec = k.inc(s.activation(x2[:, CH:TP], pb[bI1][:, 0:CH],
                                        AF.Identity, bias=carI), EV)
                for bb in (bR0, bR1, bI0, bI1):
                    k.bank_free[bb] = (EV, ec)
                k.w(v, EV, ec)
                slR = spl_slot(v)
                slI = spl_slot(v)
                v.tensor_mul(spl[:, slR * 1024:slR * 1024 + CH],
                             pb[bm0][:, 0:CH], x1[:, 0:CH])
                v.tensor_mul(spl[:, slR * 1024 + CH:slR * 1024 + TP],
                             pb[bm1][:, 0:CH], x1[:, CH:TP])
                v.tensor_mul(spl[:, slI * 1024:slI * 1024 + CH],
                             pb[bm0][:, 0:CH], x2[:, 0:CH])
                vc = k.inc(v.tensor_mul(spl[:, slI * 1024 + CH:slI * 1024 + TP],
                                        pb[bm1][:, 0:CH], x2[:, CH:TP]), VV)
                k.bank_free[bm0] = (VV, vc)
                k.bank_free[bm1] = (VV, vc)
            vc_prev = vc
            for ysl in sls:
                yb_free[ysl] = (VV, vc)
            sc = k.st(sp_z0[r0:r0 + 128, :], spl[:, slR * 1024:slR * 1024 + 1024],
                      waits=[(VV, vc)])
            sc = k.st(sp_z0[r0 + D:r0 + D + 128, :],
                      spl[:, slI * 1024:slI * 1024 + 1024])
            spl_free[slR] = (ST, sc)
            spl_free[slI] = (ST, sc)
        vc = k.inc(v.tensor_copy(big2[0:1, NF * TP:NF * TP + TP],
                                 big2[0:1, 8 * TP:9 * TP]), VV)
        return (VV, k.cnt[VV])

    def phase_M1(p, ta_vv):
        h1_st = [0] * 64
        fld_free = [(None, 0), (None, 0)]
        fld_st = [0, 0]
        for jg in range(64):
            if True:
                stg, lc, lkey = k.stage(
                    [(lambda sg: st3(sg, 32, 128),
                      r3(w_f["w1"])[:, :, jg * 128:(jg + 1) * 128])], cast=True)
                fh = jg % 2
                fb = fldb[:, fh * 2048:(fh + 1) * 2048]
                k.w(v, lkey, lc)
                if fld_free[fh][0]:
                    k.w(v, fld_free[fh][0], fld_free[fh][1])
                k.w(v, ST, fld_st[fh])
                v.tensor_add(fb[:, 0:1024], stg[:, 0:1024], stg[:, 1024:2048])
                fvc = k.inc(v.tensor_sub(fb[:, 1024:2048], stg[:, 2048:3072],
                                         stg[:, 3072:4096]), VV)
                k.stage_consumed(VV, fvc)
                fld_st[fh] = k.st(w1c[:, jg * 2048:(jg + 1) * 2048],
                                  fb[:, 0:2048], waits=[(VV, fvc)])
                fwait = [(VV, fvc)]
            else:
                stg, lc, lkey = k.stage(
                    [(lambda sg: sg[:, 0:2048],
                      w1c.ap()[:, jg * 2048:(jg + 1) * 2048])],
                    extra_wait=[(ST, wcache_st["w1"])])
                fb = stg[:, 0:2048]
                fwait = [(lkey, lc)]
            sl = spl_slot(s)
            bA, bB = k.bank(), k.bank()
            trips = ([(fb[:, kk * 128:(kk + 1) * 128],
                       big2[:, kk * TP: kk * TP + CH],
                       big2[:, kk * TP + CH: (kk + 1) * TP]) for kk in range(8)] +
                     [(fb[:, 1024 + kk * 128:1024 + (kk + 1) * 128],
                       big2[:, NF * TP + kk * TP: NF * TP + kk * TP + CH],
                       big2[:, NF * TP + kk * TP + CH: NF * TP + (kk + 1) * TP])
                      for kk in range(8)])
            mc = k.chain_pair(pb[bA][:, 0:CH], pb[bB][:, 0:CH], trips,
                              waits=fwait + [ta_vv])
            if True:
                fld_free[fh] = (MM, mc)
            else:
                k.stage_consumed(MM, mc)
            k.w(s, MM, mc)
            last = k.inc(s.activation(spl[:, sl * 1024:sl * 1024 + CH],
                                      pb[bA][:, 0:CH], AF.Relu,
                                      bias=msc[:, MB1 + jg:MB1 + jg + 1]), EV)
            last = k.inc(s.activation(spl[:, sl * 1024 + CH:sl * 1024 + TP],
                                      pb[bB][:, 0:CH], AF.Relu,
                                      bias=msc[:, MB1 + jg:MB1 + jg + 1]), EV)
            k.bank_free[bA] = (EV, last)
            k.bank_free[bB] = (EV, last)
            sc = k.st(sp_h1[jg * 128:(jg + 1) * 128, :],
                      spl[:, sl * 1024:sl * 1024 + 1024], waits=[(EV, last)])
            h1_st[jg] = sc
            spl_free[sl] = (ST, sc)
        if p == 0:
            wcache_st["w1"] = k.cnt[ST]
        return k.cnt[MM], h1_st

    def phase_M2(p, h1_st, l_tile=None):
        rW2 = r3(w_f["w2"])
        rH1 = sp_h1.ap().rearrange("(a p) c -> p a c", p=128)
        xn_half_free = [marks["xnreg"], marks["xnreg"]]
        sph_st = {}
        w2st = [0, 0]
        coff = 0
        for og in range(11):
            nm2 = 3 if og < 10 else 2
            c0 = og * 384
            wst = nm2 * 128
            banks = [k.bank() for _ in range(2 * nm2)]
            for ri in range(8):
                ka, kb_ = ri * 8, (ri + 1) * 8
                blk = 8 * wst
                if True:
                    hcur = k.stg_i % 2
                    stg, lc, lkey = k.stage(
                        [(lambda sg, w=wst: st3(sg, 8, w),
                          rW2[:, ka:kb_, c0:c0 + wst])],
                        extra_wait=[(ST, w2st[hcur])], cast=True)
                    w2st[hcur] = k.st(w2c[:, coff:coff + blk], stg[:, 0:blk],
                                      waits=[(lkey, lc)])
                else:
                    stg, lc, lkey = k.stage(
                        [(lambda sg, b=blk: sg[:, 0:b],
                          w2c.ap()[:, coff:coff + blk])],
                        extra_wait=[(ST, wcache_st["w2"])])
                coff += blk
                xh_ = ri % 2
                hlc = k.ld(xn[:, xh_ * 8192:xh_ * 8192 + 8 * TP]
                           .rearrange("p (a c) -> p a c", c=TP),
                           rH1[:, ka:kb_, :],
                           waits=[(ST, h1_st[ri * 8 + 7])] +
                                 ([xn_half_free[xh_]] if xn_half_free[xh_][0] else []))
                for m2 in range(nm2):
                    bA = banks[m2 * 2]
                    bB = banks[m2 * 2 + 1]
                    trips = [(stg[:, kk * wst + m2 * 128: kk * wst + m2 * 128 + 128],
                              xn[:, xh_ * 8192 + kk * TP: xh_ * 8192 + kk * TP + CH],
                              xn[:, xh_ * 8192 + kk * TP + CH:
                                 xh_ * 8192 + (kk + 1) * TP])
                             for kk in range(8)]
                    k.chain_pair(pb[bA][:, 0:CH], pb[bB][:, 0:CH], trips,
                                 waits=[(lkey, lc), (LD, hlc)],
                                 start=(ri == 0), stop=(ri == 7))
                xn_half_free[xh_] = (MM, k.cnt[MM])
                k.stage_consumed(MM, k.cnt[MM])
            for m2 in range(nm2):
                mglob = og * 3 + m2
                ssl = spl_slot(s)
                last = 0
                for ncz in range(2):
                    bi = banks[m2 * 2 + ncz]
                    k.w(s, MM, k.cnt[MM])
                    last = k.inc(s.activation(spl[:, ssl * 1024 + ncz * CH:
                                                  ssl * 1024 + ncz * CH + CH],
                                              pb[bi][:, 0:CH], AF.Identity,
                                              bias=msc[:, MB2 + mglob:
                                                       MB2 + mglob + 1]), EV)
                    k.bank_free[bi] = (EV, last)
                sc = k.st(sp_h[mglob * 128:(mglob + 1) * 128, :],
                          spl[:, ssl * 1024:ssl * 1024 + 1024], waits=[(EV, last)])
                sph_st[mglob] = sc
                spl_free[ssl] = (ST, sc)
            if l_tile is not None:
                for m2 in range(nm2):
                    mglob = og * 3 + m2
                    if mglob >= 16:
                        l_tile(mglob - 16, sph_st[mglob])
        if p == 0:
            wcache_st["w2"] = k.cnt[ST]
        marks["xnreg"] = (MM, k.cnt[MM])

    def make_L(p, ta_done_mm):
        sn = k.snap()
        yb_free = [(None, 0)] * 12
        nrun = [0]

        def L_tile(mf, h_st_val):
            r0 = p * 2 * D + mf * 128
            sls = [(mf % 2) * 6 + i for i in range(6)]
            mirror = mf >= NFH
            tfb = p * 2 * D + 128 * (15 - mf) + 1 if mirror else r0
            srcs = [(sp_h, mf * 128, True), (sp_h, D + mf * 128, True),
                    (sp_tf, tfb, False), (sp_tf, tfb + D, False),
                    (sp_z0, r0, False), (sp_z0, r0 + D, False)]
            lc = 0
            for i, (sp, rr, ish) in enumerate(srcs):
                ysl = sls[i]
                wts = [(ST, h_st_val if ish else sn[ST])]
                if yb_free[ysl][0]:
                    wts.append(yb_free[ysl])
                elif nrun[0] < 2:
                    wts += [(MM, sn[MM]), (VV, sn[VV])]
                lc = k.ld(ybuf[:, ysl * 1024:(ysl + 1) * 1024],
                          sp[rr:rr + 128, :], waits=wts, eng=s)
            nrun[0] += 1
            if mirror:
                ec = 0
                for i, rv in ((2, RVP), (3, RVPN)):
                    bA, bB = k.bank(), k.bank()
                    src = ybuf[:, sls[i] * 1024:(sls[i] + 1) * 1024]
                    mc = k.chain_pair(pb[bA][:, 0:CH], pb[bB][:, 0:CH],
                                      [(rv, src[:, 0:CH], src[:, CH:TP])],
                                      waits=[(LD, lc), (LD, rv_lc)])
                    k.w(s, MM, mc)
                    ec = k.inc(s.copy(src[:, 0:CH], pb[bA][:, 0:CH]), EV)
                    ec = k.inc(s.copy(src[:, CH:TP], pb[bB][:, 0:CH]), EV)
                    k.bank_free[bA] = (EV, ec)
                    k.bank_free[bB] = (EV, ec)
                k.w(v, EV, ec)
            h1c = ybuf[:, sls[0] * 1024:sls[0] * 1024 + 1024]
            h2c = ybuf[:, sls[1] * 1024:sls[1] * 1024 + 1024]
            tR = ybuf[:, sls[2] * 1024:sls[2] * 1024 + 1024]
            tI = ybuf[:, sls[3] * 1024:sls[3] * 1024 + 1024]
            z0R = ybuf[:, sls[4] * 1024:sls[4] * 1024 + 1024]
            z0I = ybuf[:, sls[5] * 1024:sls[5] * 1024 + 1024]
            k.w(v, LD, lc)
            k.w(v, MM, ta_done_mm)
            t1 = scr[:, 0:1024]
            t2 = scr[:, 1024:2048]
            zR = big2[:, mf * TP:(mf + 1) * TP]
            zP = big2[:, NF * TP + mf * TP:NF * TP + (mf + 1) * TP]
            v.tensor_mul(t1, tR, h1c)
            v.tensor_mul(t2, tI, h2c)
            v.tensor_add(t1, t1, t2)
            v.scalar_tensor_tensor(zR, t1, 1.0 / D, z0R, ALU.mult, ALU.add)
            v.tensor_mul(t1, tI, h1c)
            v.tensor_mul(t2, tR, h2c)
            v.tensor_sub(t1, t1, t2)
            vc = k.inc(v.scalar_tensor_tensor(zP, t1, 1.0 / D, z0I,
                                              ALU.mult, ALU.add), VV)
            for ysl in sls:
                yb_free[ysl] = (VV, vc)

        return L_tile

    def phase_M(p, z_vv):
        sn = k.snap()
        g_free = [(None, 0)] * 2
        st_tail = {"x_free": (None, 0), "prev_store": 0, "first": True}

        def tail_tt(tt, sg_vv):
            row0 = p * TP + tt * 128
            wts = []
            if st_tail["x_free"][0]:
                wts.append(st_tail["x_free"])
            if st_tail["first"]:
                wts += [(VV, sn[VV]), (EV, sn[EV])]
                st_tail["first"] = False
            xlc = k.ld(scr[:, 2048:4096], x_in[row0:row0 + 128, :], waits=wts,
                       eng=s)
            lastv = 0
            for mf in range(NF):
                b = k.ptbank()
                k.w(t, VV, sg_vv)
                mc = k.inc(t.transpose(pt[b][:, 0:128],
                                       ybuf[:, mf * TP + tt * 128:
                                            mf * TP + (tt + 1) * 128],
                                       idn[:, :]), MM)
                k.w(v, MM, mc)
                k.w(v, LD, xlc)
                if mf == 0:
                    k.w(v, ST, st_tail["prev_store"])
                lastv = k.inc(v.tensor_add(scr[:, mf * 128:(mf + 1) * 128],
                                           scr[:, 2048 + mf * 128:
                                               2048 + (mf + 1) * 128],
                                           pt[b][:, 0:128]), VV)
                k.pt_free[b] = (VV, lastv)
            sc = k.st(out[row0:row0 + 128, :], scr[:, 0:2048], waits=[(VV, lastv)])
            st_tail["prev_store"] = sc
            st_tail["x_free"] = (VV, lastv)

        sg_half = [0, 0]
        for h in range(2):
            for mf in range(NF):
                stg, lc, lkey = k.stage([
                    (lambda sg_: sg_[:, 0:NF * 128].rearrange(
                        "p (a c) -> p a c", c=128),
                     r3(cmat)[:, :, mf * 128:(mf + 1) * 128]),
                    (lambda sg_: sg_[:, NF * 128:2 * NF * 128].rearrange(
                        "p (a c) -> p a c", c=128),
                     r3(smat)[:, :, mf * 128:(mf + 1) * 128])])
                gsl = mf % 2
                wts = [(ST, sn[ST])]
                if g_free[gsl][0]:
                    wts.append(g_free[gsl])
                else:
                    wts.append((MM, marks["xnreg"][1]))
                glc = k.ld(xn[:, gsl * 1024:(gsl + 1) * 1024],
                           sp_g[p * D + mf * 128: p * D + (mf + 1) * 128, :],
                           waits=wts)
                b = k.bank()
                pairs = ([(stg[:, kk * 128:(kk + 1) * 128],
                           big2[:, kk * TP + h * CH: kk * TP + h * CH + CH])
                          for kk in range(NF)] +
                         [(stg[:, NF * 128 + kk * 128: NF * 128 + (kk + 1) * 128],
                           big2[:, NF * TP + kk * TP + h * CH:
                                NF * TP + kk * TP + h * CH + CH])
                          for kk in range(NF)])
                mc = k.chain(pb[b][:, 0:CH], pairs, waits=[(lkey, lc), z_vv])
                k.w(v, MM, mc)
                k.w(v, LD, glc)
                if h == 0 and mf == 0:
                    k.wsnap(v, sn, keys=(MM,))
                lastv = k.inc(v.tensor_mul(ybuf[:, mf * TP + h * CH:
                                                mf * TP + h * CH + CH],
                                           pb[b][:, 0:CH],
                                           xn[:, gsl * 1024 + h * CH:
                                              gsl * 1024 + h * CH + CH]), VV)
                k.bank_free[b] = (VV, lastv)
                g_free[gsl] = (VV, lastv)
                k.stage_consumed(MM, k.cnt[MM])
                sg_half[h] = lastv
                if h == 1 and mf % 2 == 1 and mf >= 1 and mf <= 7:
                    tail_tt(mf // 2, sg_half[0])
        for tt in range(4, 8):
            tail_tt(tt, sg_half[1])

    b2_mark = (None, 0)
    cc_id = 0
    for p in range(NP):
        xn_ev = phase_A(p)
        kf_ev = phase_C(xn_ev, b2_mark)
        kv_vv = phase_D(xn_ev, kf_ev)
        e_vv = phase_E(p, xn_ev, kv_vv)
        if p == NP - 1:
            cc_id = phase_CC_issue()
        phase_GH(p, "ws", sp_sf, xn_ev, mkm=True)
        phase_GH(p, "wt", sp_tf, xn_ev)
        b2_mark = phase_B(p, xn_ev, e_vv)
    car_vv = phase_CC_fin(cc_id)
    marks["xnreg"] = (MM, k.cnt[MM])
    l_done_mm = 0
    for p in range(NP):
        big2_mark = b2_mark if p == 0 else (MM, l_done_mm)
        ta_vv = phase_J(p, car_vv, big2_mark)
        m1_mm, h1_st = phase_M1(p, ta_vv)
        l_tile = make_L(p, m1_mm)
        phase_M2(p, h1_st, l_tile)
        zvv = (VV, k.cnt[VV])
        phase_M(p, zvv)
        l_done_mm = k.cnt[MM]
    sy.wait_ge(k.sem[ST], k.cnt[ST])
    nc._kb_stack = stack
    return nc


_HOST = {}


def _host_constants():
    if not _HOST:
        n = np.arange(D, dtype=np.float64)
        th = 2.0 * np.pi * np.outer(n, n) / D
        _HOST["cmat"] = np.cos(th).astype(ml_dtypes.bfloat16)
        _HOST["smat"] = np.sin(th).astype(ml_dtypes.bfloat16)
        _HOST["idn"] = np.eye(128, dtype=ml_dtypes.bfloat16)
        s8 = np.zeros((8, 8 * 128), dtype=ml_dtypes.bfloat16)
        for e in range(8):
            s8[e, e * 128:(e + 1) * 128] = 1.0
        _HOST["sel8"] = s8
        rvp = np.zeros((128, 128), np.float32)
        for pp in range(128):
            rvp[127 - pp, pp] = 1.0
        rva = np.zeros((128, 128), np.float32)
        for pp in range(1, 128):
            rva[128 - pp, pp] = 1.0
        rvb = np.zeros((128, 128), np.float32)
        rvb[0, 0] = 1.0
        _HOST["revs"] = np.concatenate(
            [rvp, -rvp, rva, -rva, rvb, -rvb],
            axis=1).astype(ml_dtypes.bfloat16)
    return _HOST


def kernel(**inputs):
    hc = _host_constants()
    f32 = lambda a: np.ascontiguousarray(np.asarray(a, np.float32))
    x = f32(inputs["L0_attn_x"]).reshape(8 * T, D)
    W1 = f32(inputs["L0_attn_W1"])
    zrow = np.zeros((1, 8192), np.float32)
    w1fold = np.ascontiguousarray(np.concatenate([
        W1[0:1024],
        zrow, W1[2047:1024:-1],
        W1[1024:1025], W1[2049:3072],
        zrow, W1[4095:3072:-1],
    ], axis=0))
    shared = {
        "wq": f32(inputs["L0_attn_Wq"]), "wk": f32(inputs["L0_attn_Wk"]),
        "wv": f32(inputs["L0_attn_Wv"]), "ws": f32(inputs["L0_attn_Ws"]),
        "wt": f32(inputs["L0_attn_Wt"]),
        "w1": w1fold, "w2": f32(inputs["L0_attn_W2"]),
        "wsgu": f32(inputs["L0_ffn_Ws_gate_up"]),
        "wsdn": f32(inputs["L0_ffn_Ws_down"]),
        "wegu": f32(inputs["L0_ffn_We_gate_up"]).reshape(8 * D, 2816),
        "wedn": f32(inputs["L0_ffn_We_down"]).reshape(8 * 1408, D),
        "wrt": np.ascontiguousarray(
            f32(inputs["L0_ffn_W_router"]).reshape(8, 16, 128)
            .transpose(2, 1, 0).reshape(128, 128)),
        "wsgt": np.ascontiguousarray(
            f32(inputs["L0_ffn_W_sgate"]).reshape(16, 128).T),
        "b1t": np.ascontiguousarray(f32(inputs["L0_attn_b1"]).reshape(64, 128).T),
        "b2t": np.ascontiguousarray(f32(inputs["L0_attn_b2"]).reshape(32, 128).T),
        "cmat": hc["cmat"], "smat": hc["smat"], "idn": hc["idn"],
        "sel8": hc["sel8"], "revs": hc["revs"],
    }
    in_maps = []
    for c in range(8):
        m = dict(shared)
        m["x"] = np.ascontiguousarray(x[c * T:(c + 1) * T])
        m["mask"] = np.full((128, 1), float(c % 2), np.float32)
        in_maps.append(m)
    nc = build_nc()
    res = run_bass_kernel_spmd(nc, in_maps, core_ids=list(range(8)))
    out = np.concatenate([np.asarray(r["out"]) for r in res.results], axis=0)
    if DBG:
        kernel.dbg = res.results
    return out.reshape(4, 4096, D).astype(np.float32)
